# version 13
# baseline (speedup 1.0000x reference)
import contextlib
import math
import numpy as np

import concourse.bass as bass
import concourse.mybir as mybir
from concourse.bass_utils import run_bass_kernel_spmd

F32 = mybir.dt.float32
F16 = mybir.dt.float16
I32 = mybir.dt.int32
AF = mybir.ActivationFunctionType
ALU = mybir.AluOpType
AX = mybir.AxisListType

NCORES = 8
T = 2048
D = 1024
NH = 8
HD = 128
CH = 64
NT = 16
DC = 8
IT = 24
FI = 3072
TG = 4
QMAX = 32767.0
WQ = 7.0

ENGINES = ("sync", "scalar", "vector", "tensor", "gpsimd")


class Ct:
    __slots__ = ("sem", "v")

    def __init__(self, sem):
        self.sem = sem
        self.v = 0


class Prog:

    def __init__(self):
        self.q = {e: [] for e in ENGINES}
        self.ect = {}

    def on(self, eng, fn):
        self.q[eng].append(fn)

    def wait(self, eng, ct, val=None):
        v = ct.v if val is None else val
        self.on(eng, lambda e, s=ct.sem, v=v: e.wait_ge(s, v))

    def dma(self, eng, out, in_, ct, serial=True, **kw):
        own = self.ect.get(eng)
        if own is not None:
            self.wait(eng, own, own.v)
        if serial:
            self.wait(eng, ct, ct.v)
        ct.v += 16
        self.on(eng, lambda e, o=out, i=in_, s=ct.sem, kw=kw:
                e.dma_start(out=o, in_=i, **kw).then_inc(s, 16))
        return ct.v

    def op(self, eng, fn, ct=None, inc=1):
        own = self.ect[eng]
        if ct is None or ct is own:
            ct = own
            self.wait(eng, own, own.v)
            own.v += inc
            v = own.v
            def g(e, fn=fn, s=own.sem, inc=inc):
                fn(e).then_inc(s, inc)
            self.on(eng, g)
            return v
        raise NotImplementedError


def ceil_div(a, b):
    return -(-a // b)


class Arena:

    def __init__(self, nc, base, words):
        self.nc = nc
        self.base = base
        self.W = words
        self.off = 0
        self.n = 0

    def _at(self, word_off, elems, dt, esz, part=128):
        assert (self.base + 4 * word_off) % 32 == 0, f"unaligned word {word_off}"
        end_w = word_off + (elems * esz + 3) // 4
        assert end_w <= self.W, f"arena OOM {end_w} > {self.W}"
        self.n += 1
        return self.nc.alloc_sbuf_tensor_at(
            f"arn{self.n}", [part, elems], dt, offset=self.base + 4 * word_off)

    def reserve(self, n):
        a = (self.off + 7) & ~7
        self.off = a + n
        assert self.off <= self.W, f"arena OOM {self.off} > {self.W}"
        return a

    def f32(self, n):
        return self._at(self.reserve(n), n, F32, 4)

    def f16(self, n):
        assert n % 2 == 0
        return self._at(self.reserve(n // 2), n, F16, 2)

    def at_f32(self, word_off, n, part=128):
        return self._at(word_off, n, F32, 4, part)

    def at_f16(self, word_off, n):
        return self._at(word_off, n, F16, 2)

    def at_i32(self, word_off, n):
        return self._at(word_off, n, I32, 4)


def build(debug=(), upto=99):
    nc = bass.Bass("TRN2", target_bir_lowering=False)
    dbg = set(debug)

    dram = {}
    def din(name, shape, dt=F32):
        dram[name] = nc.dram_tensor(name, list(shape), dt, kind="ExternalInput").ap()
        return dram[name]
    def dint(name, shape, dt=F32):
        dram[name] = nc.dram_tensor(name, list(shape), dt, kind="Internal").ap()
        return dram[name]
    def dout(name, shape, dt=F32):
        dram[name] = nc.dram_tensor(name, list(shape), dt, kind="ExternalOutput").ap()
        return dram[name]

    x_tok = din("x_tok", [T, D])
    x_dmaj = din("x_dmaj", [D, T])
    wwT = din("wwT", [D, D])
    wrT = din("wrT", [D, D])
    wbaT = din("wbaT", [D, 16])
    alog = din("alog", [1, NH])
    dtb = din("dtb", [1, NH])
    wuT = din("wuT", [D, FI])
    wgT = din("wgT", [D, FI])
    wdT = din("wdT", [FI, D])
    wu_nat = din("wu_nat", [FI, D])
    wg_nat = din("wg_nat", [FI, D])
    wd_nat = din("wd_nat", [D, FI])
    eyef32 = din("eyef32", [128, 128])
    eye16 = din("eye16", [128, 128], F16)
    cumtri16 = din("cumtri16", [128, 128], F16)
    blksum16 = din("blksum16", [128, 128], F16)
    kaddU16 = din("kaddU16", [128, 128], F16)
    kaddL16 = din("kaddL16", [128, 128], F16)
    ones1 = din("ones1", [1, 128])
    onescol = din("onescol", [128, 1])
    ones1k = din("ones1k", [1, 1024])
    ind8 = din("ind8", [8, 256])

    rowscr = dint("rowscr", [48, T])
    sp_gate = dint("sp_gate", [FI, T], F16)
    sp_up = dint("sp_up", [FI, T], F16)
    sp_P = dint("sp_P", [FI, T])
    sp_Q = dint("sp_Q", [FI, T])
    sp_x2 = dint("sp_x2", [D, T])
    sT_u = dint("sT_u", [32, FI])
    sTr_u = dint("sTr_u", [32, FI])
    sT_g = dint("sT_g", [32, FI])
    sTr_g = dint("sTr_g", [32, FI])
    sT_d = dint("sT_d", [96, D])
    sTr_d = dint("sTr_d", [96, D])
    cedscr = dint("cedscr", [8, 2])
    rows2 = dint("rows2", [NT, 3, 1024])
    rk42 = dint("rk42", [NT, 1024], F16)
    ced2 = dint("ced2", [NT, 16])
    cc_in = [dint(f"cc_in{i}", [1, 16]) for i in range(4)]
    cc_out = [dint(f"cc_out{i}", [1, 16]) for i in range(4)]
    y = dout("y", [D, T])

    P = Prog()
    taps = []

    class _StageDone(Exception):
        pass

    ctx = contextlib.ExitStack()
    with ctx:
        sb = lambda nm, shape, dt=F32: ctx.enter_context(nc.sbuf_tensor(nm, shape, dt))
        def sem(nm):
            return Ct(ctx.enter_context(nc.semaphore(nm)))

        ba = sb("ba", [128, NT * 16])
        wba16 = sb("wba16", [128, DC * 16], F16)
        sc_cum = sb("sc_cum", [128, 128])
        sc_ccum = sb("sc_ccum", [128, 128])
        sc_lnb = sb("sc_lnb", [128, 128])
        sc_ss = sb("sc_ss", [128, 128])
        sc_lnn2 = sb("sc_lnn2", [128, 128])
        sc_lnn2sh = sb("sc_lnn2sh", [128, 128])
        sc_p1 = sb("sc_p1", [128, 128])
        sc_p2 = sb("sc_p2", [128, 128])
        sc_tg = sb("sc_tg", [128, 128])
        sc_beta = sb("sc_beta", [128, 128])
        sc_usc = sb("sc_usc", [128, 128])
        sc_ld16 = sb("sc_ld16", [128, 128], F16)
        st_mu = sb("st_mu", [128, 16])
        st_r = sb("st_r", [128, 16])
        st_mush = sb("st_mush", [128, 16])
        st_rsh = sb("st_rsh", [128, 16])
        st_sum = sb("st_sum", [128, 16])
        st_sq = sb("st_sq", [128, 16])
        st_tmp = sb("st_tmp", [128, 16])
        rows = sb("rows", [48, T])
        alogbc = sb("alogbc", [128, NH])
        dtbbc = sb("dtbbc", [128, NH])
        s16 = sb("s16", [128, NH * 128], F16)
        arbc = sb("arbc", [128, 16])
        mxc = sb("mxc", [128, 8])
        qsc = sb("qsc", [128, 12])
        ccs = sb("ccs", [1, 16])
        c_eyef = sb("c_eyef", [128, 128])
        c_eye16 = sb("c_eye16", [128, 128], F16)
        c_cumtri = sb("c_cumtri", [128, 128], F16)
        c_blksum = sb("c_blksum", [128, 128], F16)
        c_kaddU = sb("c_kaddU", [128, 128], F16)
        c_kaddL = sb("c_kaddL", [128, 128], F16)
        c_ones1 = sb("c_ones1", [1, 128])
        c_onescol = sb("c_onescol", [128, 1])
        c_ind8 = sb("c_ind8", [8, 256])
        sck = sb("sck", [128, NT * 48])

        _la = nc._sbuf_addr_for_side("left")
        _ra = nc._sbuf_addr_for_side("right")
        _abase = (_la + 31) & ~31
        AWORDS = (_ra - _abase) // 4
        assert AWORDS >= 43800, f"arena too small: {AWORDS}"
        import os
        if os.environ.get('PRINT_AWORDS'): print('AWORDS =', AWORDS)
        ar = Arena(nc, _abase, AWORDS)

        xd = ar.f32(DC * T)
        xl1d = ar.f16(DC * (T + 1) + 8)
        wwb = ar.f16(DC * D)
        wrb = ar.f16(DC * D)
        v16 = ar.f16(D)
        o16 = ar.f16(NH * 512)
        xt16 = v16
        xsh16 = ar.f16(D)
        xshx16 = ar.f16(D)
        xcur16 = xshx16
        rk4bc = ar.f16(NH * 128)
        cedbc = ar.f32(16)
        UN0 = ar.reserve(6912)
        mrbc = ar.at_f32(UN0, 2 * T)
        xtmp = ar.at_f32(UN0 + 2 * T, T)
        fbc = ar.at_f32(UN0, 3072)
        argbuf = ar.at_f32(UN0 + 3072, 1024)
        wkU = ar.at_f16(UN0, 1024)
        wkL = ar.at_f16(UN0 + 512, 1024)
        wkQ = ar.at_f16(UN0 + 1024, 1024)
        wkU2 = ar.at_f16(UN0 + 1536, 1024)
        wkL2 = ar.at_f16(UN0 + 2048, 1024)
        wkQ2 = ar.at_f16(UN0 + 2560, 1024)
        expw = ar.at_f16(UN0 + 4096, 3072)
        wkAb = ar.at_f16(UN0 + 4096, 1024)
        wkAn = ar.at_f16(UN0 + 4608, 1024)
        mstage = ar.at_f32(UN0 + 5632, 512)
        eq16 = ar.at_f16(UN0 + 6144, 1024)
        rkds = ar.at_f16(UN0 + 6656, 512)
        sptmp = ar.at_f32(UN0 + 768, 128)
        sptmp2 = ar.at_f32(UN0 + 896, 128)
        PX0 = UN0 + 6912
        vq = ar.at_f16(PX0, 1024)
        vqu = ar.at_f16(PX0 + 512, 1024)
        wkITx = [ar.at_f16(PX0 + 1024, 1024), ar.at_f16(PX0 + 1536, 1024)]
        wkGT = ar.at_f16(PX0 + 2048, 1024)
        rk4bcx = [rk4bc, ar.at_f16(PX0 + 2560, 1024)]
        cedbcx = [cedbc, ar.at_f32(PX0 + 3072, 16)]
        rtmp = ar.at_f32(UN0 + 6144, 256, part=16)

        pb = [ctx.enter_context(nc.psum_tensor(f"pb{i}", [128, 512], F32))
              for i in range(8)]

        ALLCTS = []
        def sem2(nm):
            c = sem(nm)
            ALLCTS.append(c)
            return c
        ldS = sem2("ldS"); ldG = sem2("ldG"); ldR = sem2("ldR")
        mmC = sem2("mmC"); veC = sem2("veC"); acC = sem2("acC"); poC = sem2("poC")
        spS = sem2("spS"); ccC = sem2("ccC"); stY = sem2("stY")
        spl = [sem2(f"spl{i}") for i in range(4)]
        lxs = [sem2(f"lxs{i}") for i in range(2)]
        sty = [sem2(f"sty{i}") for i in range(2)]
        qls = [sem2(f"qls{i}") for i in range(2)]
        wls = [sem2(f"wls{i}") for i in range(2)]
        sls = [sem2(f"sls{i}") for i in range(2)]
        swl = [sem2(f"swl{i}") for i in range(2)]
        spP = [sem2(f"spP{i}") for i in range(2)]
        P.ect = {"tensor": mmC, "vector": veC, "scalar": acC, "gpsimd": poC}

        def tap(name, ap, waits):
            if name not in dbg:
                return
            t = nc.dram_tensor("tap_" + name, list(ap.shape),
                               ap.dtype, kind="ExternalOutput").ap()
            taps.append("tap_" + name)
            for (ct, v) in waits:
                P.wait("gpsimd", ct, v)
            P.dma("gpsimd", t, ap, stY)

        def _stage_gate(lvl):
            if upto <= lvl:
                raise _StageDone

        def _final_emit():
            for ct in ALLCTS:
                if ct is not stY:
                    P.wait("gpsimd", ct, ct.v)
            if stY.v == 0 and sty[0].v == 0 and sty[1].v == 0:
                P.dma("gpsimd", y[0:128, :], xd[:, 0:T], stY)
            P.wait("gpsimd", stY, stY.v)

        try:
            P.op("vector", lambda e: e.memset(rows[:], 0.0), veC)
            P.op("vector", lambda e: e.memset(sck[:], 0.0), veC)
            P.dma("sync", xd[:].rearrange("p (c t) -> p c t", c=DC), x_dmaj.rearrange("(c p) t -> p c t", p=128), ldS)
            for cst, srcd in ((c_eyef, eyef32), (c_eye16, eye16), (c_cumtri, cumtri16),
                              (c_blksum, blksum16), (c_kaddU, kaddU16), (c_kaddL, kaddL16)):
                P.dma("sync", cst[:], srcd, ldS)
            P.dma("sync", c_ones1[:], ones1, ldS)
            P.dma("sync", c_onescol[:], onescol, ldS)
            P.dma("sync", c_ind8[:], ind8, ldS)
            ld_consts = ldS.v
            P.dma("gpsimd", wwb[:].rearrange("p (c e) -> p c e", c=DC), wwT.rearrange("(c p) e -> p c e", p=128), ldG)
            P.dma("gpsimd", wrb[:].rearrange("p (c e) -> p c e", c=DC), wrT.rearrange("(c p) e -> p c e", p=128), ldG)
            P.dma("gpsimd", wba16[:].rearrange("p (c e) -> p c e", c=DC), wbaT.rearrange("(c p) e -> p c e", p=128), ldG)
            ld_ww = ldG.v
            P.dma("gpsimd", alogbc[:], alog.squeeze(0).partition_broadcast(128), ldG)
            P.dma("gpsimd", dtbbc[:], dtb.squeeze(0).partition_broadcast(128), ldG)
            ld_ab = ldG.v

            _stage_gate(0.1)
            am_ = [0, 0]
            for m in range(NT):
                s = m % 2
                buf = xt16 if s == 0 else xsh16
                P.wait("gpsimd", acC, am_[s])
                lv = P.dma("gpsimd", buf[:], x_tok[128 * m:128 * (m + 1), :], ldG)
                P.wait("scalar", ldG, lv)
                P.op("scalar", lambda e, m=m, buf=buf: e.activation(
                    xcur16[:], buf[:], AF.Copy,
                    accum_out=st_sum[:, m:m + 1]), acC)
                P.op("scalar", lambda e, m=m, buf=buf: e.activation(
                    xcur16[:], buf[:], AF.Square,
                    accum_out=st_sq[:, m:m + 1]), acC)
                am_[s] = acC.v
            P.wait("vector", acC, acC.v)
            P.op("vector", lambda e: e.tensor_scalar(st_mu[:], st_sum[:], 1.0 / D, None, ALU.mult), veC)
            P.op("vector", lambda e: e.tensor_tensor(st_tmp[:], st_mu[:], st_mu[:], ALU.mult), veC)
            P.op("vector", lambda e: e.scalar_tensor_tensor(
                st_tmp[:], st_sq[:], 1.0 / D, st_tmp[:], ALU.mult, ALU.subtract), veC)
            v_var = P.op("vector", lambda e: e.tensor_scalar(
                st_tmp[:], st_tmp[:], 1e-5, None, ALU.add), veC)
            P.wait("scalar", veC, v_var)
            a_sq = P.op("scalar", lambda e: e.activation(st_tmp[:], st_tmp[:], AF.Sqrt), acC)
            P.wait("vector", acC, a_sq)
            v_r = P.op("vector", lambda e: e.reciprocal(st_r[:], st_tmp[:]), veC)
            P.op("gpsimd", lambda e: e.memset(st_mush[0:1, :], 0.0), poC)
            P.op("gpsimd", lambda e: e.memset(st_rsh[0:1, :], 1.0), poC)
            P.wait("gpsimd", veC, v_r)
            P.dma("gpsimd", st_mush[1:128, :], st_mu[0:127, :], ldG)
            P.dma("gpsimd", st_rsh[1:128, :], st_r[0:127, :], ldG)
            P.dma("gpsimd", st_mush[0:1, 1:16], st_mu[127:128, 0:15], ldG)
            P.dma("gpsimd", st_rsh[0:1, 1:16], st_r[127:128, 0:15], ldG)
            ld_sh = ldG.v
            st_done = (veC, v_r)

            tap("st_mu", st_mu[:], [st_done])
            tap("st_r", st_r[:], [st_done])

            _stage_gate(0.2)
            P.wait("tensor", veC, v_r)
            P.wait("tensor", ldS)
            m_t1 = P.op("tensor", lambda e: e.transpose(pb[0][0:16, 0:128], st_mu[:], c_eyef[:]), mmC)
            m_t2 = P.op("tensor", lambda e: e.transpose(pb[0][0:16, 128:256], st_r[:], c_eyef[:]), mmC)
            P.wait("vector", mmC, m_t2)
            v_rw = P.op("vector", lambda e: e.tensor_copy(rtmp[0:16, 0:128], pb[0][0:16, 0:128]), veC)
            v_rw = P.op("vector", lambda e: e.tensor_copy(rtmp[0:16, 128:256], pb[0][0:16, 128:256]), veC)
            P.wait("gpsimd", veC, v_rw)
            P.dma("gpsimd", rowscr[40:41, :].rearrange("o (m t) -> (o m) t", m=16), rtmp[0:16, 0:128], ldG)
            g_rw = P.dma("gpsimd", rowscr[41:42, :].rearrange("o (m t) -> (o m) t", m=16), rtmp[0:16, 128:256], ldG)
            P.wait("sync", ldG)
            s_bc1 = P.dma("sync", mrbc[:, 0:T], rowscr[40:41, :].squeeze(0).partition_broadcast(128), ldS)
            s_bc2 = P.dma("sync", mrbc[:, T:2 * T], rowscr[41:42, :].squeeze(0).partition_broadcast(128), ldS)
            P.wait("vector", ldS)
            for k in range(DC):
                P.op("vector", lambda e, k=k: e.tensor_tensor(
                    xtmp[:], xd[:, T * k:T * (k + 1)], mrbc[:, 0:T], ALU.subtract), veC)
                P.op("vector", lambda e, k=k: e.tensor_tensor(
                    xl1d[:, (T + 1) * k + 1:(T + 1) * (k + 1)], xtmp[:], mrbc[:, T:2 * T],
                    ALU.mult), veC)
            P.op("vector", lambda e: e.memset(
                xl1d[:, 0:DC * (T + 1)].rearrange("p (c t) -> p c t", c=DC)[:, :, 0:1], 0.0), veC)
            v_xl1 = veC.v
            tap("xl1d", xl1d[:, 0:T + 1], [(veC, v_xl1)])

            _stage_gate(0.3)
            P.wait("tensor", ldG)
            P.wait("tensor", veC, v_xl1)
            ba_ev = []
            for m in range(NT):
                if m >= 2:
                    P.wait("tensor", veC, ba_ev[m - 2])
                for k in range(DC):
                    P.op("tensor", lambda e, m=m, k=k: e.matmul(
                        pb[m % 2][0:128, 0:16],
                        xl1d[:, (T + 1) * k + 1 + 128 * m:(T + 1) * k + 1 + 128 * (m + 1)],
                        wba16[:, 16 * k:16 * (k + 1)],
                        start=(k == 0), stop=(k == DC - 1)),
                        mmC if k == DC - 1 else None)
                P.wait("vector", mmC, mmC.v)
                P.op("vector", lambda e, m=m: e.tensor_copy(
                    ba[:, 16 * m:16 * (m + 1)], pb[m % 2][0:128, 0:16]), veC)
                ba_ev.append(veC.v)
                if m == 0:
                    _stage_gate(0.32)
                if m == 1:
                    _stage_gate(0.33)
                if m == 2:
                    _stage_gate(0.34)
                if m == 3:
                    _stage_gate(0.36)
            v_ba = veC.v
            tap("ba", ba[:], [(veC, v_ba)])

            _stage_gate(0.4)
            bav = ba[:].rearrange("p (m k) -> p m k", k=16)
            braw = bav[:, :, 0:8]
            araw = bav[:, :, 8:16]
            d3 = lambda t: t.rearrange("p (m h) -> p m h", h=8)
            P.wait("scalar", veC, v_ba)
            a_beta = P.op("scalar", lambda e: e.activation(d3(sc_beta[:]), braw, AF.Sigmoid), acC)
            P.op("vector", lambda e: e.tensor_scalar(d3(sptmp[:]), braw, -1.0, None, ALU.mult), veC)
            v_ab = P.op("vector", lambda e: e.tensor_tensor(d3(sptmp[:]), braw, d3(sptmp[:]), ALU.min), veC)
            P.wait("scalar", veC, v_ab)
            a_e = P.op("scalar", lambda e: e.activation(sptmp[:], sptmp[:], AF.Exp), acC)
            a_l = P.op("scalar", lambda e: e.activation(sptmp[:], sptmp[:], AF.Ln, bias=1.0), acC)
            P.wait("vector", acC, a_l)
            P.op("vector", lambda e: e.tensor_scalar(d3(sc_lnb[:]), braw, 0.0, None, ALU.min), veC)
            v_lnb = P.op("vector", lambda e: e.tensor_tensor(
                sc_lnb[:], sc_lnb[:], sptmp[:], ALU.subtract), veC)
            P.wait("vector", acC, a_beta)
            dview = dtbbc[:].unsqueeze(1).broadcast_to([128, NT, NH])
            P.op("vector", lambda e: e.tensor_tensor(d3(sptmp2[:]), araw, dview, ALU.add), veC)
            P.op("vector", lambda e: e.tensor_scalar(sptmp[:], sptmp2[:], -1.0, None, ALU.mult), veC)
            v_z = P.op("vector", lambda e: e.tensor_tensor(sptmp[:], sptmp2[:], sptmp[:], ALU.min), veC)
            P.wait("scalar", veC, v_z)
            a_e2 = P.op("scalar", lambda e: e.activation(sptmp[:], sptmp[:], AF.Exp), acC)
            a_l2 = P.op("scalar", lambda e: e.activation(sptmp[:], sptmp[:], AF.Ln, bias=1.0), acC)
            P.wait("scalar", ldG)
            a_ax = P.op("scalar", lambda e: e.activation(alogbc[:], alogbc[:], AF.Exp), acC)
            P.wait("vector", acC, a_l2)
            P.op("vector", lambda e: e.tensor_scalar(sptmp2[:], sptmp2[:], 0.0, None, ALU.max), veC)
            P.op("vector", lambda e: e.tensor_tensor(sptmp2[:], sptmp2[:], sptmp[:], ALU.add), veC)
            P.wait("vector", acC)
            aview = alogbc[:].unsqueeze(1).broadcast_to([128, NT, NH])
            v_ld = P.op("vector", lambda e: e.scalar_tensor_tensor(
                d3(sc_ld16[:]), d3(sptmp2[:]), -1.0, aview, ALU.mult, ALU.mult), veC)
            tap("ld16", sc_ld16[:], [(veC, v_ld)])

            P.wait("tensor", veC, v_ld)
            P.wait("tensor", ldG)
            for m in range(NT):
                P.op("tensor", lambda e, m=m: e.matmul(
                    pb[2][0:128, 8 * m:8 * m + 8], c_cumtri[:],
                    sc_ld16[:, 8 * m:8 * (m + 1)], start=True, stop=True), None)
                P.op("tensor", lambda e, m=m: e.matmul(
                    pb[3][0:128, 8 * m:8 * m + 8], c_blksum[:],
                    sc_ld16[:, 8 * m:8 * (m + 1)], start=True, stop=True), None)
            m_cum = P.op("tensor", lambda e: e.nop(), mmC)
            P.wait("vector", mmC, m_cum)
            P.op("vector", lambda e: e.tensor_copy(sc_cum[:], pb[2][0:128, 0:128]), veC)
            v_cum = P.op("vector", lambda e: e.tensor_copy(sc_ccum[:], pb[3][0:128, 0:128]), veC)
            tap("cum", sc_cum[:], [(veC, v_cum)])

            am_ = [0, 0]
            tsm_ = [0, 0]
            for m in range(NT):
                s = m % 2
                buf = xt16 if s == 0 else xsh16
                cur = xcur16 if s == 0 else o16
                P.wait("gpsimd", veC, tsm_[s])
                lv = P.dma("gpsimd", buf[:], x_tok[128 * m:128 * (m + 1), :], ldG)
                P.wait("vector", ldG, lv)
                P.wait("vector", veC, v_cum)
                P.wait("vector", acC, am_[s])
                vv = P.op("vector", lambda e, m=m, buf=buf, cur=cur: e.tensor_scalar(
                    cur[:, 0:1024], buf[:], st_mu[:, m:m + 1], st_r[:, m:m + 1],
                    ALU.subtract, ALU.mult), veC)
                tsm_[s] = vv
                P.wait("scalar", veC, vv)
                for h in range(NH):
                    P.op("scalar", lambda e, m=m, h=h, cur=cur: e.activation(
                        sptmp2[:, 0:128], cur[:, 128 * h:128 * (h + 1)],
                        AF.Square, accum_out=sc_ss[:, 8 * m + h:8 * m + h + 1]), acC)
                am_[s] = acC.v
            a_ss = acC.v
            P.wait("vector", acC, a_ss)
            v_c = P.op("vector", lambda e: e.tensor_scalar(
                sc_lnn2[:], sc_ss[:], 1e-19, None, ALU.max), veC)
            P.wait("scalar", veC, v_c)
            a_ln = P.op("scalar", lambda e: e.activation(sc_lnn2[:], sc_lnn2[:], AF.Ln), acC)
            P.wait("gpsimd", acC, a_ln)
            P.op("gpsimd", lambda e: e.memset(sc_lnn2sh[0:1, 0:8], 0.0), poC)
            P.dma("gpsimd", sc_lnn2sh[1:128, :], sc_lnn2[0:127, :], ldG)
            g_sh = P.dma("gpsimd", sc_lnn2sh[0:1, 8:128], sc_lnn2[127:128, 0:120], ldG)

            P.wait("vector", ldG)
            P.wait("vector", veC, v_lnb)
            P.op("vector", lambda e: e.tensor_tensor(sc_p1[:], sc_cum[:], sc_lnb[:], ALU.add), veC)
            P.op("vector", lambda e: e.scalar_tensor_tensor(
                sc_p1[:], sc_lnn2sh[:], -0.5, sc_p1[:], ALU.mult, ALU.add), veC)
            P.op("vector", lambda e: e.scalar_tensor_tensor(
                sc_p2[:], sc_lnn2sh[:], 0.5, sc_cum[:], ALU.mult, ALU.add), veC)
            P.op("vector", lambda e: e.tensor_scalar(sc_p2[:], sc_p2[:], -1.0, None, ALU.mult), veC)
            P.op("vector", lambda e: e.scalar_tensor_tensor(
                sptmp[:], sc_lnn2[:], -0.5, sc_cum[:], ALU.mult, ALU.add), veC)
            P.op("vector", lambda e: e.tensor_tensor(sptmp2[:], sc_ccum[:], sc_cum[:], ALU.subtract), veC)
            P.op("vector", lambda e: e.scalar_tensor_tensor(
                sc_usc[:], sc_lnn2sh[:], -0.5, sptmp2[:], ALU.mult, ALU.add), veC)
            v_combo = veC.v
            P.wait("scalar", veC, v_combo)
            a_tg = P.op("scalar", lambda e: e.activation(sc_tg[:], sc_p1[:], AF.Exp, scale=1.0), acC)
            a_us = P.op("scalar", lambda e: e.activation(sc_usc[:], sc_usc[:], AF.Exp), acC)
            P.wait("vector", acC, a_tg)
            v_tg = P.op("vector", lambda e: e.tensor_scalar(sc_tg[:], sc_tg[:], -1.0, None, ALU.mult), veC)

            sckv = lambda k: sck[:].rearrange("p (m k h) -> p m k h", m=NT, k=6, h=8)[:, :, k:k + 1, :]
            scv = lambda t: t.rearrange("p (m h) -> p m h", m=NT).unsqueeze(2)
            P.op("vector", lambda e: e.tensor_copy(sckv(0), scv(sc_p1[:])), veC)
            P.op("vector", lambda e: e.tensor_copy(sckv(1), scv(sc_p2[:])), veC)
            P.op("vector", lambda e: e.tensor_copy(sckv(2), scv(sptmp[:])), veC)
            v_sck1 = veC.v
            P.wait("scalar", veC, v_sck1)
            a_rk4 = P.op("scalar", lambda e: e.activation(sckv(3), scv(sptmp[:]), AF.Exp), acC)
            a_ced = P.op("scalar", lambda e: e.activation(sckv(4), scv(sc_ccum[:]), AF.Exp), acC)
            P.wait("tensor", acC, a_ced)
            P.wait("tensor", veC, v_sck1)
            for m in range(NT):
                P.op("tensor", lambda e, m=m: e.transpose(
                    pb[2][0:40, 128 * (m % 4):128 * (m % 4) + 128],
                    sck[:, 48 * m:48 * m + 40], c_eyef[:]), mmC)
                if m % 4 == 3:
                    mv = mmC.v
                    P.wait("vector", mmC, mv)
                    for mm2 in range(m - 3, m + 1):
                        P.op("vector", lambda e, mm2=mm2: e.tensor_copy(
                            rows[0:40, 128 * mm2:128 * (mm2 + 1)],
                            pb[2][0:40, 128 * (mm2 % 4):128 * (mm2 % 4) + 128]), veC)
                    P.wait("tensor", veC, veC.v)
            v_rows = veC.v
            P.wait("gpsimd", veC, v_rows)
            g_rows = P.dma("gpsimd", rowscr[0:40, :], rows[0:40, :], ldG)
            P.dma("gpsimd", rows2.rearrange("m k (h c) -> m k h c", h=8),
                  rowscr[0:24, :].rearrange("(k h) (m c) -> m k h c", h=8, c=128), ldG)
            P.dma("gpsimd", rk42.rearrange("m (h c) -> m h c", h=8),
                  rowscr[24:32, :].rearrange("h (m c) -> m h c", c=128), ldG)
            for cc2 in range(2):
                g_rows2 = P.dma(
                    "gpsimd",
                    ced2.rearrange("m (h c) -> m h c", h=8)[:, :, cc2:cc2 + 1],
                    rowscr[32:40, 63 + 64 * cc2::128].rearrange("h m -> m h").unsqueeze(2),
                    ldG, allow_slow_non_contiguous=True)
            tap("rows", rows[:, 0:256], [(veC, v_rows)])


            _stage_gate(0.8)
            P.wait("tensor", ldS)
            P.op("vector", lambda e: e.memset(s16[:], 0.0), veC)
            P.wait("vector", acC)

            TP1 = T + 1
            hview = lambda t, w=128: t.rearrange("p (h t2) -> p h t2", t2=w)
            xlv = xl1d[:, 0:DC * TP1].rearrange("p (h t2) -> p h t2", t2=TP1)

            def scvw(t, c0, n, rows_=128, p0=0, w=128):
                return t[p0:p0 + rows_, c0:c0 + n].unsqueeze(2).broadcast_to([rows_, n, w])

            kview = lambda cst: cst[:].unsqueeze(1).broadcast_to([128, 4, 128])
            g4 = lambda t, base: t[:, base:base + 512].rearrange("p (a b) -> p a b", a=4)
            pbv = lambda i: pb[i][0:128, 0:512].rearrange("p (a b) -> p a b", a=4)
            exv = lambda oi, hg: expw[:, 1024 * oi + 512 * hg:1024 * oi + 512 * hg + 512].rearrange(
                "p (a b) -> p a b", a=4)

            x2_done = []
            fbm = {}
            abm = {}
            dblm = {}
            vgtm = {}
            veqm = {}
            ssttm = {}
            supdm = {}
            intram = {}
            oevm = {}
            vqum = {}
            rkdu = {}
            st8 = {}
            for mi in range(NT + 1):
                mp = mi
                msc = mi - 1
                if mp < NT:
                    co = 8 * mp
                    ml_p = mp % 2
                    sl = slice(128 * mp, 128 * mp + 128)
                    P.wait("sync", veC, abm.get(mp - 1, 0))
                    P.wait("sync", mmC, dblm.get(mp - 1, 0))
                    P.wait("sync", veC, vqum.get(mp - 2, 0))
                    P.wait("sync", veC, ssttm.get(mp - 2, 0))
                    P.wait("sync", ldG, g_rows2)
                    P.dma("sync", rk4bcx[ml_p][:],
                          rk42[mp:mp + 1, :].squeeze(0).partition_broadcast(128),
                          ldR, serial=False)
                    P.dma("sync", cedbcx[ml_p][:],
                          ced2[mp:mp + 1, :].squeeze(0).partition_broadcast(128),
                          ldR, serial=False)
                    P.dma("sync", fbc[:, 0:3072],
                          rows2[mp].rearrange("k f -> (k f)").partition_broadcast(128),
                          ldR, serial=False)
                    fbm[mp] = ldR.v

                    P.wait("tensor", veC)
                    for half in (0, 1):
                        for k in range(DC):
                            P.op("tensor", lambda e, mp=mp, k=k, half=half: e.matmul(
                                pb[half][:, 0:512],
                                xl1d[:, TP1 * k + 1 + 128 * mp:TP1 * k + 1 + 128 * (mp + 1)],
                                wwb[:, 1024 * k + 512 * half:1024 * k + 512 * half + 512],
                                start=(k == 0), stop=(k == DC - 1)),
                                mmC if k == DC - 1 else None)
                    P.wait("vector", mmC)
                    for half in (0, 1):
                        P.op("vector", lambda e, half=half: e.tensor_copy(
                            v16[:, 512 * half:512 * half + 512], pb[half][:, 0:512]), veC)

                    PKI = (1, 0, 1)
                    P.wait("vector", ldR, fbm[mp])
                    P.wait("scalar", mmC, dblm.get(mp - 1, 0))
                    a_exp = acC.v
                    ab3 = lambda t: t.rearrange("p (h t2) -> p h t2", t2=128)
                    for oi in range(3):
                        kc = c_kaddL if oi == 1 else c_kaddU
                        if oi > 0:
                            P.wait("vector", acC, a_exp)
                        P.op("vector", lambda e, oi=oi, mp=mp, ab3=ab3: e.tensor_tensor(
                            ab3(argbuf[:]),
                            ab3(fbc[:, 1024 * oi:1024 * oi + 1024]),
                            sck[:, 48 * mp + 8 * PKI[oi]:48 * mp + 8 * PKI[oi] + 8]
                            .unsqueeze(2).broadcast_to([128, 8, 128]), ALU.add), veC)
                        vv = P.op("vector", lambda e, kc=kc, ab3=ab3: e.tensor_tensor(
                            ab3(argbuf[:]), ab3(argbuf[:]),
                            kc[:].unsqueeze(1).broadcast_to([128, 8, 128]), ALU.add), veC)
                        P.wait("scalar", veC, vv)
                        a_exp = P.op("scalar", lambda e, oi=oi: e.activation(
                            expw[:, 1024 * oi:1024 * oi + 1024], argbuf[:], AF.Exp), acC)

                    P.wait("tensor", veC)
                    for hh in range(4):
                        for hg in (0, 1):
                            h = 4 * hg + hh
                            hb = TP1 * h
                            wk_sl = xl1d[:, hb + 128 * mp:hb + 128 * mp + 128]
                            rk_sl = xl1d[:, hb + 128 * mp + 1:hb + 128 * mp + 129]
                            bU = 2 - 2 * hg
                            P.op("tensor", lambda e, wk_sl=wk_sl, hh=hh, bU=bU: e.matmul(
                                pb[bU][0:128, 128 * hh:128 * hh + 128], wk_sl, wk_sl,
                                start=True, stop=True), mmC)
                            P.op("tensor", lambda e, wk_sl=wk_sl, rk_sl=rk_sl, hh=hh, bU=bU: e.matmul(
                                pb[bU + 1][0:128, 128 * hh:128 * hh + 128], wk_sl, rk_sl,
                                start=True, stop=True), mmC)
                    gm = mmC.v
                    P.wait("vector", mmC, gm)
                    P.wait("vector", acC, a_exp)
                    P.op("vector", lambda e: e.scalar_tensor_tensor(
                        g4(wkU, 0), pbv(2), -1.0, exv(0, 0), ALU.mult, ALU.mult), veC)
                    P.op("vector", lambda e: e.scalar_tensor_tensor(
                        g4(wkL, 0), pbv(2), -1.0, exv(1, 0), ALU.mult, ALU.mult), veC)
                    P.op("vector", lambda e: e.scalar_tensor_tensor(
                        g4(wkU2, 0), pbv(0), -1.0, exv(0, 1), ALU.mult, ALU.mult), veC)
                    P.op("vector", lambda e: e.scalar_tensor_tensor(
                        g4(wkL2, 0), pbv(0), -1.0, exv(1, 1), ALU.mult, ALU.mult), veC)
                    P.op("vector", lambda e: e.tensor_tensor(
                        g4(wkQ, 0), g4(wkU, 0), kview(c_eye16), ALU.add), veC)
                    P.op("vector", lambda e: e.tensor_tensor(
                        g4(wkQ2, 0), g4(wkU2, 0), kview(c_eye16), ALU.add), veC)
                    P.op("vector", lambda e, ml_p=ml_p: e.scalar_tensor_tensor(
                        g4(wkITx[ml_p], 0), pbv(3), 1.0, exv(2, 0), ALU.mult, ALU.mult), veC)
                    qs = [veC.v, veC.v]
                    P.op("vector", lambda e, ml_p=ml_p: e.scalar_tensor_tensor(
                        g4(wkITx[ml_p], 512), pbv(1), 1.0, exv(2, 1), ALU.mult, ALU.mult), veC)
                    qs[1] = veC.v
                    st8[mp] = {"qs": qs, "lvs": [0, 0], "uvs": [qs[1], qs[1]]}

                if msc >= 0:
                    cox = 8 * msc
                    ms2 = msc % 2
                    _scan_chunk = None

                def scan_chunk(msc, c, prev_sstt=0, prev_mgts=0):
                    cs = 64 * c
                    cox = 8 * msc
                    ms2 = msc % 2
                    if c == 0:
                        P.wait("vector", ldR, fbm[msc])
                        P.wait("vector", mmC, rkdu.get(msc - 1, 0))
                    else:
                        P.wait("vector", mmC, prev_mgts)
                    P.op("vector", lambda e, msc=msc, cs=cs, ms2=ms2: e.tensor_tensor(
                        rkds[:].rearrange("p (h t2) -> p h t2", t2=64),
                        xlv[:, :, 128 * msc + 1 + cs:128 * msc + 1 + cs + 64],
                        rk4bcx[ms2][:].rearrange("p (h t2) -> p h t2", t2=128)[:, :, cs:cs + 64],
                        ALU.mult), veC)
                    v_rk = veC.v
                    if c == 0:
                        P.wait("tensor", veC, ssttm.get(msc - 1, 0))
                        P.wait("tensor", veC, vgtm[msc])
                        P.wait("tensor", veC, oevm.get(msc - 1, 0))
                        P.wait("tensor", veC, vqum.get(msc - 1, 0))
                    else:
                        P.wait("tensor", veC, prev_sstt)
                    for h in range(NH):
                        hh = h % 4
                        P.op("tensor", lambda e, h=h, hh=hh, cs=cs: e.matmul(
                            pb[4 + h // 4][cs:cs + 64, 128 * hh:128 * hh + 128],
                            wkGT[:, 128 * h + cs:128 * h + cs + 64],
                            s16[:, 128 * h:128 * h + 128],
                            start=True, stop=True), mmC)
                    P.wait("tensor", veC, v_rk)
                    for h in range(NH):
                        hh = h % 4
                        P.op("tensor", lambda e, h=h, hh=hh, cs=cs: e.matmul(
                            pb[6 + h // 4][0:128, 128 * hh + cs:128 * hh + cs + 64],
                            s16[:, 128 * h:128 * h + 128],
                            rkds[:, 64 * h:64 * h + 64],
                            start=(cs == 0 and hh == 0), stop=False,
                            skip_group_check=True), mmC)
                    mgts = mmC.v
                    if c == 1:
                        rkdu[msc] = mgts
                    P.wait("vector", mmC, mgts)
                    for b in (0, 1):
                        P.op("vector", lambda e, b=b, cs=cs: e.tensor_tensor(
                            vq[cs:cs + 64, 512 * b:512 * b + 512],
                            eq16[cs:cs + 64, 512 * b:512 * b + 512],
                            pb[4 + b][cs:cs + 64, 0:512], ALU.add), veC)
                        P.op("vector", lambda e, b=b, cs=cs, cox=cox: e.tensor_tensor(
                            vqu[cs:cs + 64, 512 * b:512 * b + 512].rearrange(
                                "p (a b2) -> p a b2", a=4),
                            vq[cs:cs + 64, 512 * b:512 * b + 512].rearrange(
                                "p (a b2) -> p a b2", a=4),
                            scvw(sc_usc, cox + 4 * b, 4, rows_=64, p0=cs), ALU.mult), veC)
                    v_vq = veC.v
                    if c == 1:
                        vqum[msc] = v_vq
                    P.wait("tensor", veC, v_vq)
                    for h in range(NH):
                        hh = h % 4
                        P.op("tensor", lambda e, h=h, hh=hh, cs=cs, ms2=ms2: e.matmul(
                            pb[4 + h // 4][0:128, 128 * hh:128 * hh + 128],
                            xsh16[cs:cs + 64, 128 * h:128 * h + 128],
                            vqu[cs:cs + 64, 512 * (h // 4) + 128 * hh:512 * (h // 4) + 128 * hh + 128],
                            start=True, stop=True), mmC)
                    msup = mmC.v
                    if c == 1:
                        supdm[msc] = msup
                    P.wait("vector", mmC, msup)
                    for h in range(NH):
                        hh = h % 4
                        P.op("vector", lambda e, h=h, hh=hh, c=c, ms2=ms2: e.scalar_tensor_tensor(
                            s16[:, 128 * h:128 * h + 128], s16[:, 128 * h:128 * h + 128],
                            cedbcx[ms2][:, 2 * h + c:2 * h + c + 1],
                            pb[4 + h // 4][0:128, 128 * hh:128 * hh + 128],
                            ALU.mult, ALU.add), veC)
                    if c == 1:
                        ssttm[msc] = veC.v
                    return veC.v, mgts

                sc0 = {}
                if msc >= 0:
                    sc0[msc] = scan_chunk(msc, 0)

                if mp < NT:
                    st = st8[mp]
                    HGS = ((wkU, wkL, wkQ, 2, 3), (wkU2, wkL2, wkQ2, 0, 1))

                def dbl_levels(mp, l0, l1):
                    st = st8[mp]
                    qs, lvs, uvs = st["qs"], st["lvs"], st["uvs"]
                    HGS = ((wkU, wkL, wkQ, 2, 3), (wkU2, wkL2, wkQ2, 0, 1))
                    for lvl in range(l0, l1):
                        mul = [0, 0]
                        for hg in (0, 1):
                            U, L, Q, bU, bL = HGS[hg]
                            P.wait("tensor", veC, uvs[hg])
                            P.wait("tensor", veC, qs[hg])
                            P.wait("tensor", acC, lvs[hg])
                            if lvl < 5:
                                for hh in range(4):
                                    P.op("tensor", lambda e, hh=hh, U=U, L=L, bU=bU: e.matmul(
                                        pb[bU][0:128, 128 * hh:128 * hh + 128],
                                        L[:, 128 * hh:128 * hh + 128],
                                        U[:, 128 * hh:128 * hh + 128],
                                        start=True, stop=True), mmC)
                            for hh in range(4):
                                P.op("tensor", lambda e, hh=hh, U=U, L=L, bL=bL: e.matmul(
                                    pb[bL][0:128, 128 * hh:128 * hh + 128],
                                    U[:, 128 * hh:128 * hh + 128],
                                    L[:, 128 * hh:128 * hh + 128],
                                    start=True, stop=True), mmC)
                            mul[hg] = mmC.v
                        for hg in (0, 1):
                            U, L, Q, bU, bL = HGS[hg]
                            if lvl < 5:
                                P.wait("vector", mmC, mul[hg])
                                P.op("vector", lambda e, U=U, bU=bU: e.tensor_copy(
                                    g4(U, 0), pbv(bU)), veC)
                                uvs[hg] = veC.v
                            P.wait("scalar", mmC, mul[hg])
                            P.op("scalar", lambda e, L=L, bL=bL: e.activation(
                                g4(L, 0), pbv(bL), AF.Copy), acC)
                            lvs[hg] = acC.v
                        for hg in (0, 1):
                            U, L, Q, bU, bL = HGS[hg]
                            P.wait("tensor", acC, lvs[hg])
                            P.wait("tensor", veC, uvs[hg])
                            P.wait("tensor", veC, qs[hg])
                            for hh in range(4):
                                P.op("tensor", lambda e, hh=hh, L=L, Q=Q, bU=bU: e.matmul(
                                    pb[bU][0:128, 128 * hh:128 * hh + 128],
                                    L[:, 128 * hh:128 * hh + 128],
                                    Q[:, 128 * hh:128 * hh + 128],
                                    start=True, stop=True), mmC)
                            mq = mmC.v
                            P.wait("vector", mmC, mq)
                            P.op("vector", lambda e, Q=Q, bU=bU: e.scalar_tensor_tensor(
                                g4(Q, 0), pbv(bU), 1.0, g4(Q, 0), ALU.mult, ALU.add), veC)
                            qs[hg] = veC.v

                if mp < NT:
                    dbl_levels(mp, 1, 4)
                if msc >= 0:
                    scan_chunk(msc, 1, prev_sstt=sc0[msc][0], prev_mgts=sc0[msc][1])
                if mp < NT:
                    dbl_levels(mp, 4, 6)
                    st = st8.pop(mp)
                    qs = st["qs"]
                    co = 8 * mp
                    ml_p = mp % 2
                    P.wait("gpsimd", mmC, supdm.get(mp - 1, 0))
                    if mp == 0:
                        P.wait("vector", ldG)
                        P.op("vector", lambda e: e.memset(xshx16[0:1, :], 0.0), veC)
                        P.wait("gpsimd", veC)
                        ldX_t = P.dma("gpsimd", xshx16[1:128, :], x_tok[0:127, :], ldG)
                    else:
                        ldX_t = P.dma("gpsimd", xshx16[:],
                                      x_tok[128 * mp - 1:128 * mp + 127, :], ldG)
                    P.wait("vector", ldG, ldX_t)
                    P.wait("vector", mmC, supdm.get(mp - 1, 0))
                    P.op("vector", lambda e, mp=mp: e.tensor_scalar(
                        xsh16[:], xshx16[:], st_mush[:, mp:mp + 1], st_rsh[:, mp:mp + 1],
                        ALU.subtract, ALU.mult), veC)
                    v_xsh = veC.v
                    m_eg = 0
                    for hg in (0, 1):
                        hs = 4 * hg
                        wkQx = (wkQ, wkQ2)[hg]
                        bE = 2 - 2 * hg
                        if hg == 1:
                            P.wait("vector", mmC, m_eg)
                        P.op("vector", lambda e, hs=hs, co=co, wkQx=wkQx: e.tensor_tensor(
                            g4(wkAb, 0), g4(wkQx, 0), scvw(sc_beta, co + hs, 4), ALU.mult), veC)
                        P.op("vector", lambda e, hs=hs, co=co, wkQx=wkQx: e.tensor_tensor(
                            g4(wkAn, 0), g4(wkQx, 0), scvw(sc_tg, co + hs, 4), ALU.mult), veC)
                        vab = veC.v
                        P.wait("tensor", veC, max(vab, v_xsh))
                        for hh in range(4):
                            h = hs + hh
                            P.op("tensor", lambda e, h=h, hh=hh, bE=bE: e.matmul(
                                pb[bE][0:128, 128 * hh:128 * hh + 128],
                                wkAb[:, 128 * hh:128 * hh + 128],
                                v16[:, 128 * h:128 * h + 128],
                                start=True, stop=True), mmC)
                            P.op("tensor", lambda e, h=h, hh=hh, bE=bE: e.matmul(
                                pb[bE + 1][0:128, 128 * hh:128 * hh + 128],
                                xsh16[:, 128 * h:128 * h + 128],
                                wkAn[:, 128 * hh:128 * hh + 128],
                                start=True, stop=True), mmC)
                        m_eg = mmC.v
                        P.wait("vector", mmC, m_eg)
                        P.op("vector", lambda e, hg=hg, bE=bE: e.tensor_copy(
                            eq16[:, 512 * hg:512 * hg + 512].rearrange(
                                "p (a b) -> p a b", a=4), pbv(bE)), veC)
                        veqm[mp] = veC.v
                        P.op("vector", lambda e, hg=hg, bE=bE: e.tensor_copy(
                            wkGT[:, 512 * hg:512 * hg + 512].rearrange(
                                "p (a b) -> p a b", a=4), pbv(bE + 1)), veC)
                        vgtm[mp] = veC.v
                    abm[mp] = veC.v
                    dblm[mp] = mmC.v

                if msc >= 0:
                    cox = 8 * msc
                    ms2 = msc % 2
                    g = msc // 4
                    ml = msc % 4
                    P.wait("tensor", veC, vqum[msc])
                    for h in range(NH):
                        hh = h % 4
                        P.op("tensor", lambda e, h=h, hh=hh, ms2=ms2: e.matmul(
                            pb[6 + h // 4][0:128, 128 * hh:128 * hh + 128],
                            vq[:, 512 * (h // 4) + 128 * hh:512 * (h // 4) + 128 * hh + 128],
                            wkITx[ms2][:, 128 * h:128 * h + 128],
                            start=False, stop=True, skip_group_check=True), mmC)
                    intram[msc] = mmC.v
                    P.wait("vector", mmC, intram[msc])
                    o16v = o16[:].rearrange("p (h t2) -> p h t2", t2=512)
                    for b in (0, 1):
                        P.op("vector", lambda e, b=b, ml=ml: e.tensor_copy(
                            o16v[:, 4 * b:4 * b + 4, 128 * ml:128 * ml + 128],
                            pb[6 + b][0:128, 0:512].rearrange("p (a b2) -> p a b2", a=4)), veC)
                    oevm[msc] = veC.v
                    if ml == 3:
                        P.wait("tensor", veC)
                        for ec in range(DC):
                            pj = pb[ec % 2]
                            for k in range(DC):
                                P.op("tensor", lambda e, ec=ec, k=k, pj=pj: e.matmul(
                                    pj[:, 0:512],
                                    wrb[:, 1024 * k + 128 * ec:1024 * k + 128 * ec + 128],
                                    o16[:, 512 * k:512 * k + 512],
                                    start=(k == 0), stop=(k == DC - 1)),
                                    mmC if k == DC - 1 else None)
                            P.wait("vector", mmC)
                            P.op("vector", lambda e, ec=ec, g=g, pj=pj: e.tensor_tensor(
                                mstage[:, 0:512], pj[:, 0:512],
                                xd[:, 2048 * ec + 512 * g:2048 * ec + 512 * g + 512], ALU.add), veC)
                            P.op("vector", lambda e, ec=ec, g=g: e.tensor_tensor(
                                xd[:, 2048 * ec + 512 * g:2048 * ec + 512 * g + 512],
                                mstage[:, 0:512],
                                xl1d[:, TP1 * ec + 1 + 512 * g:TP1 * ec + 1 + 512 * g + 512],
                                ALU.add), veC)
                            if ec >= 1:
                                P.wait("tensor", veC)
                        x2_done.append(veC.v)

            tap("s16", s16[:], [(veC, veC.v)])
            tap("o16", o16[:], [(veC, veC.v)])
            tap("x2", xd[:, 0:T], [(veC, veC.v)])
            _stage_gate(1)

            _stage_gate(2)
            xl2q16 = ar.at_f16(16384, 16400)
            wgq16 = ar.at_f16(0, 24576)
            wuq16 = ar.at_f16(24584, 24576)
            deq_wld = ar.at_f32(12288, 512)
            deq_wld2 = ar.at_f32(12800, 512)
            deq_rint = ar.at_i32(13312, 512)
            sstgA = ar.at_f32(13824, 512, part=8)
            sstgB = ar.at_f32(14336, 512, part=8)
            ev16 = ar.at_f16(14848, 2048)
            sTs = ar.at_f32(12288, 3072)
            sTr = ar.at_f32(15360, 512)
            r2mu = ar.at_f32(UN0, 2048, part=1)
            r2sq = ar.at_f32(UN0 + 2048, 2048, part=1)
            r2tmp = ar.at_f32(UN0 + 4096, 2048, part=1)
            h16t = ar.at_f16(0, 2048)
            rgt = ar.at_i32(1024, 2048)
            tft = ar.at_f32(3072, 2048)
            ptt = ar.at_f32(5120, 2048)
            wdq16 = ar.at_f16(24584, 24576)
            of16 = ar.at_f16(0, 12288)
            qdt = ar.at_f32(6144, 6144)
            x2g = ar.at_f32(12288, 512)
            yt = ar.at_f32(12800, 512)

            P.wait("tensor", veC)
            P.wait("scalar", veC)
            for t in range(TG):
                for k in range(DC):
                    xsl = xd[:, 2048 * k + 512 * t:2048 * k + 512 * t + 512]
                    av = P.op("scalar", lambda e, xsl=xsl: e.activation(
                        mstage[:], xsl, AF.Square), acC)
                    P.op("tensor", lambda e, xsl=xsl, k=k: e.matmul(
                        pb[0][0:1, 0:512], c_onescol[:], xsl,
                        start=(k == 0), stop=(k == DC - 1)), mmC)
                    P.wait("tensor", acC, av)
                    mv = P.op("tensor", lambda e, k=k: e.matmul(
                        pb[1][0:1, 0:512], c_onescol[:], mstage[:],
                        start=(k == 0), stop=(k == DC - 1)), mmC)
                    P.wait("scalar", mmC, mv)
                P.wait("vector", mmC)
                P.op("vector", lambda e, t=t: e.tensor_copy(
                    r2mu[:, 512 * t:512 * t + 512], pb[0][0:1, 0:512]), veC)
                P.op("vector", lambda e, t=t: e.tensor_copy(
                    r2sq[:, 512 * t:512 * t + 512], pb[1][0:1, 0:512]), veC)
                P.wait("tensor", veC)
            P.op("vector", lambda e: e.tensor_scalar(
                r2mu[:], r2mu[:], 1.0 / D, None, ALU.mult), veC)
            P.op("vector", lambda e: e.tensor_tensor(r2tmp[:], r2mu[:], r2mu[:], ALU.mult), veC)
            P.op("vector", lambda e: e.scalar_tensor_tensor(
                r2sq[:], r2sq[:], 1.0 / D, r2tmp[:], ALU.mult, ALU.subtract), veC)
            vv = P.op("vector", lambda e: e.tensor_scalar(
                r2sq[:], r2sq[:], 1e-5, None, ALU.add), veC)
            P.wait("scalar", veC, vv)
            av = P.op("scalar", lambda e: e.activation(r2sq[:], r2sq[:], AF.Sqrt), acC)
            P.wait("vector", acC, av)
            P.op("vector", lambda e: e.reciprocal(r2sq[:], r2sq[:]), veC)
            P.wait("gpsimd", veC)
            P.dma("gpsimd", rowscr[42:43, :], r2mu[:], ldG)
            g_r2 = P.dma("gpsimd", rowscr[43:44, :], r2sq[:], ldG)
            P.wait("sync", ldG)
            P.wait("sync", veC)
            s_b1 = P.dma("sync", mrbc[:, 0:T], rowscr[42:43, :].squeeze(0).partition_broadcast(128), ldS)
            s_b2 = P.dma("sync", mrbc[:, T:2 * T], rowscr[43:44, :].squeeze(0).partition_broadcast(128), ldS)

            P.wait("gpsimd", veC)
            for k in range(DC):
                P.dma("gpsimd", sp_x2[128 * k:128 * (k + 1), :], xd[:, 2048 * k:2048 * (k + 1)], spS)
            sp_x2_done = spS.v
            P.wait("vector", ldS)
            P.op("vector", lambda e: e.memset(mxc[:], 0.0), veC)
            P.op("vector", lambda e: e.memset(ccs[:], 0.0), veC)
            for k in range(DC):
                P.op("vector", lambda e, k=k: e.tensor_tensor(
                    xtmp[:], xd[:, 2048 * k:2048 * (k + 1)], mrbc[:, 0:T], ALU.subtract), veC)
                P.op("vector", lambda e: e.tensor_tensor(
                    xtmp[:], xtmp[:], mrbc[:, T:2 * T], ALU.mult), veC)
                P.op("vector", lambda e: e.tensor_reduce(
                    mxc[:, 1:2], xtmp[:], AX.X, ALU.max, apply_absolute_value=True), veC)
                P.op("vector", lambda e: e.tensor_tensor(
                    mxc[:, 0:1], mxc[:, 0:1], mxc[:, 1:2], ALU.max), veC)

            def cross_max(src_col, ar_i, slot=0):
                P.wait("tensor", veC)
                P.wait("tensor", acC)
                tv = P.op("tensor", lambda e, src_col=src_col: e.transpose(
                    pb[0][0:1, 0:128], src_col, c_eyef[:]), mmC)
                P.wait("vector", mmC, tv)
                P.op("vector", lambda e, slot=slot: e.tensor_reduce(
                    ccs[0:1, slot:slot + 1], pb[0][0:1, 0:128], AX.X, ALU.max), veC)

            def run_ar(ar_i, nslots=16):
                P.wait("gpsimd", veC)
                P.dma("gpsimd", cc_in[ar_i], ccs[:], ldG)
                P.wait("gpsimd", ldG)
                P.op("gpsimd", lambda e, ar_i=ar_i: e.collective_compute(
                    "AllReduce", ALU.max, replica_groups=[list(range(NCORES))],
                    ins=[cc_in[ar_i].opt()], outs=[cc_out[ar_i].opt()]), poC)
                P.wait("gpsimd", poC)
                P.dma("gpsimd", arbc[:], cc_out[ar_i].squeeze(0).partition_broadcast(128), ldG)
                return ldG.v

            cross_max(mxc[:, 0:1], 0, 0)
            arv = run_ar(0)
            P.wait("vector", ldG, arv)
            P.op("vector", lambda e: e.tensor_scalar(
                qsc[:, 0:1], arbc[:, 0:1], 1.0 / QMAX, 1e-12, ALU.mult, ALU.max), veC)
            P.op("vector", lambda e: e.reciprocal(qsc[:, 1:2], qsc[:, 0:1]), veC)

            for k in range(DC):
                P.op("vector", lambda e, k=k: e.tensor_tensor(
                    xtmp[:], xd[:, 2048 * k:2048 * (k + 1)], mrbc[:, 0:T], ALU.subtract), veC)
                P.op("vector", lambda e: e.tensor_tensor(
                    xtmp[:], xtmp[:], mrbc[:, T:2 * T], ALU.mult), veC)
                P.op("vector", lambda e: e.tensor_scalar(
                    xtmp[:].bitcast(I32), xtmp[:], qsc[:, 1:2], None, ALU.mult), veC)
                P.op("vector", lambda e, k=k: e.tensor_scalar(
                    xl2q16[:, 2048 * k:2048 * (k + 1)], xtmp[:].bitcast(I32),
                    qsc[:, 0:1], None, ALU.mult), veC)
            v_xl2q = veC.v
            tap("xl2q", xl2q16[:, 0:T], [(veC, v_xl2q)])

            swld = ar.at_f32(7680, 3072)
            sTr = ar.at_f32(0, 3072)
            sTr2 = ar.at_f32(10752, 1024)

            def emit_scales(nat, sTd, sTrd, ntile, ncols, sTrt=None):
                nblk = ncols // 32
                sTrt = sTr if sTrt is None else sTrt
                nslot = 2 if 2 * ncols <= 3072 else 1
                for i in range(4):
                    P.wait("vector", spl[i], spl_v[i])
                P.wait("sync", veC)
                P.wait("sync", spS)
                P.wait("sync", mmC)
                sw_u = [0] * nslot
                for it in range(ntile):
                    s = it % nslot
                    P.wait("sync", veC, sw_u[s])
                    lv = P.dma("sync", swld[:, ncols * s:ncols * s + ncols],
                               nat[128 * it:128 * it + 128, :], swl[s], serial=True)
                    P.wait("vector", swl[s], lv)
                    P.op("vector", lambda e, s=s, ncols=ncols, nblk=nblk: e.tensor_reduce(
                        mstage[:, 0:nblk],
                        swld[:, ncols * s:ncols * s + ncols].rearrange("p (b w) -> p b w", w=32),
                        AX.X, ALU.max, apply_absolute_value=True), veC)
                    sw_u[s] = veC.v
                    P.op("vector", lambda e, nblk=nblk: e.tensor_scalar(
                        mstage[:, 0:nblk], mstage[:, 0:nblk], 1.0 / WQ, 1e-12,
                        ALU.mult, ALU.max), veC)
                    P.wait("tensor", veC)
                    tv = P.op("tensor", lambda e, nblk=nblk: e.transpose(
                        pb[0][0:nblk, 0:128], mstage[:, 0:nblk], c_eyef[:]), mmC)
                    P.wait("vector", mmC, tv)
                    P.op("vector", lambda e, it=it, nblk=nblk: e.tensor_copy(
                        sTs[0:nblk, 128 * it:128 * it + 128], pb[0][0:nblk, 0:128]), veC)
                P.wait("tensor", veC)
                P.op("vector", lambda e, ntile=ntile, nblk=nblk, sTrt=sTrt: e.reciprocal(
                    sTrt[0:nblk, 0:ntile * 128], sTs[0:nblk, 0:ntile * 128]), veC)
                P.wait("gpsimd", veC)
                P.dma("gpsimd", sTd, sTs[0:nblk, 0:ntile * 128], ldG)
                v = P.dma("gpsimd", sTrd, sTrt[0:nblk, 0:ntile * 128], ldG)
                return v

            def emit_deq_prep():
                P.wait("tensor", veC)
                P.wait("tensor", acC)
                P.wait("sync", ldG)
                return {"wld_u": [0, 0], "sMM_u": [0, 0], "bank_u": [0, 0], "i": 0}

            def emit_deq_pc(st, wT, sTd, sTrd, dst16, nk, kcols, blk_per_128, pc):
                c0 = 512 * pc
                for k in range(nk):
                    s = st["i"] % 2
                    wld = deq_wld if s == 0 else deq_wld2
                    sst = sstgA if s == 0 else sstgB
                    bR = pb[2 + 2 * s]
                    bS = pb[3 + 2 * s]
                    P.wait("sync", veC, st["wld_u"][s])
                    lw = P.dma("sync", wld[:, 0:512],
                               wT[128 * k:128 * k + 128, c0:c0 + 512],
                               wls[s], serial=True)
                    P.wait("sync", mmC, st["sMM_u"][s])
                    row = blk_per_128 * k
                    P.dma("sync", sst[0:4, 0:512], sTrd[row:row + 4, c0:c0 + 512],
                          sls[s], serial=True)
                    ls = P.dma("sync", sst[4:8, 0:512], sTd[row:row + 4, c0:c0 + 512],
                               sls[s], serial=False)
                    P.wait("tensor", sls[s], ls)
                    P.wait("tensor", veC, st["bank_u"][s])
                    P.op("tensor", lambda e, bR=bR, sst=sst: e.matmul(
                        bR[:, 0:512], c_ind8[0:8, 0:128], sst[0:8, 0:512],
                        start=True, stop=True), mmC)
                    mmv = P.op("tensor", lambda e, bS=bS, sst=sst: e.matmul(
                        bS[:, 0:512], c_ind8[0:8, 128:256], sst[0:8, 0:512],
                        start=True, stop=True), mmC)
                    st["sMM_u"][s] = mmv
                    P.wait("vector", wls[s], lw)
                    P.wait("vector", mmC, mmv)
                    P.op("vector", lambda e, wld=wld, bR=bR: e.tensor_tensor(
                        deq_rint[:, 0:512], wld[:, 0:512], bR[:, 0:512], ALU.mult), veC)
                    P.op("vector", lambda e, k=k, c0=c0, dst16=dst16, kcols=kcols, bS=bS: e.tensor_tensor(
                        dst16[:, kcols * k + c0:kcols * k + c0 + 512],
                        deq_rint[:, 0:512], bS[:, 0:512], ALU.mult), veC)
                    st["wld_u"][s] = veC.v
                    st["bank_u"][s] = veC.v
                    st["i"] += 1
                return veC.v

            def emit_deq(wT, sTd, sTrd, dst16, nk, kcols, blk_per_128):
                st = emit_deq_prep()
                for pc in range(kcols // 512):
                    emit_deq_pc(st, wT, sTd, sTrd, dst16, nk, kcols, blk_per_128, pc)
                return veC.v

            spl_v = [0, 0, 0, 0]

            def emit_big_mms_range(w16, spill, mx_slot, kcols, its, evst):
                for it in its:
                    for t in range(TG):
                        key = evst["key"]
                        evst["key"] += 1
                        pj = pb[key % 2]
                        sl = key % 4
                        if key >= 2:
                            P.wait("tensor", veC, evst["v"][key - 2])
                            P.wait("tensor", acC, evst["a"][key - 2])
                        for k in range(DC):
                            P.op("tensor", lambda e, pj=pj, k=k, it=it, t=t, w16=w16, kcols=kcols: e.matmul(
                                pj[:, 0:512],
                                w16[:, kcols * k + 128 * it:kcols * k + 128 * it + 128],
                                xl2q16[:, 2048 * k + 512 * t:2048 * k + 512 * t + 512],
                                start=(k == 0), stop=(k == DC - 1)),
                                mmC if k == DC - 1 else None)
                        mmv = mmC.v
                        P.wait("scalar", mmC, mmv)
                        P.wait("scalar", spl[sl], spl_v[sl])
                        P.op("scalar", lambda e, pj=pj, sl=sl: e.activation(
                            ev16[:, 512 * sl:512 * sl + 512], pj[:, 0:512], AF.Copy), acC)
                        evst["a"][key] = acC.v
                        P.wait("vector", acC, evst["a"][key])
                        P.op("vector", lambda e, pj=pj: e.tensor_reduce(
                            mxc[:, 6:7], pj[:, 0:512], AX.X, ALU.max,
                            apply_absolute_value=True), veC)
                        P.op("vector", lambda e, mx_slot=mx_slot: e.tensor_tensor(
                            mxc[:, mx_slot:mx_slot + 1], mxc[:, mx_slot:mx_slot + 1],
                            mxc[:, 6:7], ALU.max), veC)
                        evst["v"][key] = veC.v
                        P.wait("sync", acC, evst["a"][key])
                        spl_v[sl] = P.dma(
                            "sync", spill[128 * it:128 * it + 128, 512 * t:512 * t + 512],
                            ev16[:, 512 * sl:512 * sl + 512], spl[sl], serial=False)

            def gated_path(nat, sTd, sTrd, wT, dst16, spill, mx_slot):
                sv = emit_scales(nat, sTd, sTrd, 24, D)
                P.wait("gpsimd", ldG, sv)
                st = emit_deq_prep()
                P.op("vector", lambda e, mx_slot=mx_slot: e.memset(
                    mxc[:, mx_slot:mx_slot + 1], 0.0), veC)
                evst = {"v": {}, "a": {}, "key": 0}
                for pc in range(FI // 512):
                    dv = emit_deq_pc(st, wT, sTd, sTrd, dst16, DC, FI, 4, pc)
                    P.wait("tensor", veC, dv)
                    emit_big_mms_range(dst16, spill, mx_slot, FI,
                                       range(4 * pc, 4 * pc + 4), evst)

            _stage_gate(3)
            P.wait("gpsimd", veC, v_xl2q)
            P.wait("gpsimd", spS, sp_x2_done)
            gated_path(wg_nat, sT_g, sTr_g, wgT, wgq16, sp_gate, 2)
            cross_max(mxc[:, 2:3], 1, 0)
            arv = run_ar(1)
            P.wait("vector", ldG, arv)
            P.op("vector", lambda e: e.tensor_scalar(
                qsc[:, 2:3], arbc[:, 0:1], 1.0 / QMAX, 1e-12, ALU.mult, ALU.max), veC)
            P.op("vector", lambda e: e.reciprocal(qsc[:, 3:4], qsc[:, 2:3]), veC)

            _stage_gate(5)
            gated_path(wu_nat, sT_u, sTr_u, wuT, wuq16, sp_up, 4)

            _stage_gate(5.5)
            sv = emit_scales(wd_nat, sT_d, sTr_d, DC, FI, sTrt=sTr2)
            P.wait("gpsimd", ldG, sv)
            emit_deq(wdT, sT_d, sTr_d, wdq16, IT, D, 4)

            _stage_gate(6)
            pttB = [ptt, ar.at_f32(16384, 2048)]
            rgtB = [rgt, ar.at_i32(18432, 2048)]
            tftB = [tft, ar.at_f32(20480, 2048)]
            h16B = [h16t, ar.at_f16(22528, 2048)]
            hRm = [0, 0]
            rgU = [0, 0]
            tfW = [0, 0]
            spPv = [0, 0]
            P.op("vector", lambda e: e.memset(mxc[:, 5:6], 0.0), veC)
            P.wait("scalar", veC)
            for i in range(4):
                P.wait("gpsimd", spl[i], spl_v[i])
                P.wait("sync", spl[i], spl_v[i])
            for it in range(IT):
                s = it % 2
                P.wait("sync", acC, hRm[s])
                lv = P.dma("sync", h16B[s][:, 0:T], sp_gate[128 * it:128 * it + 128, :],
                           wls[s], serial=True)
                P.wait("scalar", wls[s], lv)
                P.wait("scalar", veC, rgU[s])
                rv = P.op("scalar", lambda e, s=s: e.activation(
                    rgtB[s][:, 0:T], h16B[s][:, 0:T], AF.Copy, scale=qsc[:, 3:4]), acC)
                hRm[s] = rv
                P.wait("scalar", veC, tfW[s])
                av = P.op("scalar", lambda e, s=s: e.activation(
                    tftB[s][:, 0:T], rgtB[s][:, 0:T], AF.Sigmoid, scale=qsc[:, 2:3]), acC)
                P.wait("vector", acC, av)
                P.op("vector", lambda e, s=s: e.tensor_scalar(
                    tftB[s][:, 0:T].bitcast(I32), tftB[s][:, 0:T], 65536.0, 65535.0,
                    ALU.mult, ALU.min), veC)
                P.wait("vector", spP[s], spPv[s])
                P.op("vector", lambda e, s=s: e.tensor_tensor(
                    pttB[s][:, 0:T], rgtB[s][:, 0:T], tftB[s][:, 0:T].bitcast(I32),
                    ALU.mult), veC)
                rgU[s] = veC.v
                tfW[s] = veC.v
                P.op("vector", lambda e, s=s: e.tensor_reduce(
                    mxc[:, 6:7], pttB[s][:, 0:T], AX.X, ALU.max,
                    apply_absolute_value=True), veC)
                P.op("vector", lambda e: e.tensor_tensor(
                    mxc[:, 5:6], mxc[:, 5:6], mxc[:, 6:7], ALU.max), veC)
                P.wait("gpsimd", veC)
                spPv[s] = P.dma("gpsimd", sp_P[128 * it:128 * it + 128, :],
                                pttB[s][:, 0:T], spP[s], serial=False)
            cross_max(mxc[:, 4:5], 2, 0)
            cross_max(mxc[:, 5:6], 2, 1)
            arv = run_ar(2)
            P.wait("vector", ldG, arv)
            P.op("vector", lambda e: e.tensor_scalar(
                qsc[:, 4:5], arbc[:, 0:1], 1.0 / QMAX, 1e-12, ALU.mult, ALU.max), veC)
            P.op("vector", lambda e: e.reciprocal(qsc[:, 5:6], qsc[:, 4:5]), veC)
            P.op("vector", lambda e: e.tensor_scalar(
                mxc[:, 7:8], arbc[:, 1:2], qsc[:, 2:3], 1.0 / 65536.0, ALU.mult, ALU.mult), veC)
            P.op("vector", lambda e: e.tensor_scalar(
                qsc[:, 6:7], mxc[:, 7:8], 1.0 / QMAX, 1e-12, ALU.mult, ALU.max), veC)
            P.op("vector", lambda e: e.reciprocal(qsc[:, 7:8], qsc[:, 6:7]), veC)
            P.op("vector", lambda e: e.tensor_scalar(
                qsc[:, 7:8], qsc[:, 7:8], qsc[:, 2:3], 1.0 / 65536.0, ALU.mult, ALU.mult), veC)

            _stage_gate(7)
            P.op("vector", lambda e: e.memset(mxc[:, 5:6], 0.0), veC)
            P.wait("scalar", veC)
            for i in range(4):
                P.wait("gpsimd", spl[i], spl_v[i])
                P.wait("sync", spl[i], spl_v[i])
            for it in range(IT):
                s = it % 2
                P.wait("sync", acC, hRm[s])
                lv = P.dma("sync", h16B[s][:, 0:T], sp_up[128 * it:128 * it + 128, :],
                           wls[s], serial=True)
                P.wait("gpsimd", veC, tfW[s])
                lv2 = P.dma("gpsimd", tftB[s][:, 0:T], sp_P[128 * it:128 * it + 128, :],
                            ldG, serial=True)
                P.wait("scalar", wls[s], lv)
                P.wait("scalar", veC, rgU[s])
                rv = P.op("scalar", lambda e, s=s: e.activation(
                    rgtB[s][:, 0:T], h16B[s][:, 0:T], AF.Copy, scale=qsc[:, 5:6]), acC)
                hRm[s] = rv
                P.wait("scalar", ldG, lv2)
                av = P.op("scalar", lambda e, s=s: e.activation(
                    tftB[s][:, 0:T].bitcast(I32), tftB[s][:, 0:T], AF.Copy,
                    scale=qsc[:, 7:8]), acC)
                P.wait("vector", acC, av)
                P.wait("vector", spP[s], spPv[s])
                P.op("vector", lambda e, s=s: e.tensor_tensor(
                    pttB[s][:, 0:T], tftB[s][:, 0:T].bitcast(I32), rgtB[s][:, 0:T],
                    ALU.mult), veC)
                rgU[s] = veC.v
                tfW[s] = veC.v
                P.op("vector", lambda e, s=s: e.tensor_reduce(
                    mxc[:, 6:7], pttB[s][:, 0:T], AX.X, ALU.max,
                    apply_absolute_value=True), veC)
                P.op("vector", lambda e: e.tensor_tensor(
                    mxc[:, 5:6], mxc[:, 5:6], mxc[:, 6:7], ALU.max), veC)
                P.wait("gpsimd", veC)
                spPv[s] = P.dma("gpsimd", sp_Q[128 * it:128 * it + 128, :],
                                pttB[s][:, 0:T], spP[s], serial=False)
            cross_max(mxc[:, 5:6], 3, 0)
            arv = run_ar(3)
            P.wait("vector", ldG, arv)
            P.op("vector", lambda e: e.tensor_scalar(
                mxc[:, 7:8], arbc[:, 0:1], qsc[:, 6:7], None, ALU.mult), veC)
            P.op("vector", lambda e: e.tensor_tensor(
                mxc[:, 7:8], mxc[:, 7:8], qsc[:, 4:5], ALU.mult), veC)
            P.op("vector", lambda e: e.tensor_scalar(
                qsc[:, 8:9], mxc[:, 7:8], 1.0 / QMAX, 1e-12, ALU.mult, ALU.max), veC)
            P.op("vector", lambda e: e.reciprocal(qsc[:, 9:10], qsc[:, 8:9]), veC)
            P.op("vector", lambda e: e.tensor_scalar(
                qsc[:, 9:10], qsc[:, 9:10], qsc[:, 6:7], None, ALU.mult), veC)
            P.op("vector", lambda e: e.tensor_tensor(
                qsc[:, 9:10], qsc[:, 9:10], qsc[:, 4:5], ALU.mult), veC)

            _stage_gate(8)
            of16B = ar.at_f16(16384, 12288)
            x2gB = ar.at_f32(13312, 512)
            ytB = ar.at_f32(13824, 512)
            qld_v = [0, 0]
            x2u_v = [0, 0]
            sty_v = [0, 0]
            dn_v = {}
            dn_a = {}

            qdt_u = [0, 0]

            def build_of(t, dst):
                def one(it):
                    p = it % 2
                    P.wait("sync", veC, qdt_u[p])
                    lv = P.dma("sync", qdt[:, 1024 * p:1024 * p + 512],
                               sp_Q[128 * it:128 * it + 128, 512 * t:512 * t + 512],
                               qls[p], serial=True)
                    P.wait("vector", qls[p], lv)
                    P.op("vector", lambda e, p=p: e.tensor_scalar(
                        qdt[:, 1024 * p + 512:1024 * p + 1024].bitcast(I32),
                        qdt[:, 1024 * p:1024 * p + 512], qsc[:, 9:10],
                        None, ALU.mult), veC)
                    P.op("vector", lambda e, it=it, p=p, dst=dst: e.tensor_scalar(
                        dst[:, 512 * it:512 * it + 512],
                        qdt[:, 1024 * p + 512:1024 * p + 1024].bitcast(I32),
                        qsc[:, 8:9], None, ALU.mult), veC)
                    qdt_u[p] = veC.v
                return one

            buf = [of16, of16B]
            P.wait("sync", spS)
            P.wait("sync", spP[0], spPv[0])
            P.wait("sync", spP[1], spPv[1])
            P.wait("sync", ldG)
            P.wait("sync", veC)
            x2u_v[0] = veC.v
            x2u_v[1] = veC.v
            b0 = build_of(0, buf[0])
            for it in range(IT):
                b0(it)
            for t in range(TG):
                cur = buf[t % 2]
                bnext = build_of(t + 1, buf[(t + 1) % 2]) if t + 1 < TG else None
                P.wait("tensor", veC)
                P.wait("tensor", acC)
                for hc in range(DC):
                    key = t * DC + hc
                    pj = pb[key % 4]
                    s = key % 2
                    xg = x2g if s == 0 else x2gB
                    yy = yt if s == 0 else ytB
                    P.wait("sync", veC, x2u_v[s])
                    lx = P.dma("sync", xg[:, 0:512],
                               sp_x2[128 * hc:128 * hc + 128, 512 * t:512 * t + 512],
                               lxs[s], serial=True)
                    if key >= 4:
                        P.wait("tensor", veC, dn_v[key - 4])
                    for it in range(IT):
                        P.op("tensor", lambda e, pj=pj, hc=hc, it=it, cur=cur: e.matmul(
                            pj[:, 0:512],
                            wdq16[:, D * it + 128 * hc:D * it + 128 * hc + 128],
                            cur[:, 512 * it:512 * it + 512],
                            start=(it == 0), stop=(it == IT - 1)),
                            mmC if it == IT - 1 else None)
                    mmv = mmC.v
                    P.wait("vector", lxs[s], lx)
                    P.wait("vector", mmC, mmv)
                    P.wait("vector", sty[s], sty_v[s])
                    P.op("vector", lambda e, pj=pj, yy=yy, xg=xg: e.tensor_tensor(
                        yy[:, 0:512], pj[:, 0:512], xg[:, 0:512], ALU.add), veC)
                    dn_v[key] = veC.v
                    x2u_v[s] = veC.v
                    P.wait("sync", veC, dn_v[key])
                    sty_v[s] = P.dma(
                        "sync", y[128 * hc:128 * hc + 128, 512 * t:512 * t + 512],
                        yy[:, 0:512], sty[s], serial=False)
                    if bnext is not None:
                        for j in range(3):
                            bnext(3 * hc + j)


        except _StageDone:
            pass
        _final_emit()

        block = ctx.enter_context(nc.Block())

        def mk(eng_name):
            def body(eng):
                for fn in P.q[eng_name]:
                    fn(eng)
            return body
        block.sync(mk("sync"))
        block.scalar(mk("scalar"))
        block.vector(mk("vector"))
        block.tensor(mk("tensor"))
        block.gpsimd(mk("gpsimd"))

    return nc, taps


def prepare_inputs(inputs):
    x = np.asarray(inputs["L0_attn_out"], dtype=np.float32)
    ww = np.asarray(inputs["L0_attn_W_write"], dtype=np.float32)
    wr = np.asarray(inputs["L0_attn_W_read"], dtype=np.float32)
    wa = np.asarray(inputs["L0_attn_W_alpha"], dtype=np.float32)
    wb = np.asarray(inputs["L0_attn_W_beta"], dtype=np.float32)
    alog = np.asarray(inputs["L0_attn_A_log"], dtype=np.float32).reshape(1, NH)
    dtb = np.asarray(inputs["L0_attn_dt_bias"], dtype=np.float32).reshape(1, NH)
    wg = np.asarray(inputs["L0_ffn_w_gate"], dtype=np.float32)
    wu = np.asarray(inputs["L0_ffn_w_up"], dtype=np.float32)
    wd = np.asarray(inputs["L0_ffn_w_down"], dtype=np.float32)

    shared = {
        "wwT": np.ascontiguousarray(ww.T),
        "wrT": np.ascontiguousarray(wr.T),
        "wbaT": np.ascontiguousarray(np.concatenate([wb.T, wa.T], axis=1)),
        "alog": np.exp(alog) if False else alog,
        "dtb": dtb,
        "wuT": np.ascontiguousarray(wu.T),
        "wgT": np.ascontiguousarray(wg.T),
        "wdT": np.ascontiguousarray(wd.T),
        "wu_nat": wu, "wg_nat": wg, "wd_nat": wd,
        "eyef32": np.eye(128, dtype=np.float32),
        "eye16": np.eye(128, dtype=np.float16),
        "ones1": np.ones((1, 128), dtype=np.float32),
    }
    j = np.arange(128)[:, None]
    i = np.arange(128)[None, :]
    same = (j // 64) == (i // 64)
    shared["cumtri16"] = (same & (j <= i)).astype(np.float16)
    shared["blksum16"] = same.astype(np.float16)
    keepU = (same & (j < i))
    keepL = (same & (j > i))
    shared["kaddU16"] = np.where(keepU, 0.0, -60000.0).astype(np.float16)
    shared["kaddL16"] = np.where(keepL, 0.0, -60000.0).astype(np.float16)
    shared["onescol"] = np.ones((128, 1), dtype=np.float32)
    shared["ones1k"] = np.ones((1, 1024), dtype=np.float32)
    r8 = np.arange(8)[:, None]
    p8 = np.arange(128)[None, :]
    i8 = np.zeros((8, 256), dtype=np.float32)
    i8[:, 0:128] = (r8 == p8 // 32) & (r8 < 4)
    i8[:, 128:256] = (r8 - 4) == (p8 // 32)
    shared["ind8"] = i8

    in_maps = []
    for b in range(NCORES):
        m = dict(shared)
        m["x_tok"] = np.ascontiguousarray(x[b])
        m["x_dmaj"] = np.ascontiguousarray(x[b].T)
        in_maps.append(m)
    return in_maps


def kernel(**inputs):
    in_maps = prepare_inputs(inputs)
    nc, _ = build()
    res = run_bass_kernel_spmd(nc, in_maps, core_ids=list(range(NCORES)))
    out = np.stack([np.asarray(r["y"]).T for r in res.results], axis=0)
    return np.ascontiguousarray(out, dtype=np.float32)


if __name__ == "__main__":
    pass


# version 14 (final)
# speedup vs baseline: 1.0156x; 1.0156x over previous
import contextlib
import math
import numpy as np

import concourse.bass as bass
import concourse.mybir as mybir
from concourse.bass_utils import run_bass_kernel_spmd

F32 = mybir.dt.float32
F16 = mybir.dt.float16
I32 = mybir.dt.int32
AF = mybir.ActivationFunctionType
ALU = mybir.AluOpType
AX = mybir.AxisListType

NCORES = 8
T = 2048
D = 1024
NH = 8
HD = 128
CH = 64
NT = 16
DC = 8
IT = 24
FI = 3072
TG = 4
QMAX = 32767.0
WQ = 7.0

ENGINES = ("sync", "scalar", "vector", "tensor", "gpsimd")


class Ct:
    __slots__ = ("sem", "v")

    def __init__(self, sem):
        self.sem = sem
        self.v = 0


class Prog:

    def __init__(self):
        self.q = {e: [] for e in ENGINES}
        self.ect = {}

    def on(self, eng, fn):
        self.q[eng].append(fn)

    def wait(self, eng, ct, val=None):
        v = ct.v if val is None else val
        self.on(eng, lambda e, s=ct.sem, v=v: e.wait_ge(s, v))

    def dma(self, eng, out, in_, ct, serial=True, **kw):
        own = self.ect.get(eng)
        if own is not None:
            self.wait(eng, own, own.v)
        if serial:
            self.wait(eng, ct, ct.v)
        ct.v += 16
        self.on(eng, lambda e, o=out, i=in_, s=ct.sem, kw=kw:
                e.dma_start(out=o, in_=i, **kw).then_inc(s, 16))
        return ct.v

    def op(self, eng, fn, ct=None, inc=1):
        own = self.ect[eng]
        if ct is None or ct is own:
            ct = own
            self.wait(eng, own, own.v)
            own.v += inc
            v = own.v
            def g(e, fn=fn, s=own.sem, inc=inc):
                fn(e).then_inc(s, inc)
            self.on(eng, g)
            return v
        raise NotImplementedError


def ceil_div(a, b):
    return -(-a // b)


class Arena:

    def __init__(self, nc, base, words):
        self.nc = nc
        self.base = base
        self.W = words
        self.off = 0
        self.n = 0

    def _at(self, word_off, elems, dt, esz, part=128):
        assert (self.base + 4 * word_off) % 32 == 0, f"unaligned word {word_off}"
        end_w = word_off + (elems * esz + 3) // 4
        assert end_w <= self.W, f"arena OOM {end_w} > {self.W}"
        self.n += 1
        return self.nc.alloc_sbuf_tensor_at(
            f"arn{self.n}", [part, elems], dt, offset=self.base + 4 * word_off)

    def reserve(self, n):
        a = (self.off + 7) & ~7
        self.off = a + n
        assert self.off <= self.W, f"arena OOM {self.off} > {self.W}"
        return a

    def f32(self, n):
        return self._at(self.reserve(n), n, F32, 4)

    def f16(self, n):
        assert n % 2 == 0
        return self._at(self.reserve(n // 2), n, F16, 2)

    def at_f32(self, word_off, n, part=128):
        return self._at(word_off, n, F32, 4, part)

    def at_f16(self, word_off, n):
        return self._at(word_off, n, F16, 2)

    def at_i32(self, word_off, n):
        return self._at(word_off, n, I32, 4)


def build(debug=(), upto=99):
    nc = bass.Bass("TRN2", target_bir_lowering=False)
    dbg = set(debug)

    dram = {}
    def din(name, shape, dt=F32):
        dram[name] = nc.dram_tensor(name, list(shape), dt, kind="ExternalInput").ap()
        return dram[name]
    def dint(name, shape, dt=F32):
        dram[name] = nc.dram_tensor(name, list(shape), dt, kind="Internal").ap()
        return dram[name]
    def dout(name, shape, dt=F32):
        dram[name] = nc.dram_tensor(name, list(shape), dt, kind="ExternalOutput").ap()
        return dram[name]

    x_tok = din("x_tok", [T, D])
    x_dmaj = din("x_dmaj", [D, T])
    wwT = din("wwT", [D, D])
    wrT = din("wrT", [D, D])
    wbaT = din("wbaT", [D, 16])
    alog = din("alog", [1, NH])
    dtb = din("dtb", [1, NH])
    wuT = din("wuT", [D, FI])
    wgT = din("wgT", [D, FI])
    wdT = din("wdT", [FI, D])
    wu_nat = din("wu_nat", [FI, D])
    wg_nat = din("wg_nat", [FI, D])
    wd_nat = din("wd_nat", [D, FI])
    eyef32 = din("eyef32", [128, 128])
    eye16 = din("eye16", [128, 128], F16)
    cumtri16 = din("cumtri16", [128, 128], F16)
    blksum16 = din("blksum16", [128, 128], F16)
    kaddU16 = din("kaddU16", [128, 128], F16)
    kaddL16 = din("kaddL16", [128, 128], F16)
    ones1 = din("ones1", [1, 128])
    onescol = din("onescol", [128, 1])
    ones1k = din("ones1k", [1, 1024])
    ind8 = din("ind8", [8, 256])

    rowscr = dint("rowscr", [48, T])
    sp_gate = dint("sp_gate", [FI, T], F16)
    sp_up = dint("sp_up", [FI, T], F16)
    sp_P = dint("sp_P", [FI, T])
    sp_Q = dint("sp_Q", [FI, T])
    sp_x2 = dint("sp_x2", [D, T])
    sT_u = dint("sT_u", [32, FI])
    sTr_u = dint("sTr_u", [32, FI])
    sT_g = dint("sT_g", [32, FI])
    sTr_g = dint("sTr_g", [32, FI])
    sT_d = dint("sT_d", [96, D])
    sTr_d = dint("sTr_d", [96, D])
    cedscr = dint("cedscr", [8, 2])
    rows2 = dint("rows2", [NT, 3, 1024])
    rk42 = dint("rk42", [NT, 1024], F16)
    ced2 = dint("ced2", [NT, 16])
    cc_in = [dint(f"cc_in{i}", [1, 16]) for i in range(4)]
    cc_out = [dint(f"cc_out{i}", [1, 16]) for i in range(4)]
    y = dout("y", [D, T])

    P = Prog()
    taps = []

    class _StageDone(Exception):
        pass

    ctx = contextlib.ExitStack()
    with ctx:
        sb = lambda nm, shape, dt=F32: ctx.enter_context(nc.sbuf_tensor(nm, shape, dt))
        def sem(nm):
            return Ct(ctx.enter_context(nc.semaphore(nm)))

        ba = sb("ba", [128, NT * 16])
        wba16 = sb("wba16", [128, DC * 16], F16)
        sc_cum = sb("sc_cum", [128, 128])
        sc_ccum = sb("sc_ccum", [128, 128])
        sc_lnb = sb("sc_lnb", [128, 128])
        sc_ss = sb("sc_ss", [128, 128])
        sc_lnn2 = sb("sc_lnn2", [128, 128])
        sc_lnn2sh = sb("sc_lnn2sh", [128, 128])
        sc_p1 = sb("sc_p1", [128, 128])
        sc_p2 = sb("sc_p2", [128, 128])
        sc_tg = sb("sc_tg", [128, 128])
        sc_beta = sb("sc_beta", [128, 128])
        sc_usc = sb("sc_usc", [128, 128])
        sc_ld16 = sb("sc_ld16", [128, 128], F16)
        st_mu = sb("st_mu", [128, 16])
        st_r = sb("st_r", [128, 16])
        st_mush = sb("st_mush", [128, 16])
        st_rsh = sb("st_rsh", [128, 16])
        st_sum = sb("st_sum", [128, 16])
        st_sq = sb("st_sq", [128, 16])
        st_tmp = sb("st_tmp", [128, 16])
        rows = sb("rows", [48, T])
        alogbc = sb("alogbc", [128, NH])
        dtbbc = sb("dtbbc", [128, NH])
        s16 = sb("s16", [128, NH * 128], F16)
        arbc = sb("arbc", [128, 16])
        mxc = sb("mxc", [128, 8])
        qsc = sb("qsc", [128, 12])
        ccs = sb("ccs", [1, 16])
        c_eyef = sb("c_eyef", [128, 128])
        c_eye16 = sb("c_eye16", [128, 128], F16)
        c_cumtri = sb("c_cumtri", [128, 128], F16)
        c_blksum = sb("c_blksum", [128, 128], F16)
        c_kaddU = sb("c_kaddU", [128, 128], F16)
        c_kaddL = sb("c_kaddL", [128, 128], F16)
        c_ones1 = sb("c_ones1", [1, 128])
        c_onescol = sb("c_onescol", [128, 1])
        c_ind8 = sb("c_ind8", [8, 256])
        sck = sb("sck", [128, NT * 48])

        _la = nc._sbuf_addr_for_side("left")
        _ra = nc._sbuf_addr_for_side("right")
        _abase = (_la + 31) & ~31
        AWORDS = (_ra - _abase) // 4
        assert AWORDS >= 43800, f"arena too small: {AWORDS}"
        import os
        if os.environ.get('PRINT_AWORDS'): print('AWORDS =', AWORDS)
        ar = Arena(nc, _abase, AWORDS)

        xd = ar.f32(DC * T)
        xl1d = ar.f16(DC * (T + 1) + 8)
        wwb = ar.f16(DC * D)
        wrb = ar.f16(DC * D)
        v16 = ar.f16(D)
        o16 = ar.f16(NH * 512)
        xt16 = v16
        xsh16 = ar.f16(D)
        xshx16 = ar.f16(D)
        xcur16 = xshx16
        rk4bc = ar.f16(NH * 128)
        cedbc = ar.f32(16)
        UN0 = ar.reserve(6912)
        mrbc = ar.at_f32(UN0, 2 * T)
        xtmp = ar.at_f32(UN0 + 2 * T, T)
        fbc = ar.at_f32(UN0, 3072)
        argbuf = ar.at_f32(UN0 + 3072, 1024)
        wkU = ar.at_f16(UN0, 1024)
        wkL = ar.at_f16(UN0 + 512, 1024)
        wkQ = ar.at_f16(UN0 + 1024, 1024)
        wkU2 = ar.at_f16(UN0 + 1536, 1024)
        wkL2 = ar.at_f16(UN0 + 2048, 1024)
        wkQ2 = ar.at_f16(UN0 + 2560, 1024)
        expw = ar.at_f16(UN0 + 4096, 3072)
        wkAb = ar.at_f16(UN0 + 4096, 1024)
        wkAn = ar.at_f16(UN0 + 4608, 1024)
        mstage = ar.at_f32(UN0 + 5632, 512)
        eq16 = ar.at_f16(UN0 + 6144, 1024)
        rkds = ar.at_f16(UN0 + 6656, 512)
        sptmp = ar.at_f32(UN0 + 768, 128)
        sptmp2 = ar.at_f32(UN0 + 896, 128)
        PX0 = UN0 + 6912
        vq = ar.at_f16(PX0, 1024)
        vqu = ar.at_f16(PX0 + 512, 1024)
        wkITx = [ar.at_f16(PX0 + 1024, 1024), ar.at_f16(PX0 + 1536, 1024)]
        wkGT = ar.at_f16(PX0 + 2048, 1024)
        rk4bcx = [rk4bc, ar.at_f16(PX0 + 2560, 1024)]
        cedbcx = [cedbc, ar.at_f32(PX0 + 3072, 16)]
        rtmp = ar.at_f32(UN0 + 6144, 256, part=16)

        pb = [ctx.enter_context(nc.psum_tensor(f"pb{i}", [128, 512], F32))
              for i in range(8)]

        ALLCTS = []
        def sem2(nm):
            c = sem(nm)
            ALLCTS.append(c)
            return c
        ldS = sem2("ldS"); ldG = sem2("ldG"); ldR = sem2("ldR")
        mmC = sem2("mmC"); veC = sem2("veC"); acC = sem2("acC"); poC = sem2("poC")
        spS = sem2("spS"); ccC = sem2("ccC"); stY = sem2("stY")
        spl = [sem2(f"spl{i}") for i in range(4)]
        lxs = [sem2(f"lxs{i}") for i in range(2)]
        sty = [sem2(f"sty{i}") for i in range(2)]
        qls = [sem2(f"qls{i}") for i in range(2)]
        wls = [sem2(f"wls{i}") for i in range(2)]
        sls = [sem2(f"sls{i}") for i in range(2)]
        swl = [sem2(f"swl{i}") for i in range(2)]
        spP = [sem2(f"spP{i}") for i in range(2)]
        P.ect = {"tensor": mmC, "vector": veC, "scalar": acC, "gpsimd": poC}

        def tap(name, ap, waits):
            if name not in dbg:
                return
            t = nc.dram_tensor("tap_" + name, list(ap.shape),
                               ap.dtype, kind="ExternalOutput").ap()
            taps.append("tap_" + name)
            for (ct, v) in waits:
                P.wait("gpsimd", ct, v)
            P.dma("gpsimd", t, ap, stY)

        def _stage_gate(lvl):
            if upto <= lvl:
                raise _StageDone

        def _final_emit():
            for ct in ALLCTS:
                if ct is not stY:
                    P.wait("gpsimd", ct, ct.v)
            if stY.v == 0 and sty[0].v == 0 and sty[1].v == 0:
                P.dma("gpsimd", y[0:128, :], xd[:, 0:T], stY)
            P.wait("gpsimd", stY, stY.v)

        try:
            P.op("vector", lambda e: e.memset(rows[:], 0.0), veC)
            P.op("vector", lambda e: e.memset(sck[:], 0.0), veC)
            P.dma("sync", xd[:].rearrange("p (c t) -> p c t", c=DC), x_dmaj.rearrange("(c p) t -> p c t", p=128), ldS)
            for cst, srcd in ((c_eyef, eyef32), (c_eye16, eye16), (c_cumtri, cumtri16),
                              (c_blksum, blksum16), (c_kaddU, kaddU16), (c_kaddL, kaddL16)):
                P.dma("sync", cst[:], srcd, ldS)
            P.dma("sync", c_ones1[:], ones1, ldS)
            P.dma("sync", c_onescol[:], onescol, ldS)
            P.dma("sync", c_ind8[:], ind8, ldS)
            ld_consts = ldS.v
            P.dma("gpsimd", wwb[:].rearrange("p (c e) -> p c e", c=DC), wwT.rearrange("(c p) e -> p c e", p=128), ldG)
            P.dma("gpsimd", wrb[:].rearrange("p (c e) -> p c e", c=DC), wrT.rearrange("(c p) e -> p c e", p=128), ldG)
            P.dma("gpsimd", wba16[:].rearrange("p (c e) -> p c e", c=DC), wbaT.rearrange("(c p) e -> p c e", p=128), ldG)
            ld_ww = ldG.v
            P.dma("gpsimd", alogbc[:], alog.squeeze(0).partition_broadcast(128), ldG)
            P.dma("gpsimd", dtbbc[:], dtb.squeeze(0).partition_broadcast(128), ldG)
            ld_ab = ldG.v

            _stage_gate(0.1)
            am_ = [0, 0]
            for m in range(NT):
                s = m % 2
                buf = xt16 if s == 0 else xsh16
                P.wait("gpsimd", acC, am_[s])
                lv = P.dma("gpsimd", buf[:], x_tok[128 * m:128 * (m + 1), :], ldG)
                P.wait("scalar", ldG, lv)
                P.op("scalar", lambda e, m=m, buf=buf: e.activation(
                    xcur16[:], buf[:], AF.Copy,
                    accum_out=st_sum[:, m:m + 1]), acC)
                P.op("scalar", lambda e, m=m, buf=buf: e.activation(
                    xcur16[:], buf[:], AF.Square,
                    accum_out=st_sq[:, m:m + 1]), acC)
                am_[s] = acC.v
            P.wait("vector", acC, acC.v)
            P.op("vector", lambda e: e.tensor_scalar(st_mu[:], st_sum[:], 1.0 / D, None, ALU.mult), veC)
            P.op("vector", lambda e: e.tensor_tensor(st_tmp[:], st_mu[:], st_mu[:], ALU.mult), veC)
            P.op("vector", lambda e: e.scalar_tensor_tensor(
                st_tmp[:], st_sq[:], 1.0 / D, st_tmp[:], ALU.mult, ALU.subtract), veC)
            v_var = P.op("vector", lambda e: e.tensor_scalar(
                st_tmp[:], st_tmp[:], 1e-5, None, ALU.add), veC)
            P.wait("scalar", veC, v_var)
            a_sq = P.op("scalar", lambda e: e.activation(st_tmp[:], st_tmp[:], AF.Sqrt), acC)
            P.wait("vector", acC, a_sq)
            v_r = P.op("vector", lambda e: e.reciprocal(st_r[:], st_tmp[:]), veC)
            P.op("gpsimd", lambda e: e.memset(st_mush[0:1, :], 0.0), poC)
            P.op("gpsimd", lambda e: e.memset(st_rsh[0:1, :], 1.0), poC)
            P.wait("gpsimd", veC, v_r)
            P.dma("gpsimd", st_mush[1:128, :], st_mu[0:127, :], ldG)
            P.dma("gpsimd", st_rsh[1:128, :], st_r[0:127, :], ldG)
            P.dma("gpsimd", st_mush[0:1, 1:16], st_mu[127:128, 0:15], ldG)
            P.dma("gpsimd", st_rsh[0:1, 1:16], st_r[127:128, 0:15], ldG)
            ld_sh = ldG.v
            st_done = (veC, v_r)

            tap("st_mu", st_mu[:], [st_done])
            tap("st_r", st_r[:], [st_done])

            _stage_gate(0.2)
            P.wait("tensor", veC, v_r)
            P.wait("tensor", ldS)
            m_t1 = P.op("tensor", lambda e: e.transpose(pb[0][0:16, 0:128], st_mu[:], c_eyef[:]), mmC)
            m_t2 = P.op("tensor", lambda e: e.transpose(pb[0][0:16, 128:256], st_r[:], c_eyef[:]), mmC)
            P.wait("vector", mmC, m_t2)
            v_rw = P.op("vector", lambda e: e.tensor_copy(rtmp[0:16, 0:128], pb[0][0:16, 0:128]), veC)
            v_rw = P.op("vector", lambda e: e.tensor_copy(rtmp[0:16, 128:256], pb[0][0:16, 128:256]), veC)
            P.wait("gpsimd", veC, v_rw)
            P.dma("gpsimd", rowscr[40:41, :].rearrange("o (m t) -> (o m) t", m=16), rtmp[0:16, 0:128], ldG)
            g_rw = P.dma("gpsimd", rowscr[41:42, :].rearrange("o (m t) -> (o m) t", m=16), rtmp[0:16, 128:256], ldG)
            P.wait("sync", ldG)
            s_bc1 = P.dma("sync", mrbc[:, 0:T], rowscr[40:41, :].squeeze(0).partition_broadcast(128), ldS)
            s_bc2 = P.dma("sync", mrbc[:, T:2 * T], rowscr[41:42, :].squeeze(0).partition_broadcast(128), ldS)
            P.wait("vector", ldS)
            for k in range(DC):
                P.op("vector", lambda e, k=k: e.tensor_tensor(
                    xtmp[:], xd[:, T * k:T * (k + 1)], mrbc[:, 0:T], ALU.subtract), veC)
                P.op("vector", lambda e, k=k: e.tensor_tensor(
                    xl1d[:, (T + 1) * k + 1:(T + 1) * (k + 1)], xtmp[:], mrbc[:, T:2 * T],
                    ALU.mult), veC)
            P.op("vector", lambda e: e.memset(
                xl1d[:, 0:DC * (T + 1)].rearrange("p (c t) -> p c t", c=DC)[:, :, 0:1], 0.0), veC)
            v_xl1 = veC.v
            tap("xl1d", xl1d[:, 0:T + 1], [(veC, v_xl1)])

            _stage_gate(0.3)
            P.wait("tensor", ldG)
            P.wait("tensor", veC, v_xl1)
            ba_ev = []
            for m in range(NT):
                if m >= 2:
                    P.wait("tensor", veC, ba_ev[m - 2])
                for k in range(DC):
                    P.op("tensor", lambda e, m=m, k=k: e.matmul(
                        pb[m % 2][0:128, 0:16],
                        xl1d[:, (T + 1) * k + 1 + 128 * m:(T + 1) * k + 1 + 128 * (m + 1)],
                        wba16[:, 16 * k:16 * (k + 1)],
                        start=(k == 0), stop=(k == DC - 1)),
                        mmC if k == DC - 1 else None)
                P.wait("vector", mmC, mmC.v)
                P.op("vector", lambda e, m=m: e.tensor_copy(
                    ba[:, 16 * m:16 * (m + 1)], pb[m % 2][0:128, 0:16]), veC)
                ba_ev.append(veC.v)
                if m == 0:
                    _stage_gate(0.32)
                if m == 1:
                    _stage_gate(0.33)
                if m == 2:
                    _stage_gate(0.34)
                if m == 3:
                    _stage_gate(0.36)
            v_ba = veC.v
            tap("ba", ba[:], [(veC, v_ba)])

            _stage_gate(0.4)
            bav = ba[:].rearrange("p (m k) -> p m k", k=16)
            braw = bav[:, :, 0:8]
            araw = bav[:, :, 8:16]
            d3 = lambda t: t.rearrange("p (m h) -> p m h", h=8)
            P.wait("scalar", veC, v_ba)
            a_beta = P.op("scalar", lambda e: e.activation(d3(sc_beta[:]), braw, AF.Sigmoid), acC)
            P.op("vector", lambda e: e.tensor_scalar(d3(sptmp[:]), braw, -1.0, None, ALU.mult), veC)
            v_ab = P.op("vector", lambda e: e.tensor_tensor(d3(sptmp[:]), braw, d3(sptmp[:]), ALU.min), veC)
            P.wait("scalar", veC, v_ab)
            a_e = P.op("scalar", lambda e: e.activation(sptmp[:], sptmp[:], AF.Exp), acC)
            a_l = P.op("scalar", lambda e: e.activation(sptmp[:], sptmp[:], AF.Ln, bias=1.0), acC)
            P.wait("vector", acC, a_l)
            P.op("vector", lambda e: e.tensor_scalar(d3(sc_lnb[:]), braw, 0.0, None, ALU.min), veC)
            v_lnb = P.op("vector", lambda e: e.tensor_tensor(
                sc_lnb[:], sc_lnb[:], sptmp[:], ALU.subtract), veC)
            P.wait("vector", acC, a_beta)
            dview = dtbbc[:].unsqueeze(1).broadcast_to([128, NT, NH])
            P.op("vector", lambda e: e.tensor_tensor(d3(sptmp2[:]), araw, dview, ALU.add), veC)
            P.op("vector", lambda e: e.tensor_scalar(sptmp[:], sptmp2[:], -1.0, None, ALU.mult), veC)
            v_z = P.op("vector", lambda e: e.tensor_tensor(sptmp[:], sptmp2[:], sptmp[:], ALU.min), veC)
            P.wait("scalar", veC, v_z)
            a_e2 = P.op("scalar", lambda e: e.activation(sptmp[:], sptmp[:], AF.Exp), acC)
            a_l2 = P.op("scalar", lambda e: e.activation(sptmp[:], sptmp[:], AF.Ln, bias=1.0), acC)
            P.wait("scalar", ldG)
            a_ax = P.op("scalar", lambda e: e.activation(alogbc[:], alogbc[:], AF.Exp), acC)
            P.wait("vector", acC, a_l2)
            P.op("vector", lambda e: e.tensor_scalar(sptmp2[:], sptmp2[:], 0.0, None, ALU.max), veC)
            P.op("vector", lambda e: e.tensor_tensor(sptmp2[:], sptmp2[:], sptmp[:], ALU.add), veC)
            P.wait("vector", acC)
            aview = alogbc[:].unsqueeze(1).broadcast_to([128, NT, NH])
            v_ld = P.op("vector", lambda e: e.scalar_tensor_tensor(
                d3(sc_ld16[:]), d3(sptmp2[:]), -1.0, aview, ALU.mult, ALU.mult), veC)
            tap("ld16", sc_ld16[:], [(veC, v_ld)])

            P.wait("tensor", veC, v_ld)
            P.wait("tensor", ldG)
            for m in range(NT):
                P.op("tensor", lambda e, m=m: e.matmul(
                    pb[2][0:128, 8 * m:8 * m + 8], c_cumtri[:],
                    sc_ld16[:, 8 * m:8 * (m + 1)], start=True, stop=True), None)
                P.op("tensor", lambda e, m=m: e.matmul(
                    pb[3][0:128, 8 * m:8 * m + 8], c_blksum[:],
                    sc_ld16[:, 8 * m:8 * (m + 1)], start=True, stop=True), None)
            m_cum = P.op("tensor", lambda e: e.nop(), mmC)
            P.wait("vector", mmC, m_cum)
            P.op("vector", lambda e: e.tensor_copy(sc_cum[:], pb[2][0:128, 0:128]), veC)
            v_cum = P.op("vector", lambda e: e.tensor_copy(sc_ccum[:], pb[3][0:128, 0:128]), veC)
            tap("cum", sc_cum[:], [(veC, v_cum)])

            am_ = [0, 0]
            tsm_ = [0, 0]
            for m in range(NT):
                s = m % 2
                buf = xt16 if s == 0 else xsh16
                cur = xcur16 if s == 0 else o16
                P.wait("gpsimd", veC, tsm_[s])
                lv = P.dma("gpsimd", buf[:], x_tok[128 * m:128 * (m + 1), :], ldG)
                P.wait("vector", ldG, lv)
                P.wait("vector", veC, v_cum)
                P.wait("vector", acC, am_[s])
                vv = P.op("vector", lambda e, m=m, buf=buf, cur=cur: e.tensor_scalar(
                    cur[:, 0:1024], buf[:], st_mu[:, m:m + 1], st_r[:, m:m + 1],
                    ALU.subtract, ALU.mult), veC)
                tsm_[s] = vv
                P.wait("scalar", veC, vv)
                for h in range(NH):
                    P.op("scalar", lambda e, m=m, h=h, cur=cur: e.activation(
                        sptmp2[:, 0:128], cur[:, 128 * h:128 * (h + 1)],
                        AF.Square, accum_out=sc_ss[:, 8 * m + h:8 * m + h + 1]), acC)
                am_[s] = acC.v
            a_ss = acC.v
            P.wait("vector", acC, a_ss)
            v_c = P.op("vector", lambda e: e.tensor_scalar(
                sc_lnn2[:], sc_ss[:], 1e-19, None, ALU.max), veC)
            P.wait("scalar", veC, v_c)
            a_ln = P.op("scalar", lambda e: e.activation(sc_lnn2[:], sc_lnn2[:], AF.Ln), acC)
            P.wait("gpsimd", acC, a_ln)
            P.op("gpsimd", lambda e: e.memset(sc_lnn2sh[0:1, 0:8], 0.0), poC)
            P.dma("gpsimd", sc_lnn2sh[1:128, :], sc_lnn2[0:127, :], ldG)
            g_sh = P.dma("gpsimd", sc_lnn2sh[0:1, 8:128], sc_lnn2[127:128, 0:120], ldG)

            P.wait("vector", ldG)
            P.wait("vector", veC, v_lnb)
            P.op("vector", lambda e: e.tensor_tensor(sc_p1[:], sc_cum[:], sc_lnb[:], ALU.add), veC)
            P.op("vector", lambda e: e.scalar_tensor_tensor(
                sc_p1[:], sc_lnn2sh[:], -0.5, sc_p1[:], ALU.mult, ALU.add), veC)
            P.op("vector", lambda e: e.scalar_tensor_tensor(
                sc_p2[:], sc_lnn2sh[:], 0.5, sc_cum[:], ALU.mult, ALU.add), veC)
            P.op("vector", lambda e: e.tensor_scalar(sc_p2[:], sc_p2[:], -1.0, None, ALU.mult), veC)
            P.op("vector", lambda e: e.scalar_tensor_tensor(
                sptmp[:], sc_lnn2[:], -0.5, sc_cum[:], ALU.mult, ALU.add), veC)
            P.op("vector", lambda e: e.tensor_tensor(sptmp2[:], sc_ccum[:], sc_cum[:], ALU.subtract), veC)
            P.op("vector", lambda e: e.scalar_tensor_tensor(
                sc_usc[:], sc_lnn2sh[:], -0.5, sptmp2[:], ALU.mult, ALU.add), veC)
            v_combo = veC.v
            P.wait("scalar", veC, v_combo)
            a_tg = P.op("scalar", lambda e: e.activation(sc_tg[:], sc_p1[:], AF.Exp, scale=1.0), acC)
            a_us = P.op("scalar", lambda e: e.activation(sc_usc[:], sc_usc[:], AF.Exp), acC)
            P.wait("vector", acC, a_tg)
            v_tg = P.op("vector", lambda e: e.tensor_scalar(sc_tg[:], sc_tg[:], -1.0, None, ALU.mult), veC)

            sckv = lambda k: sck[:].rearrange("p (m k h) -> p m k h", m=NT, k=6, h=8)[:, :, k:k + 1, :]
            scv = lambda t: t.rearrange("p (m h) -> p m h", m=NT).unsqueeze(2)
            P.op("vector", lambda e: e.tensor_copy(sckv(0), scv(sc_p1[:])), veC)
            P.op("vector", lambda e: e.tensor_copy(sckv(1), scv(sc_p2[:])), veC)
            P.op("vector", lambda e: e.tensor_copy(sckv(2), scv(sptmp[:])), veC)
            v_sck1 = veC.v
            P.wait("scalar", veC, v_sck1)
            a_rk4 = P.op("scalar", lambda e: e.activation(sckv(3), scv(sptmp[:]), AF.Exp), acC)
            a_ced = P.op("scalar", lambda e: e.activation(sckv(4), scv(sc_ccum[:]), AF.Exp), acC)
            P.wait("tensor", acC, a_ced)
            P.wait("tensor", veC, v_sck1)
            for m in range(NT):
                P.op("tensor", lambda e, m=m: e.transpose(
                    pb[2][0:40, 128 * (m % 4):128 * (m % 4) + 128],
                    sck[:, 48 * m:48 * m + 40], c_eyef[:]), mmC)
                if m % 4 == 3:
                    mv = mmC.v
                    P.wait("vector", mmC, mv)
                    for mm2 in range(m - 3, m + 1):
                        P.op("vector", lambda e, mm2=mm2: e.tensor_copy(
                            rows[0:40, 128 * mm2:128 * (mm2 + 1)],
                            pb[2][0:40, 128 * (mm2 % 4):128 * (mm2 % 4) + 128]), veC)
                    P.wait("tensor", veC, veC.v)
            v_rows = veC.v
            P.wait("gpsimd", veC, v_rows)
            g_rows = P.dma("gpsimd", rowscr[0:40, :], rows[0:40, :], ldG)
            P.dma("gpsimd", rows2.rearrange("m k (h c) -> m k h c", h=8),
                  rowscr[0:24, :].rearrange("(k h) (m c) -> m k h c", h=8, c=128), ldG)
            P.dma("gpsimd", rk42.rearrange("m (h c) -> m h c", h=8),
                  rowscr[24:32, :].rearrange("h (m c) -> m h c", c=128), ldG)
            for cc2 in range(2):
                g_rows2 = P.dma(
                    "gpsimd",
                    ced2.rearrange("m (h c) -> m h c", h=8)[:, :, cc2:cc2 + 1],
                    rowscr[32:40, 63 + 64 * cc2::128].rearrange("h m -> m h").unsqueeze(2),
                    ldG, allow_slow_non_contiguous=True)
            tap("rows", rows[:, 0:256], [(veC, v_rows)])


            _stage_gate(0.8)
            P.wait("tensor", ldS)
            P.op("vector", lambda e: e.memset(s16[:], 0.0), veC)
            P.wait("vector", acC)

            TP1 = T + 1
            hview = lambda t, w=128: t.rearrange("p (h t2) -> p h t2", t2=w)
            xlv = xl1d[:, 0:DC * TP1].rearrange("p (h t2) -> p h t2", t2=TP1)

            def scvw(t, c0, n, rows_=128, p0=0, w=128):
                return t[p0:p0 + rows_, c0:c0 + n].unsqueeze(2).broadcast_to([rows_, n, w])

            kview = lambda cst: cst[:].unsqueeze(1).broadcast_to([128, 4, 128])
            g4 = lambda t, base: t[:, base:base + 512].rearrange("p (a b) -> p a b", a=4)
            pbv = lambda i: pb[i][0:128, 0:512].rearrange("p (a b) -> p a b", a=4)
            exv = lambda oi, hg: expw[:, 1024 * oi + 512 * hg:1024 * oi + 512 * hg + 512].rearrange(
                "p (a b) -> p a b", a=4)

            x2_done = []
            fbm = {}
            abm = {}
            dblm = {}
            vgtm = {}
            veqm = {}
            ssttm = {}
            supdm = {}
            intram = {}
            oevm = {}
            vqum = {}
            rkdu = {}
            st8 = {}
            for mi in range(NT + 1):
                mp = mi
                msc = mi - 1
                if mp < NT:
                    co = 8 * mp
                    ml_p = mp % 2
                    sl = slice(128 * mp, 128 * mp + 128)
                    P.wait("sync", veC, abm.get(mp - 1, 0))
                    P.wait("sync", mmC, dblm.get(mp - 1, 0))
                    P.wait("sync", veC, vqum.get(mp - 2, 0))
                    P.wait("sync", veC, ssttm.get(mp - 2, 0))
                    P.wait("sync", ldG, g_rows2)
                    P.dma("sync", rk4bcx[ml_p][:],
                          rk42[mp:mp + 1, :].squeeze(0).partition_broadcast(128),
                          ldR, serial=False)
                    P.dma("sync", cedbcx[ml_p][:],
                          ced2[mp:mp + 1, :].squeeze(0).partition_broadcast(128),
                          ldR, serial=False)
                    P.dma("sync", fbc[:, 0:3072],
                          rows2[mp].rearrange("k f -> (k f)").partition_broadcast(128),
                          ldR, serial=False)
                    fbm[mp] = ldR.v

                    P.wait("tensor", veC)
                    for half in (0, 1):
                        for k in range(DC):
                            P.op("tensor", lambda e, mp=mp, k=k, half=half: e.matmul(
                                pb[half][:, 0:512],
                                xl1d[:, TP1 * k + 1 + 128 * mp:TP1 * k + 1 + 128 * (mp + 1)],
                                wwb[:, 1024 * k + 512 * half:1024 * k + 512 * half + 512],
                                start=(k == 0), stop=(k == DC - 1)),
                                mmC if k == DC - 1 else None)
                    P.wait("vector", mmC)
                    for half in (0, 1):
                        P.op("vector", lambda e, half=half: e.tensor_copy(
                            v16[:, 512 * half:512 * half + 512], pb[half][:, 0:512]), veC)

                    PKI = (1, 0, 1)
                    P.wait("vector", ldR, fbm[mp])
                    P.wait("scalar", mmC, dblm.get(mp - 1, 0))
                    a_exp = acC.v
                    ab3 = lambda t: t.rearrange("p (h t2) -> p h t2", t2=128)
                    for oi in range(3):
                        kc = c_kaddL if oi == 1 else c_kaddU
                        if oi > 0:
                            P.wait("vector", acC, a_exp)
                        P.op("vector", lambda e, oi=oi, mp=mp, ab3=ab3: e.tensor_tensor(
                            ab3(argbuf[:]),
                            ab3(fbc[:, 1024 * oi:1024 * oi + 1024]),
                            sck[:, 48 * mp + 8 * PKI[oi]:48 * mp + 8 * PKI[oi] + 8]
                            .unsqueeze(2).broadcast_to([128, 8, 128]), ALU.add), veC)
                        vv = P.op("vector", lambda e, kc=kc, ab3=ab3: e.tensor_tensor(
                            ab3(argbuf[:]), ab3(argbuf[:]),
                            kc[:].unsqueeze(1).broadcast_to([128, 8, 128]), ALU.add), veC)
                        P.wait("scalar", veC, vv)
                        a_exp = P.op("scalar", lambda e, oi=oi: e.activation(
                            expw[:, 1024 * oi:1024 * oi + 1024], argbuf[:], AF.Exp), acC)

                    P.wait("tensor", veC)
                    for hh in range(4):
                        for hg in (0, 1):
                            h = 4 * hg + hh
                            hb = TP1 * h
                            wk_sl = xl1d[:, hb + 128 * mp:hb + 128 * mp + 128]
                            rk_sl = xl1d[:, hb + 128 * mp + 1:hb + 128 * mp + 129]
                            bU = 2 - 2 * hg
                            P.op("tensor", lambda e, wk_sl=wk_sl, hh=hh, bU=bU: e.matmul(
                                pb[bU][0:128, 128 * hh:128 * hh + 128], wk_sl, wk_sl,
                                start=True, stop=True), mmC)
                            P.op("tensor", lambda e, wk_sl=wk_sl, rk_sl=rk_sl, hh=hh, bU=bU: e.matmul(
                                pb[bU + 1][0:128, 128 * hh:128 * hh + 128], wk_sl, rk_sl,
                                start=True, stop=True), mmC)
                    gm = mmC.v
                    P.wait("vector", mmC, gm)
                    P.wait("vector", acC, a_exp)
                    P.op("vector", lambda e: e.scalar_tensor_tensor(
                        g4(wkU, 0), pbv(2), -1.0, exv(0, 0), ALU.mult, ALU.mult), veC)
                    P.op("vector", lambda e: e.scalar_tensor_tensor(
                        g4(wkL, 0), pbv(2), -1.0, exv(1, 0), ALU.mult, ALU.mult), veC)
                    P.op("vector", lambda e: e.scalar_tensor_tensor(
                        g4(wkU2, 0), pbv(0), -1.0, exv(0, 1), ALU.mult, ALU.mult), veC)
                    P.op("vector", lambda e: e.scalar_tensor_tensor(
                        g4(wkL2, 0), pbv(0), -1.0, exv(1, 1), ALU.mult, ALU.mult), veC)
                    P.op("vector", lambda e: e.tensor_tensor(
                        g4(wkQ, 0), g4(wkU, 0), kview(c_eye16), ALU.add), veC)
                    P.op("vector", lambda e: e.tensor_tensor(
                        g4(wkQ2, 0), g4(wkU2, 0), kview(c_eye16), ALU.add), veC)
                    P.op("vector", lambda e, ml_p=ml_p: e.scalar_tensor_tensor(
                        g4(wkITx[ml_p], 0), pbv(3), 1.0, exv(2, 0), ALU.mult, ALU.mult), veC)
                    qs = [veC.v, veC.v]
                    P.op("vector", lambda e, ml_p=ml_p: e.scalar_tensor_tensor(
                        g4(wkITx[ml_p], 512), pbv(1), 1.0, exv(2, 1), ALU.mult, ALU.mult), veC)
                    qs[1] = veC.v
                    st8[mp] = {"qs": qs, "lvs": [0, 0], "uvs": [qs[1], qs[1]]}

                if msc >= 0:
                    cox = 8 * msc
                    ms2 = msc % 2
                    _scan_chunk = None

                def scan_chunk(msc, c, prev_sstt=0, prev_mgts=0):
                    cs = 64 * c
                    cox = 8 * msc
                    ms2 = msc % 2
                    if c == 0:
                        P.wait("vector", ldR, fbm[msc])
                        P.wait("vector", mmC, rkdu.get(msc - 1, 0))
                    else:
                        P.wait("vector", mmC, prev_mgts)
                    P.op("vector", lambda e, msc=msc, cs=cs, ms2=ms2: e.tensor_tensor(
                        rkds[:].rearrange("p (h t2) -> p h t2", t2=64),
                        xlv[:, :, 128 * msc + 1 + cs:128 * msc + 1 + cs + 64],
                        rk4bcx[ms2][:].rearrange("p (h t2) -> p h t2", t2=128)[:, :, cs:cs + 64],
                        ALU.mult), veC)
                    v_rk = veC.v
                    if c == 0:
                        P.wait("tensor", veC, ssttm.get(msc - 1, 0))
                        P.wait("tensor", veC, vgtm[msc])
                        P.wait("tensor", veC, oevm.get(msc - 1, 0))
                        P.wait("tensor", veC, vqum.get(msc - 1, 0))
                    else:
                        P.wait("tensor", veC, prev_sstt)
                    for h in range(NH):
                        hh = h % 4
                        P.op("tensor", lambda e, h=h, hh=hh, cs=cs: e.matmul(
                            pb[4 + h // 4][cs:cs + 64, 128 * hh:128 * hh + 128],
                            wkGT[:, 128 * h + cs:128 * h + cs + 64],
                            s16[:, 128 * h:128 * h + 128],
                            start=True, stop=True), mmC)
                    P.wait("tensor", veC, v_rk)
                    for h in range(NH):
                        hh = h % 4
                        P.op("tensor", lambda e, h=h, hh=hh, cs=cs: e.matmul(
                            pb[6 + h // 4][0:128, 128 * hh + cs:128 * hh + cs + 64],
                            s16[:, 128 * h:128 * h + 128],
                            rkds[:, 64 * h:64 * h + 64],
                            start=(cs == 0 and hh == 0), stop=False,
                            skip_group_check=True), mmC)
                    mgts = mmC.v
                    if c == 1:
                        rkdu[msc] = mgts
                    P.wait("vector", mmC, mgts)
                    for b in (0, 1):
                        P.op("vector", lambda e, b=b, cs=cs: e.tensor_tensor(
                            vq[cs:cs + 64, 512 * b:512 * b + 512],
                            eq16[cs:cs + 64, 512 * b:512 * b + 512],
                            pb[4 + b][cs:cs + 64, 0:512], ALU.add), veC)
                        P.op("vector", lambda e, b=b, cs=cs, cox=cox: e.tensor_tensor(
                            vqu[cs:cs + 64, 512 * b:512 * b + 512].rearrange(
                                "p (a b2) -> p a b2", a=4),
                            vq[cs:cs + 64, 512 * b:512 * b + 512].rearrange(
                                "p (a b2) -> p a b2", a=4),
                            scvw(sc_usc, cox + 4 * b, 4, rows_=64, p0=cs), ALU.mult), veC)
                    v_vq = veC.v
                    if c == 1:
                        vqum[msc] = v_vq
                    P.wait("tensor", veC, v_vq)
                    for h in range(NH):
                        hh = h % 4
                        P.op("tensor", lambda e, h=h, hh=hh, cs=cs, ms2=ms2: e.matmul(
                            pb[4 + h // 4][0:128, 128 * hh:128 * hh + 128],
                            xsh16[cs:cs + 64, 128 * h:128 * h + 128],
                            vqu[cs:cs + 64, 512 * (h // 4) + 128 * hh:512 * (h // 4) + 128 * hh + 128],
                            start=True, stop=True), mmC)
                    msup = mmC.v
                    if c == 1:
                        supdm[msc] = msup
                    P.wait("vector", mmC, msup)
                    for h in range(NH):
                        hh = h % 4
                        P.op("vector", lambda e, h=h, hh=hh, c=c, ms2=ms2: e.scalar_tensor_tensor(
                            s16[:, 128 * h:128 * h + 128], s16[:, 128 * h:128 * h + 128],
                            cedbcx[ms2][:, 2 * h + c:2 * h + c + 1],
                            pb[4 + h // 4][0:128, 128 * hh:128 * hh + 128],
                            ALU.mult, ALU.add), veC)
                    if c == 1:
                        ssttm[msc] = veC.v
                    return veC.v, mgts

                sc0 = {}
                if msc >= 0:
                    sc0[msc] = scan_chunk(msc, 0)

                if mp < NT:
                    st = st8[mp]
                    HGS = ((wkU, wkL, wkQ, 2, 3), (wkU2, wkL2, wkQ2, 0, 1))

                def dbl_levels(mp, l0, l1):
                    st = st8[mp]
                    qs, lvs, uvs = st["qs"], st["lvs"], st["uvs"]
                    HGS = ((wkU, wkL, wkQ, 2, 3), (wkU2, wkL2, wkQ2, 0, 1))
                    for lvl in range(l0, l1):
                        mul = [0, 0]
                        for hg in (0, 1):
                            U, L, Q, bU, bL = HGS[hg]
                            P.wait("tensor", veC, uvs[hg])
                            P.wait("tensor", veC, qs[hg])
                            P.wait("tensor", acC, lvs[hg])
                            if lvl < 5:
                                for hh in range(4):
                                    P.op("tensor", lambda e, hh=hh, U=U, L=L, bU=bU: e.matmul(
                                        pb[bU][0:128, 128 * hh:128 * hh + 128],
                                        L[:, 128 * hh:128 * hh + 128],
                                        U[:, 128 * hh:128 * hh + 128],
                                        start=True, stop=True), mmC)
                            for hh in range(4):
                                P.op("tensor", lambda e, hh=hh, U=U, L=L, bL=bL: e.matmul(
                                    pb[bL][0:128, 128 * hh:128 * hh + 128],
                                    U[:, 128 * hh:128 * hh + 128],
                                    L[:, 128 * hh:128 * hh + 128],
                                    start=True, stop=True), mmC)
                            mul[hg] = mmC.v
                        for hg in (0, 1):
                            U, L, Q, bU, bL = HGS[hg]
                            if lvl < 5:
                                P.wait("vector", mmC, mul[hg])
                                P.op("vector", lambda e, U=U, bU=bU: e.tensor_copy(
                                    g4(U, 0), pbv(bU)), veC)
                                uvs[hg] = veC.v
                            P.wait("scalar", mmC, mul[hg])
                            P.op("scalar", lambda e, L=L, bL=bL: e.activation(
                                g4(L, 0), pbv(bL), AF.Copy), acC)
                            lvs[hg] = acC.v
                        for hg in (0, 1):
                            U, L, Q, bU, bL = HGS[hg]
                            P.wait("tensor", acC, lvs[hg])
                            P.wait("tensor", veC, uvs[hg])
                            P.wait("tensor", veC, qs[hg])
                            for hh in range(4):
                                P.op("tensor", lambda e, hh=hh, L=L, Q=Q, bU=bU: e.matmul(
                                    pb[bU][0:128, 128 * hh:128 * hh + 128],
                                    L[:, 128 * hh:128 * hh + 128],
                                    Q[:, 128 * hh:128 * hh + 128],
                                    start=True, stop=True), mmC)
                            mq = mmC.v
                            P.wait("vector", mmC, mq)
                            P.op("vector", lambda e, Q=Q, bU=bU: e.scalar_tensor_tensor(
                                g4(Q, 0), pbv(bU), 1.0, g4(Q, 0), ALU.mult, ALU.add), veC)
                            qs[hg] = veC.v

                if mp < NT:
                    dbl_levels(mp, 1, 4)
                if msc >= 0:
                    scan_chunk(msc, 1, prev_sstt=sc0[msc][0], prev_mgts=sc0[msc][1])
                if mp < NT:
                    dbl_levels(mp, 4, 6)
                    st = st8.pop(mp)
                    qs = st["qs"]
                    co = 8 * mp
                    ml_p = mp % 2
                    P.wait("gpsimd", mmC, supdm.get(mp - 1, 0))
                    if mp == 0:
                        P.wait("vector", ldG)
                        P.op("vector", lambda e: e.memset(xshx16[0:1, :], 0.0), veC)
                        P.wait("gpsimd", veC)
                        ldX_t = P.dma("gpsimd", xshx16[1:128, :], x_tok[0:127, :], ldG)
                    else:
                        ldX_t = P.dma("gpsimd", xshx16[:],
                                      x_tok[128 * mp - 1:128 * mp + 127, :], ldG)
                    P.wait("vector", ldG, ldX_t)
                    P.wait("vector", mmC, supdm.get(mp - 1, 0))
                    P.op("vector", lambda e, mp=mp: e.tensor_scalar(
                        xsh16[:], xshx16[:], st_mush[:, mp:mp + 1], st_rsh[:, mp:mp + 1],
                        ALU.subtract, ALU.mult), veC)
                    v_xsh = veC.v
                    m_eg = 0
                    for hg in (0, 1):
                        hs = 4 * hg
                        wkQx = (wkQ, wkQ2)[hg]
                        bE = 2 - 2 * hg
                        if hg == 1:
                            P.wait("vector", mmC, m_eg)
                        P.op("vector", lambda e, hs=hs, co=co, wkQx=wkQx: e.tensor_tensor(
                            g4(wkAb, 0), g4(wkQx, 0), scvw(sc_beta, co + hs, 4), ALU.mult), veC)
                        P.op("vector", lambda e, hs=hs, co=co, wkQx=wkQx: e.tensor_tensor(
                            g4(wkAn, 0), g4(wkQx, 0), scvw(sc_tg, co + hs, 4), ALU.mult), veC)
                        vab = veC.v
                        P.wait("tensor", veC, max(vab, v_xsh))
                        for hh in range(4):
                            h = hs + hh
                            P.op("tensor", lambda e, h=h, hh=hh, bE=bE: e.matmul(
                                pb[bE][0:128, 128 * hh:128 * hh + 128],
                                wkAb[:, 128 * hh:128 * hh + 128],
                                v16[:, 128 * h:128 * h + 128],
                                start=True, stop=True), mmC)
                            P.op("tensor", lambda e, h=h, hh=hh, bE=bE: e.matmul(
                                pb[bE + 1][0:128, 128 * hh:128 * hh + 128],
                                xsh16[:, 128 * h:128 * h + 128],
                                wkAn[:, 128 * hh:128 * hh + 128],
                                start=True, stop=True), mmC)
                        m_eg = mmC.v
                        P.wait("vector", mmC, m_eg)
                        P.op("vector", lambda e, hg=hg, bE=bE: e.tensor_copy(
                            eq16[:, 512 * hg:512 * hg + 512].rearrange(
                                "p (a b) -> p a b", a=4), pbv(bE)), veC)
                        veqm[mp] = veC.v
                        P.op("vector", lambda e, hg=hg, bE=bE: e.tensor_copy(
                            wkGT[:, 512 * hg:512 * hg + 512].rearrange(
                                "p (a b) -> p a b", a=4), pbv(bE + 1)), veC)
                        vgtm[mp] = veC.v
                    abm[mp] = veC.v
                    dblm[mp] = mmC.v

                if msc >= 0:
                    cox = 8 * msc
                    ms2 = msc % 2
                    g = msc // 4
                    ml = msc % 4
                    P.wait("tensor", veC, vqum[msc])
                    for h in range(NH):
                        hh = h % 4
                        P.op("tensor", lambda e, h=h, hh=hh, ms2=ms2: e.matmul(
                            pb[6 + h // 4][0:128, 128 * hh:128 * hh + 128],
                            vq[:, 512 * (h // 4) + 128 * hh:512 * (h // 4) + 128 * hh + 128],
                            wkITx[ms2][:, 128 * h:128 * h + 128],
                            start=False, stop=True, skip_group_check=True), mmC)
                    intram[msc] = mmC.v
                    P.wait("vector", mmC, intram[msc])
                    o16v = o16[:].rearrange("p (h t2) -> p h t2", t2=512)
                    for b in (0, 1):
                        P.op("vector", lambda e, b=b, ml=ml: e.tensor_copy(
                            o16v[:, 4 * b:4 * b + 4, 128 * ml:128 * ml + 128],
                            pb[6 + b][0:128, 0:512].rearrange("p (a b2) -> p a b2", a=4)), veC)
                    oevm[msc] = veC.v
                    if ml == 3:
                        P.wait("tensor", veC)
                        for ec in range(DC):
                            pj = pb[ec % 2]
                            for k in range(DC):
                                P.op("tensor", lambda e, ec=ec, k=k, pj=pj: e.matmul(
                                    pj[:, 0:512],
                                    wrb[:, 1024 * k + 128 * ec:1024 * k + 128 * ec + 128],
                                    o16[:, 512 * k:512 * k + 512],
                                    start=(k == 0), stop=(k == DC - 1)),
                                    mmC if k == DC - 1 else None)
                            P.wait("vector", mmC)
                            P.op("vector", lambda e, ec=ec, g=g, pj=pj: e.tensor_tensor(
                                mstage[:, 0:512], pj[:, 0:512],
                                xd[:, 2048 * ec + 512 * g:2048 * ec + 512 * g + 512], ALU.add), veC)
                            P.op("vector", lambda e, ec=ec, g=g: e.tensor_tensor(
                                xd[:, 2048 * ec + 512 * g:2048 * ec + 512 * g + 512],
                                mstage[:, 0:512],
                                xl1d[:, TP1 * ec + 1 + 512 * g:TP1 * ec + 1 + 512 * g + 512],
                                ALU.add), veC)
                            if ec >= 1:
                                P.wait("tensor", veC)
                        x2_done.append(veC.v)
                        P.wait("gpsimd", veC, x2_done[-1])
                        for k in range(DC):
                            P.dma("gpsimd",
                                  sp_x2[128 * k:128 * k + 128, 512 * g:512 * g + 512],
                                  xd[:, 2048 * k + 512 * g:2048 * k + 512 * g + 512],
                                  spS, serial=False)

            tap("s16", s16[:], [(veC, veC.v)])
            tap("o16", o16[:], [(veC, veC.v)])
            tap("x2", xd[:, 0:T], [(veC, veC.v)])
            _stage_gate(1)

            _stage_gate(2)
            xl2q16 = ar.at_f16(16384, 16400)
            wgq16 = ar.at_f16(0, 24576)
            wuq16 = ar.at_f16(24584, 24576)
            deq_wld = ar.at_f32(12288, 512)
            deq_wld2 = ar.at_f32(12800, 512)
            deq_rint = ar.at_i32(13312, 512)
            sstgA = ar.at_f32(13824, 512, part=8)
            sstgB = ar.at_f32(14336, 512, part=8)
            ev16 = ar.at_f16(14848, 2048)
            sTs = ar.at_f32(12288, 3072)
            sTr = ar.at_f32(15360, 512)
            r2mu = ar.at_f32(UN0, 2048, part=1)
            r2sq = ar.at_f32(UN0 + 2048, 2048, part=1)
            r2tmp = ar.at_f32(UN0 + 4096, 2048, part=1)
            h16t = ar.at_f16(0, 2048)
            rgt = ar.at_i32(1024, 2048)
            tft = ar.at_f32(3072, 2048)
            ptt = ar.at_f32(5120, 2048)
            wdq16 = ar.at_f16(24584, 24576)
            of16 = ar.at_f16(0, 12288)
            qdt = ar.at_f32(6144, 6144)
            x2g = ar.at_f32(12288, 512)
            yt = ar.at_f32(12800, 512)

            P.wait("tensor", veC)
            P.wait("scalar", veC)
            for t in range(TG):
                for k in range(DC):
                    xsl = xd[:, 2048 * k + 512 * t:2048 * k + 512 * t + 512]
                    av = P.op("scalar", lambda e, xsl=xsl: e.activation(
                        mstage[:], xsl, AF.Square), acC)
                    P.op("tensor", lambda e, xsl=xsl, k=k: e.matmul(
                        pb[0][0:1, 0:512], c_onescol[:], xsl,
                        start=(k == 0), stop=(k == DC - 1)), mmC)
                    P.wait("tensor", acC, av)
                    mv = P.op("tensor", lambda e, k=k: e.matmul(
                        pb[1][0:1, 0:512], c_onescol[:], mstage[:],
                        start=(k == 0), stop=(k == DC - 1)), mmC)
                    P.wait("scalar", mmC, mv)
                P.wait("vector", mmC)
                P.op("vector", lambda e, t=t: e.tensor_copy(
                    r2mu[:, 512 * t:512 * t + 512], pb[0][0:1, 0:512]), veC)
                P.op("vector", lambda e, t=t: e.tensor_copy(
                    r2sq[:, 512 * t:512 * t + 512], pb[1][0:1, 0:512]), veC)
                P.wait("tensor", veC)
            P.op("vector", lambda e: e.tensor_scalar(
                r2mu[:], r2mu[:], 1.0 / D, None, ALU.mult), veC)
            P.op("vector", lambda e: e.tensor_tensor(r2tmp[:], r2mu[:], r2mu[:], ALU.mult), veC)
            P.op("vector", lambda e: e.scalar_tensor_tensor(
                r2sq[:], r2sq[:], 1.0 / D, r2tmp[:], ALU.mult, ALU.subtract), veC)
            vv = P.op("vector", lambda e: e.tensor_scalar(
                r2sq[:], r2sq[:], 1e-5, None, ALU.add), veC)
            P.wait("scalar", veC, vv)
            av = P.op("scalar", lambda e: e.activation(r2sq[:], r2sq[:], AF.Sqrt), acC)
            P.wait("vector", acC, av)
            P.op("vector", lambda e: e.reciprocal(r2sq[:], r2sq[:]), veC)
            P.wait("gpsimd", veC)
            P.dma("gpsimd", rowscr[42:43, :], r2mu[:], ldG)
            g_r2 = P.dma("gpsimd", rowscr[43:44, :], r2sq[:], ldG)
            P.wait("sync", ldG)
            P.wait("sync", veC)
            s_b1 = P.dma("sync", mrbc[:, 0:T], rowscr[42:43, :].squeeze(0).partition_broadcast(128), ldS)
            s_b2 = P.dma("sync", mrbc[:, T:2 * T], rowscr[43:44, :].squeeze(0).partition_broadcast(128), ldS)

            sp_x2_done = spS.v
            P.wait("vector", ldS)
            P.op("vector", lambda e: e.memset(mxc[:], 0.0), veC)
            P.op("vector", lambda e: e.memset(ccs[:], 0.0), veC)
            for k in range(DC):
                P.op("vector", lambda e, k=k: e.tensor_tensor(
                    xtmp[:], xd[:, 2048 * k:2048 * (k + 1)], mrbc[:, 0:T], ALU.subtract), veC)
                P.op("vector", lambda e: e.tensor_tensor(
                    xtmp[:], xtmp[:], mrbc[:, T:2 * T], ALU.mult), veC)
                P.op("vector", lambda e: e.tensor_reduce(
                    mxc[:, 1:2], xtmp[:], AX.X, ALU.max, apply_absolute_value=True), veC)
                P.op("vector", lambda e: e.tensor_tensor(
                    mxc[:, 0:1], mxc[:, 0:1], mxc[:, 1:2], ALU.max), veC)

            def cross_max(src_col, ar_i, slot=0):
                P.wait("tensor", veC)
                P.wait("tensor", acC)
                tv = P.op("tensor", lambda e, src_col=src_col: e.transpose(
                    pb[0][0:1, 0:128], src_col, c_eyef[:]), mmC)
                P.wait("vector", mmC, tv)
                P.op("vector", lambda e, slot=slot: e.tensor_reduce(
                    ccs[0:1, slot:slot + 1], pb[0][0:1, 0:128], AX.X, ALU.max), veC)

            def run_ar(ar_i, nslots=16):
                P.wait("gpsimd", veC)
                P.dma("gpsimd", cc_in[ar_i], ccs[:], ldG)
                P.wait("gpsimd", ldG)
                P.op("gpsimd", lambda e, ar_i=ar_i: e.collective_compute(
                    "AllReduce", ALU.max, replica_groups=[list(range(NCORES))],
                    ins=[cc_in[ar_i].opt()], outs=[cc_out[ar_i].opt()]), poC)
                P.wait("gpsimd", poC)
                P.dma("gpsimd", arbc[:], cc_out[ar_i].squeeze(0).partition_broadcast(128), ldG)
                return ldG.v

            cross_max(mxc[:, 0:1], 0, 0)
            arv = run_ar(0)
            P.wait("vector", ldG, arv)
            P.op("vector", lambda e: e.tensor_scalar(
                qsc[:, 0:1], arbc[:, 0:1], 1.0 / QMAX, 1e-12, ALU.mult, ALU.max), veC)
            P.op("vector", lambda e: e.reciprocal(qsc[:, 1:2], qsc[:, 0:1]), veC)

            for k in range(DC):
                P.op("vector", lambda e, k=k: e.tensor_tensor(
                    xtmp[:], xd[:, 2048 * k:2048 * (k + 1)], mrbc[:, 0:T], ALU.subtract), veC)
                P.op("vector", lambda e: e.tensor_tensor(
                    xtmp[:], xtmp[:], mrbc[:, T:2 * T], ALU.mult), veC)
                P.op("vector", lambda e: e.tensor_scalar(
                    xtmp[:].bitcast(I32), xtmp[:], qsc[:, 1:2], None, ALU.mult), veC)
                P.op("vector", lambda e, k=k: e.tensor_scalar(
                    xl2q16[:, 2048 * k:2048 * (k + 1)], xtmp[:].bitcast(I32),
                    qsc[:, 0:1], None, ALU.mult), veC)
            v_xl2q = veC.v
            tap("xl2q", xl2q16[:, 0:T], [(veC, v_xl2q)])

            swld = ar.at_f32(7680, 3072)
            sTr = ar.at_f32(0, 3072)
            sTr2 = ar.at_f32(10752, 1024)

            def emit_scales(nat, sTd, sTrd, ntile, ncols, sTrt=None):
                nblk = ncols // 32
                sTrt = sTr if sTrt is None else sTrt
                nslot = 2 if 2 * ncols <= 3072 else 1
                for i in range(4):
                    P.wait("vector", spl[i], spl_v[i])
                P.wait("sync", veC)
                P.wait("sync", spS)
                P.wait("sync", mmC)
                sw_u = [0] * nslot
                for it in range(ntile):
                    s = it % nslot
                    P.wait("sync", veC, sw_u[s])
                    lv = P.dma("sync", swld[:, ncols * s:ncols * s + ncols],
                               nat[128 * it:128 * it + 128, :], swl[s], serial=True)
                    P.wait("vector", swl[s], lv)
                    P.op("vector", lambda e, s=s, ncols=ncols, nblk=nblk: e.tensor_reduce(
                        mstage[:, 0:nblk],
                        swld[:, ncols * s:ncols * s + ncols].rearrange("p (b w) -> p b w", w=32),
                        AX.X, ALU.max, apply_absolute_value=True), veC)
                    sw_u[s] = veC.v
                    P.op("vector", lambda e, nblk=nblk: e.tensor_scalar(
                        mstage[:, 0:nblk], mstage[:, 0:nblk], 1.0 / WQ, 1e-12,
                        ALU.mult, ALU.max), veC)
                    P.wait("tensor", veC)
                    tv = P.op("tensor", lambda e, nblk=nblk: e.transpose(
                        pb[0][0:nblk, 0:128], mstage[:, 0:nblk], c_eyef[:]), mmC)
                    P.wait("vector", mmC, tv)
                    P.op("vector", lambda e, it=it, nblk=nblk: e.tensor_copy(
                        sTs[0:nblk, 128 * it:128 * it + 128], pb[0][0:nblk, 0:128]), veC)
                P.wait("tensor", veC)
                P.op("vector", lambda e, ntile=ntile, nblk=nblk, sTrt=sTrt: e.reciprocal(
                    sTrt[0:nblk, 0:ntile * 128], sTs[0:nblk, 0:ntile * 128]), veC)
                P.wait("gpsimd", veC)
                P.dma("gpsimd", sTd, sTs[0:nblk, 0:ntile * 128], ldG)
                v = P.dma("gpsimd", sTrd, sTrt[0:nblk, 0:ntile * 128], ldG)
                return v

            def emit_deq(wT, sTd, sTrd, dst16, nk, kcols, blk_per_128):
                P.wait("tensor", veC)
                P.wait("tensor", acC)
                P.wait("sync", ldG)
                wld_u = [0, 0]
                sMM_u = [0, 0]
                bank_u = [0, 0]
                i = 0
                for k in range(nk):
                    for pc in range(kcols // 512):
                        c0 = 512 * pc
                        s = i % 2
                        wld = deq_wld if s == 0 else deq_wld2
                        sst = sstgA if s == 0 else sstgB
                        bR = pb[2 + 2 * s]
                        bS = pb[3 + 2 * s]
                        P.wait("sync", veC, wld_u[s])
                        lw = P.dma("sync", wld[:, 0:512],
                                   wT[128 * k:128 * k + 128, c0:c0 + 512],
                                   wls[s], serial=True)
                        P.wait("sync", mmC, sMM_u[s])
                        row = blk_per_128 * k
                        P.dma("sync", sst[0:4, 0:512], sTrd[row:row + 4, c0:c0 + 512],
                              sls[s], serial=True)
                        ls = P.dma("sync", sst[4:8, 0:512], sTd[row:row + 4, c0:c0 + 512],
                                   sls[s], serial=False)
                        P.wait("tensor", sls[s], ls)
                        P.wait("tensor", veC, bank_u[s])
                        P.op("tensor", lambda e, bR=bR, sst=sst: e.matmul(
                            bR[:, 0:512], c_ind8[0:8, 0:128], sst[0:8, 0:512],
                            start=True, stop=True), mmC)
                        mmv = P.op("tensor", lambda e, bS=bS, sst=sst: e.matmul(
                            bS[:, 0:512], c_ind8[0:8, 128:256], sst[0:8, 0:512],
                            start=True, stop=True), mmC)
                        sMM_u[s] = mmv
                        P.wait("vector", wls[s], lw)
                        P.wait("vector", mmC, mmv)
                        P.op("vector", lambda e, wld=wld, bR=bR: e.tensor_tensor(
                            deq_rint[:, 0:512], wld[:, 0:512], bR[:, 0:512], ALU.mult), veC)
                        P.op("vector", lambda e, k=k, c0=c0, dst16=dst16, kcols=kcols, bS=bS: e.tensor_tensor(
                            dst16[:, kcols * k + c0:kcols * k + c0 + 512],
                            deq_rint[:, 0:512], bS[:, 0:512], ALU.mult), veC)
                        wld_u[s] = veC.v
                        bank_u[s] = veC.v
                        i += 1
                return veC.v

            spl_v = [0, 0, 0, 0]

            def emit_big_mms(w16, src16, spill, mx_slot, kcols):
                P.op("vector", lambda e, mx_slot=mx_slot: e.memset(mxc[:, mx_slot:mx_slot + 1], 0.0), veC)
                P.wait("tensor", veC)
                P.wait("tensor", acC)
                ev_v = {}
                ev_a = {}
                for t in range(TG):
                    for it in range(IT):
                        key = t * IT + it
                        pj = pb[key % 4]
                        sl = key % 4
                        if key >= 4:
                            P.wait("tensor", veC, ev_v[key - 4])
                            P.wait("tensor", acC, ev_a[key - 4])
                        for k in range(DC):
                            P.op("tensor", lambda e, pj=pj, k=k, it=it, t=t, w16=w16, kcols=kcols: e.matmul(
                                pj[:, 0:512],
                                w16[:, kcols * k + 128 * it:kcols * k + 128 * it + 128],
                                xl2q16[:, 2048 * k + 512 * t:2048 * k + 512 * t + 512],
                                start=(k == 0), stop=(k == DC - 1)),
                                mmC if k == DC - 1 else None)
                        mmv = mmC.v
                        P.wait("scalar", mmC, mmv)
                        P.wait("scalar", spl[sl], spl_v[sl])
                        P.op("scalar", lambda e, pj=pj, sl=sl: e.activation(
                            ev16[:, 512 * sl:512 * sl + 512], pj[:, 0:512], AF.Copy), acC)
                        ev_a[key] = acC.v
                        P.wait("vector", acC, ev_a[key])
                        P.op("vector", lambda e, pj=pj: e.tensor_reduce(
                            mxc[:, 6:7], pj[:, 0:512], AX.X, ALU.max,
                            apply_absolute_value=True), veC)
                        P.op("vector", lambda e, mx_slot=mx_slot: e.tensor_tensor(
                            mxc[:, mx_slot:mx_slot + 1], mxc[:, mx_slot:mx_slot + 1],
                            mxc[:, 6:7], ALU.max), veC)
                        ev_v[key] = veC.v
                        P.wait("sync", acC, ev_a[key])
                        spl_v[sl] = P.dma(
                            "sync", spill[128 * it:128 * it + 128, 512 * t:512 * t + 512],
                            ev16[:, 512 * sl:512 * sl + 512], spl[sl], serial=False)

            _stage_gate(3)
            P.wait("gpsimd", veC, v_xl2q)
            P.wait("gpsimd", spS, sp_x2_done)
            sv = emit_scales(wg_nat, sT_g, sTr_g, 24, D)
            P.wait("gpsimd", ldG, sv)
            emit_deq(wgT, sT_g, sTr_g, wgq16, DC, FI, 4)
            P.wait("tensor", veC)
            emit_big_mms(wgq16, xl2q16, sp_gate, 2, FI)
            cross_max(mxc[:, 2:3], 1, 0)
            arv = run_ar(1)

            _stage_gate(5)
            sv = emit_scales(wu_nat, sT_u, sTr_u, 24, D)
            P.wait("gpsimd", ldG, sv)
            emit_deq(wuT, sT_u, sTr_u, wuq16, DC, FI, 4)
            P.wait("vector", ldG, arv)
            P.op("vector", lambda e: e.tensor_scalar(
                qsc[:, 2:3], arbc[:, 0:1], 1.0 / QMAX, 1e-12, ALU.mult, ALU.max), veC)
            P.op("vector", lambda e: e.reciprocal(qsc[:, 3:4], qsc[:, 2:3]), veC)
            P.wait("tensor", veC)
            emit_big_mms(wuq16, xl2q16, sp_up, 4, FI)

            _stage_gate(5.5)
            sv = emit_scales(wd_nat, sT_d, sTr_d, DC, FI, sTrt=sTr2)
            P.wait("gpsimd", ldG, sv)
            emit_deq(wdT, sT_d, sTr_d, wdq16, IT, D, 4)

            _stage_gate(6)
            pttB = [ptt, ar.at_f32(16384, 2048)]
            rgtB = [rgt, ar.at_i32(18432, 2048)]
            tftB = [tft, ar.at_f32(20480, 2048)]
            h16B = [h16t, ar.at_f16(22528, 2048)]
            hRm = [0, 0]
            rgU = [0, 0]
            tfW = [0, 0]
            spPv = [0, 0]
            P.op("vector", lambda e: e.memset(mxc[:, 5:6], 0.0), veC)
            P.wait("scalar", veC)
            for i in range(4):
                P.wait("gpsimd", spl[i], spl_v[i])
                P.wait("sync", spl[i], spl_v[i])
            for it in range(IT):
                s = it % 2
                P.wait("sync", acC, hRm[s])
                lv = P.dma("sync", h16B[s][:, 0:T], sp_gate[128 * it:128 * it + 128, :],
                           wls[s], serial=True)
                P.wait("scalar", wls[s], lv)
                P.wait("scalar", veC, rgU[s])
                rv = P.op("scalar", lambda e, s=s: e.activation(
                    rgtB[s][:, 0:T], h16B[s][:, 0:T], AF.Copy, scale=qsc[:, 3:4]), acC)
                hRm[s] = rv
                P.wait("scalar", veC, tfW[s])
                av = P.op("scalar", lambda e, s=s: e.activation(
                    tftB[s][:, 0:T], rgtB[s][:, 0:T], AF.Sigmoid, scale=qsc[:, 2:3]), acC)
                P.wait("vector", acC, av)
                P.op("vector", lambda e, s=s: e.tensor_scalar(
                    tftB[s][:, 0:T].bitcast(I32), tftB[s][:, 0:T], 65536.0, 65535.0,
                    ALU.mult, ALU.min), veC)
                P.wait("vector", spP[s], spPv[s])
                P.op("vector", lambda e, s=s: e.tensor_tensor(
                    pttB[s][:, 0:T], rgtB[s][:, 0:T], tftB[s][:, 0:T].bitcast(I32),
                    ALU.mult), veC)
                rgU[s] = veC.v
                tfW[s] = veC.v
                P.op("vector", lambda e, s=s: e.tensor_reduce(
                    mxc[:, 6:7], pttB[s][:, 0:T], AX.X, ALU.max,
                    apply_absolute_value=True), veC)
                P.op("vector", lambda e: e.tensor_tensor(
                    mxc[:, 5:6], mxc[:, 5:6], mxc[:, 6:7], ALU.max), veC)
                P.wait("gpsimd", veC)
                spPv[s] = P.dma("gpsimd", sp_P[128 * it:128 * it + 128, :],
                                pttB[s][:, 0:T], spP[s], serial=False)
            cross_max(mxc[:, 4:5], 2, 0)
            cross_max(mxc[:, 5:6], 2, 1)
            arv = run_ar(2)
            P.wait("vector", ldG, arv)
            P.op("vector", lambda e: e.tensor_scalar(
                qsc[:, 4:5], arbc[:, 0:1], 1.0 / QMAX, 1e-12, ALU.mult, ALU.max), veC)
            P.op("vector", lambda e: e.reciprocal(qsc[:, 5:6], qsc[:, 4:5]), veC)
            P.op("vector", lambda e: e.tensor_scalar(
                mxc[:, 7:8], arbc[:, 1:2], qsc[:, 2:3], 1.0 / 65536.0, ALU.mult, ALU.mult), veC)
            P.op("vector", lambda e: e.tensor_scalar(
                qsc[:, 6:7], mxc[:, 7:8], 1.0 / QMAX, 1e-12, ALU.mult, ALU.max), veC)
            P.op("vector", lambda e: e.reciprocal(qsc[:, 7:8], qsc[:, 6:7]), veC)
            P.op("vector", lambda e: e.tensor_scalar(
                qsc[:, 7:8], qsc[:, 7:8], qsc[:, 2:3], 1.0 / 65536.0, ALU.mult, ALU.mult), veC)

            _stage_gate(7)
            P.op("vector", lambda e: e.memset(mxc[:, 5:6], 0.0), veC)
            P.wait("scalar", veC)
            for i in range(4):
                P.wait("gpsimd", spl[i], spl_v[i])
                P.wait("sync", spl[i], spl_v[i])
            for it in range(IT):
                s = it % 2
                P.wait("sync", acC, hRm[s])
                lv = P.dma("sync", h16B[s][:, 0:T], sp_up[128 * it:128 * it + 128, :],
                           wls[s], serial=True)
                P.wait("gpsimd", veC, tfW[s])
                lv2 = P.dma("gpsimd", tftB[s][:, 0:T], sp_P[128 * it:128 * it + 128, :],
                            ldG, serial=True)
                P.wait("scalar", wls[s], lv)
                P.wait("scalar", veC, rgU[s])
                rv = P.op("scalar", lambda e, s=s: e.activation(
                    rgtB[s][:, 0:T], h16B[s][:, 0:T], AF.Copy, scale=qsc[:, 5:6]), acC)
                hRm[s] = rv
                P.wait("scalar", ldG, lv2)
                av = P.op("scalar", lambda e, s=s: e.activation(
                    tftB[s][:, 0:T].bitcast(I32), tftB[s][:, 0:T], AF.Copy,
                    scale=qsc[:, 7:8]), acC)
                P.wait("vector", acC, av)
                P.wait("vector", spP[s], spPv[s])
                P.op("vector", lambda e, s=s: e.tensor_tensor(
                    pttB[s][:, 0:T], tftB[s][:, 0:T].bitcast(I32), rgtB[s][:, 0:T],
                    ALU.mult), veC)
                rgU[s] = veC.v
                tfW[s] = veC.v
                P.op("vector", lambda e, s=s: e.tensor_reduce(
                    mxc[:, 6:7], pttB[s][:, 0:T], AX.X, ALU.max,
                    apply_absolute_value=True), veC)
                P.op("vector", lambda e: e.tensor_tensor(
                    mxc[:, 5:6], mxc[:, 5:6], mxc[:, 6:7], ALU.max), veC)
                P.wait("gpsimd", veC)
                spPv[s] = P.dma("gpsimd", sp_Q[128 * it:128 * it + 128, :],
                                pttB[s][:, 0:T], spP[s], serial=False)
            cross_max(mxc[:, 5:6], 3, 0)
            arv = run_ar(3)
            P.wait("vector", ldG, arv)
            P.op("vector", lambda e: e.tensor_scalar(
                mxc[:, 7:8], arbc[:, 0:1], qsc[:, 6:7], None, ALU.mult), veC)
            P.op("vector", lambda e: e.tensor_tensor(
                mxc[:, 7:8], mxc[:, 7:8], qsc[:, 4:5], ALU.mult), veC)
            P.op("vector", lambda e: e.tensor_scalar(
                qsc[:, 8:9], mxc[:, 7:8], 1.0 / QMAX, 1e-12, ALU.mult, ALU.max), veC)
            P.op("vector", lambda e: e.reciprocal(qsc[:, 9:10], qsc[:, 8:9]), veC)
            P.op("vector", lambda e: e.tensor_scalar(
                qsc[:, 9:10], qsc[:, 9:10], qsc[:, 6:7], None, ALU.mult), veC)
            P.op("vector", lambda e: e.tensor_tensor(
                qsc[:, 9:10], qsc[:, 9:10], qsc[:, 4:5], ALU.mult), veC)

            _stage_gate(8)
            of16B = ar.at_f16(16384, 12288)
            x2gB = ar.at_f32(13312, 512)
            ytB = ar.at_f32(13824, 512)
            qld_v = [0, 0]
            x2u_v = [0, 0]
            sty_v = [0, 0]
            dn_v = {}
            dn_a = {}

            qdt_u = [0, 0]

            def build_of(t, dst):
                def one(it):
                    p = it % 2
                    P.wait("sync", veC, qdt_u[p])
                    lv = P.dma("sync", qdt[:, 1024 * p:1024 * p + 512],
                               sp_Q[128 * it:128 * it + 128, 512 * t:512 * t + 512],
                               qls[p], serial=True)
                    P.wait("vector", qls[p], lv)
                    P.op("vector", lambda e, p=p: e.tensor_scalar(
                        qdt[:, 1024 * p + 512:1024 * p + 1024].bitcast(I32),
                        qdt[:, 1024 * p:1024 * p + 512], qsc[:, 9:10],
                        None, ALU.mult), veC)
                    P.op("vector", lambda e, it=it, p=p, dst=dst: e.tensor_scalar(
                        dst[:, 512 * it:512 * it + 512],
                        qdt[:, 1024 * p + 512:1024 * p + 1024].bitcast(I32),
                        qsc[:, 8:9], None, ALU.mult), veC)
                    qdt_u[p] = veC.v
                return one

            buf = [of16, of16B]
            P.wait("sync", spS)
            P.wait("sync", spP[0], spPv[0])
            P.wait("sync", spP[1], spPv[1])
            P.wait("sync", ldG)
            P.wait("sync", veC)
            x2u_v[0] = veC.v
            x2u_v[1] = veC.v
            b0 = build_of(0, buf[0])
            for it in range(IT):
                b0(it)
            for t in range(TG):
                cur = buf[t % 2]
                bnext = build_of(t + 1, buf[(t + 1) % 2]) if t + 1 < TG else None
                P.wait("tensor", veC)
                P.wait("tensor", acC)
                for hc in range(DC):
                    key = t * DC + hc
                    pj = pb[key % 4]
                    s = key % 2
                    xg = x2g if s == 0 else x2gB
                    yy = yt if s == 0 else ytB
                    P.wait("sync", veC, x2u_v[s])
                    lx = P.dma("sync", xg[:, 0:512],
                               sp_x2[128 * hc:128 * hc + 128, 512 * t:512 * t + 512],
                               lxs[s], serial=True)
                    if key >= 4:
                        P.wait("tensor", veC, dn_v[key - 4])
                    for it in range(IT):
                        P.op("tensor", lambda e, pj=pj, hc=hc, it=it, cur=cur: e.matmul(
                            pj[:, 0:512],
                            wdq16[:, D * it + 128 * hc:D * it + 128 * hc + 128],
                            cur[:, 512 * it:512 * it + 512],
                            start=(it == 0), stop=(it == IT - 1)),
                            mmC if it == IT - 1 else None)
                    mmv = mmC.v
                    P.wait("vector", lxs[s], lx)
                    P.wait("vector", mmC, mmv)
                    P.wait("vector", sty[s], sty_v[s])
                    P.op("vector", lambda e, pj=pj, yy=yy, xg=xg: e.tensor_tensor(
                        yy[:, 0:512], pj[:, 0:512], xg[:, 0:512], ALU.add), veC)
                    dn_v[key] = veC.v
                    x2u_v[s] = veC.v
                    P.wait("sync", veC, dn_v[key])
                    sty_v[s] = P.dma(
                        "sync", y[128 * hc:128 * hc + 128, 512 * t:512 * t + 512],
                        yy[:, 0:512], sty[s], serial=False)
                    if bnext is not None:
                        for j in range(3):
                            bnext(3 * hc + j)


        except _StageDone:
            pass
        _final_emit()

        block = ctx.enter_context(nc.Block())

        def mk(eng_name):
            def body(eng):
                for fn in P.q[eng_name]:
                    fn(eng)
            return body
        block.sync(mk("sync"))
        block.scalar(mk("scalar"))
        block.vector(mk("vector"))
        block.tensor(mk("tensor"))
        block.gpsimd(mk("gpsimd"))

    return nc, taps


def prepare_inputs(inputs):
    x = np.asarray(inputs["L0_attn_out"], dtype=np.float32)
    ww = np.asarray(inputs["L0_attn_W_write"], dtype=np.float32)
    wr = np.asarray(inputs["L0_attn_W_read"], dtype=np.float32)
    wa = np.asarray(inputs["L0_attn_W_alpha"], dtype=np.float32)
    wb = np.asarray(inputs["L0_attn_W_beta"], dtype=np.float32)
    alog = np.asarray(inputs["L0_attn_A_log"], dtype=np.float32).reshape(1, NH)
    dtb = np.asarray(inputs["L0_attn_dt_bias"], dtype=np.float32).reshape(1, NH)
    wg = np.asarray(inputs["L0_ffn_w_gate"], dtype=np.float32)
    wu = np.asarray(inputs["L0_ffn_w_up"], dtype=np.float32)
    wd = np.asarray(inputs["L0_ffn_w_down"], dtype=np.float32)

    shared = {
        "wwT": np.ascontiguousarray(ww.T),
        "wrT": np.ascontiguousarray(wr.T),
        "wbaT": np.ascontiguousarray(np.concatenate([wb.T, wa.T], axis=1)),
        "alog": np.exp(alog) if False else alog,
        "dtb": dtb,
        "wuT": np.ascontiguousarray(wu.T),
        "wgT": np.ascontiguousarray(wg.T),
        "wdT": np.ascontiguousarray(wd.T),
        "wu_nat": wu, "wg_nat": wg, "wd_nat": wd,
        "eyef32": np.eye(128, dtype=np.float32),
        "eye16": np.eye(128, dtype=np.float16),
        "ones1": np.ones((1, 128), dtype=np.float32),
    }
    j = np.arange(128)[:, None]
    i = np.arange(128)[None, :]
    same = (j // 64) == (i // 64)
    shared["cumtri16"] = (same & (j <= i)).astype(np.float16)
    shared["blksum16"] = same.astype(np.float16)
    keepU = (same & (j < i))
    keepL = (same & (j > i))
    shared["kaddU16"] = np.where(keepU, 0.0, -60000.0).astype(np.float16)
    shared["kaddL16"] = np.where(keepL, 0.0, -60000.0).astype(np.float16)
    shared["onescol"] = np.ones((128, 1), dtype=np.float32)
    shared["ones1k"] = np.ones((1, 1024), dtype=np.float32)
    r8 = np.arange(8)[:, None]
    p8 = np.arange(128)[None, :]
    i8 = np.zeros((8, 256), dtype=np.float32)
    i8[:, 0:128] = (r8 == p8 // 32) & (r8 < 4)
    i8[:, 128:256] = (r8 - 4) == (p8 // 32)
    shared["ind8"] = i8

    in_maps = []
    for b in range(NCORES):
        m = dict(shared)
        m["x_tok"] = np.ascontiguousarray(x[b])
        m["x_dmaj"] = np.ascontiguousarray(x[b].T)
        in_maps.append(m)
    return in_maps


def kernel(**inputs):
    in_maps = prepare_inputs(inputs)
    nc, _ = build()
    res = run_bass_kernel_spmd(nc, in_maps, core_ids=list(range(NCORES)))
    out = np.stack([np.asarray(r["y"]).T for r in res.results], axis=0)
    return np.ascontiguousarray(out, dtype=np.float32)


if __name__ == "__main__":
    pass


# version 15
# speedup vs baseline: 1.0161x; 1.0005x over previous
import contextlib
import math
import numpy as np

import concourse.bass as bass
import concourse.mybir as mybir
from concourse.bass_utils import run_bass_kernel_spmd

F32 = mybir.dt.float32
F16 = mybir.dt.float16
I32 = mybir.dt.int32
AF = mybir.ActivationFunctionType
ALU = mybir.AluOpType
AX = mybir.AxisListType

NCORES = 8
T = 2048
D = 1024
NH = 8
HD = 128
CH = 64
NT = 16
DC = 8
IT = 24
FI = 3072
TG = 4
QMAX = 32767.0
WQ = 7.0

ENGINES = ("sync", "scalar", "vector", "tensor", "gpsimd")


class Ct:
    __slots__ = ("sem", "v")

    def __init__(self, sem):
        self.sem = sem
        self.v = 0


class Prog:

    def __init__(self):
        self.q = {e: [] for e in ENGINES}
        self.ect = {}

    def on(self, eng, fn):
        self.q[eng].append(fn)

    def wait(self, eng, ct, val=None):
        v = ct.v if val is None else val
        self.on(eng, lambda e, s=ct.sem, v=v: e.wait_ge(s, v))

    def dma(self, eng, out, in_, ct, serial=True, **kw):
        own = self.ect.get(eng)
        if own is not None:
            self.wait(eng, own, own.v)
        if serial:
            self.wait(eng, ct, ct.v)
        ct.v += 16
        self.on(eng, lambda e, o=out, i=in_, s=ct.sem, kw=kw:
                e.dma_start(out=o, in_=i, **kw).then_inc(s, 16))
        return ct.v

    def op(self, eng, fn, ct=None, inc=1):
        own = self.ect[eng]
        if ct is None or ct is own:
            ct = own
            self.wait(eng, own, own.v)
            own.v += inc
            v = own.v
            def g(e, fn=fn, s=own.sem, inc=inc):
                fn(e).then_inc(s, inc)
            self.on(eng, g)
            return v
        raise NotImplementedError


def ceil_div(a, b):
    return -(-a // b)


class Arena:

    def __init__(self, nc, base, words):
        self.nc = nc
        self.base = base
        self.W = words
        self.off = 0
        self.n = 0

    def _at(self, word_off, elems, dt, esz, part=128):
        assert (self.base + 4 * word_off) % 32 == 0, f"unaligned word {word_off}"
        end_w = word_off + (elems * esz + 3) // 4
        assert end_w <= self.W, f"arena OOM {end_w} > {self.W}"
        self.n += 1
        return self.nc.alloc_sbuf_tensor_at(
            f"arn{self.n}", [part, elems], dt, offset=self.base + 4 * word_off)

    def reserve(self, n):
        a = (self.off + 7) & ~7
        self.off = a + n
        assert self.off <= self.W, f"arena OOM {self.off} > {self.W}"
        return a

    def f32(self, n):
        return self._at(self.reserve(n), n, F32, 4)

    def f16(self, n):
        assert n % 2 == 0
        return self._at(self.reserve(n // 2), n, F16, 2)

    def at_f32(self, word_off, n, part=128):
        return self._at(word_off, n, F32, 4, part)

    def at_f16(self, word_off, n):
        return self._at(word_off, n, F16, 2)

    def at_i32(self, word_off, n):
        return self._at(word_off, n, I32, 4)


def build(debug=(), upto=99):
    nc = bass.Bass("TRN2", target_bir_lowering=False)
    dbg = set(debug)

    dram = {}
    def din(name, shape, dt=F32):
        dram[name] = nc.dram_tensor(name, list(shape), dt, kind="ExternalInput").ap()
        return dram[name]
    def dint(name, shape, dt=F32):
        dram[name] = nc.dram_tensor(name, list(shape), dt, kind="Internal").ap()
        return dram[name]
    def dout(name, shape, dt=F32):
        dram[name] = nc.dram_tensor(name, list(shape), dt, kind="ExternalOutput").ap()
        return dram[name]

    x_tok = din("x_tok", [T, D])
    x_dmaj = din("x_dmaj", [D, T])
    wwT = din("wwT", [D, D])
    wrT = din("wrT", [D, D])
    wbaT = din("wbaT", [D, 16])
    alog = din("alog", [1, NH])
    dtb = din("dtb", [1, NH])
    wuT = din("wuT", [D, FI])
    wgT = din("wgT", [D, FI])
    wdT = din("wdT", [FI, D])
    wu_nat = din("wu_nat", [FI, D])
    wg_nat = din("wg_nat", [FI, D])
    wd_nat = din("wd_nat", [D, FI])
    eyef32 = din("eyef32", [128, 128])
    eye16 = din("eye16", [128, 128], F16)
    cumtri16 = din("cumtri16", [128, 128], F16)
    blksum16 = din("blksum16", [128, 128], F16)
    kaddU16 = din("kaddU16", [128, 128], F16)
    kaddL16 = din("kaddL16", [128, 128], F16)
    ones1 = din("ones1", [1, 128])
    onescol = din("onescol", [128, 1])
    ones1k = din("ones1k", [1, 1024])
    ind8 = din("ind8", [8, 256])

    rowscr = dint("rowscr", [48, T])
    sp_gate = dint("sp_gate", [FI, T], F16)
    sp_up = dint("sp_up", [FI, T], F16)
    sp_P = dint("sp_P", [FI, T])
    sp_Q = dint("sp_Q", [FI, T])
    sp_x2 = dint("sp_x2", [D, T])
    sT_u = dint("sT_u", [32, FI])
    sTr_u = dint("sTr_u", [32, FI])
    sT_g = dint("sT_g", [32, FI])
    sTr_g = dint("sTr_g", [32, FI])
    sT_d = dint("sT_d", [96, D])
    sTr_d = dint("sTr_d", [96, D])
    cedscr = dint("cedscr", [8, 2])
    rows2 = dint("rows2", [NT, 3, 1024])
    rk42 = dint("rk42", [NT, 1024], F16)
    ced2 = dint("ced2", [NT, 16])
    cc_in = [dint(f"cc_in{i}", [1, 16]) for i in range(4)]
    cc_out = [dint(f"cc_out{i}", [1, 16]) for i in range(4)]
    y = dout("y", [D, T])

    P = Prog()
    taps = []

    class _StageDone(Exception):
        pass

    ctx = contextlib.ExitStack()
    with ctx:
        sb = lambda nm, shape, dt=F32: ctx.enter_context(nc.sbuf_tensor(nm, shape, dt))
        def sem(nm):
            return Ct(ctx.enter_context(nc.semaphore(nm)))

        ba = sb("ba", [128, NT * 16])
        wba16 = sb("wba16", [128, DC * 16], F16)
        sc_cum = sb("sc_cum", [128, 128])
        sc_ccum = sb("sc_ccum", [128, 128])
        sc_lnb = sb("sc_lnb", [128, 128])
        sc_ss = sb("sc_ss", [128, 128])
        sc_lnn2 = sb("sc_lnn2", [128, 128])
        sc_lnn2sh = sb("sc_lnn2sh", [128, 128])
        sc_p1 = sb("sc_p1", [128, 128])
        sc_p2 = sb("sc_p2", [128, 128])
        sc_tg = sb("sc_tg", [128, 128])
        sc_beta = sb("sc_beta", [128, 128])
        sc_usc = sb("sc_usc", [128, 128])
        sc_ld16 = sb("sc_ld16", [128, 128], F16)
        st_mu = sb("st_mu", [128, 16])
        st_r = sb("st_r", [128, 16])
        st_mush = sb("st_mush", [128, 16])
        st_rsh = sb("st_rsh", [128, 16])
        st_sum = sb("st_sum", [128, 16])
        st_sq = sb("st_sq", [128, 16])
        st_tmp = sb("st_tmp", [128, 16])
        rows = sb("rows", [48, T])
        alogbc = sb("alogbc", [128, NH])
        dtbbc = sb("dtbbc", [128, NH])
        s16 = sb("s16", [128, NH * 128], F16)
        arbc = sb("arbc", [128, 16])
        mxc = sb("mxc", [128, 8])
        qsc = sb("qsc", [128, 12])
        ccs = sb("ccs", [1, 16])
        c_eyef = sb("c_eyef", [128, 128])
        c_eye16 = sb("c_eye16", [128, 128], F16)
        c_cumtri = sb("c_cumtri", [128, 128], F16)
        c_blksum = sb("c_blksum", [128, 128], F16)
        c_kaddU = sb("c_kaddU", [128, 128], F16)
        c_kaddL = sb("c_kaddL", [128, 128], F16)
        c_ones1 = sb("c_ones1", [1, 128])
        c_onescol = sb("c_onescol", [128, 1])
        c_ind8 = sb("c_ind8", [8, 256])
        sck = sb("sck", [128, NT * 48])

        _la = nc._sbuf_addr_for_side("left")
        _ra = nc._sbuf_addr_for_side("right")
        _abase = (_la + 31) & ~31
        AWORDS = (_ra - _abase) // 4
        assert AWORDS >= 43800, f"arena too small: {AWORDS}"
        import os
        if os.environ.get('PRINT_AWORDS'): print('AWORDS =', AWORDS)
        ar = Arena(nc, _abase, AWORDS)

        xd = ar.f32(DC * T)
        xl1d = ar.f16(DC * (T + 1) + 8)
        wwb = ar.f16(DC * D)
        wrb = ar.f16(DC * D)
        v16 = ar.f16(D)
        o16 = ar.f16(NH * 512)
        xt16 = v16
        xsh16 = ar.f16(D)
        xshx16 = ar.f16(D)
        xcur16 = xshx16
        rk4bc = ar.f16(NH * 128)
        cedbc = ar.f32(16)
        UN0 = ar.reserve(6912)
        mrbc = ar.at_f32(UN0, 2 * T)
        xtmp = ar.at_f32(UN0 + 2 * T, T)
        fbc = ar.at_f32(UN0, 3072)
        argbuf = ar.at_f32(UN0 + 3072, 1024)
        wkU = ar.at_f16(UN0, 1024)
        wkL = ar.at_f16(UN0 + 512, 1024)
        wkQ = ar.at_f16(UN0 + 1024, 1024)
        wkU2 = ar.at_f16(UN0 + 1536, 1024)
        wkL2 = ar.at_f16(UN0 + 2048, 1024)
        wkQ2 = ar.at_f16(UN0 + 2560, 1024)
        expw = ar.at_f16(UN0 + 4096, 3072)
        wkAb = ar.at_f16(UN0 + 4096, 1024)
        wkAn = ar.at_f16(UN0 + 4608, 1024)
        mstage = ar.at_f32(UN0 + 5632, 512)
        eq16 = ar.at_f16(UN0 + 6144, 1024)
        rkds = ar.at_f16(UN0 + 6656, 512)
        sptmp = ar.at_f32(UN0 + 768, 128)
        sptmp2 = ar.at_f32(UN0 + 896, 128)
        PX0 = UN0 + 6912
        vq = ar.at_f16(PX0, 1024)
        vqu = ar.at_f16(PX0 + 512, 1024)
        wkITx = [ar.at_f16(PX0 + 1024, 1024), ar.at_f16(PX0 + 1536, 1024)]
        wkGT = ar.at_f16(PX0 + 2048, 1024)
        rk4bcx = [rk4bc, ar.at_f16(PX0 + 2560, 1024)]
        cedbcx = [cedbc, ar.at_f32(PX0 + 3072, 16)]
        rtmp = ar.at_f32(UN0 + 6144, 256, part=16)

        pb = [ctx.enter_context(nc.psum_tensor(f"pb{i}", [128, 512], F32))
              for i in range(8)]

        ALLCTS = []
        def sem2(nm):
            c = sem(nm)
            ALLCTS.append(c)
            return c
        ldS = sem2("ldS"); ldG = sem2("ldG"); ldR = sem2("ldR")
        mmC = sem2("mmC"); veC = sem2("veC"); acC = sem2("acC"); poC = sem2("poC")
        spS = sem2("spS"); ccC = sem2("ccC"); stY = sem2("stY")
        spl = [sem2(f"spl{i}") for i in range(4)]
        lxs = [sem2(f"lxs{i}") for i in range(2)]
        sty = [sem2(f"sty{i}") for i in range(2)]
        qls = [sem2(f"qls{i}") for i in range(2)]
        wls = [sem2(f"wls{i}") for i in range(2)]
        sls = [sem2(f"sls{i}") for i in range(2)]
        swl = [sem2(f"swl{i}") for i in range(2)]
        spP = [sem2(f"spP{i}") for i in range(2)]
        ltf = [sem2(f"ltf{i}") for i in range(2)]
        P.ect = {"tensor": mmC, "vector": veC, "scalar": acC, "gpsimd": poC}

        def tap(name, ap, waits):
            if name not in dbg:
                return
            t = nc.dram_tensor("tap_" + name, list(ap.shape),
                               ap.dtype, kind="ExternalOutput").ap()
            taps.append("tap_" + name)
            for (ct, v) in waits:
                P.wait("gpsimd", ct, v)
            P.dma("gpsimd", t, ap, stY)

        def _stage_gate(lvl):
            if upto <= lvl:
                raise _StageDone

        def _final_emit():
            for ct in ALLCTS:
                if ct is not stY:
                    P.wait("gpsimd", ct, ct.v)
            if stY.v == 0 and sty[0].v == 0 and sty[1].v == 0:
                P.dma("gpsimd", y[0:128, :], xd[:, 0:T], stY)
            P.wait("gpsimd", stY, stY.v)

        try:
            P.op("vector", lambda e: e.memset(rows[:], 0.0), veC)
            P.op("vector", lambda e: e.memset(sck[:], 0.0), veC)
            P.dma("sync", xd[:].rearrange("p (c t) -> p c t", c=DC), x_dmaj.rearrange("(c p) t -> p c t", p=128), ldS)
            for cst, srcd in ((c_eyef, eyef32), (c_eye16, eye16), (c_cumtri, cumtri16),
                              (c_blksum, blksum16), (c_kaddU, kaddU16), (c_kaddL, kaddL16)):
                P.dma("sync", cst[:], srcd, ldS)
            P.dma("sync", c_ones1[:], ones1, ldS)
            P.dma("sync", c_onescol[:], onescol, ldS)
            P.dma("sync", c_ind8[:], ind8, ldS)
            ld_consts = ldS.v
            P.dma("gpsimd", wwb[:].rearrange("p (c e) -> p c e", c=DC), wwT.rearrange("(c p) e -> p c e", p=128), ldG)
            P.dma("gpsimd", wrb[:].rearrange("p (c e) -> p c e", c=DC), wrT.rearrange("(c p) e -> p c e", p=128), ldG)
            P.dma("gpsimd", wba16[:].rearrange("p (c e) -> p c e", c=DC), wbaT.rearrange("(c p) e -> p c e", p=128), ldG)
            ld_ww = ldG.v
            P.dma("gpsimd", alogbc[:], alog.squeeze(0).partition_broadcast(128), ldG)
            P.dma("gpsimd", dtbbc[:], dtb.squeeze(0).partition_broadcast(128), ldG)
            ld_ab = ldG.v

            _stage_gate(0.1)
            am_ = [0, 0]
            for m in range(NT):
                s = m % 2
                buf = xt16 if s == 0 else xsh16
                P.wait("gpsimd", acC, am_[s])
                lv = P.dma("gpsimd", buf[:], x_tok[128 * m:128 * (m + 1), :], ldG)
                P.wait("scalar", ldG, lv)
                P.op("scalar", lambda e, m=m, buf=buf: e.activation(
                    xcur16[:], buf[:], AF.Copy,
                    accum_out=st_sum[:, m:m + 1]), acC)
                P.op("scalar", lambda e, m=m, buf=buf: e.activation(
                    xcur16[:], buf[:], AF.Square,
                    accum_out=st_sq[:, m:m + 1]), acC)
                am_[s] = acC.v
            P.wait("vector", acC, acC.v)
            P.op("vector", lambda e: e.tensor_scalar(st_mu[:], st_sum[:], 1.0 / D, None, ALU.mult), veC)
            P.op("vector", lambda e: e.tensor_tensor(st_tmp[:], st_mu[:], st_mu[:], ALU.mult), veC)
            P.op("vector", lambda e: e.scalar_tensor_tensor(
                st_tmp[:], st_sq[:], 1.0 / D, st_tmp[:], ALU.mult, ALU.subtract), veC)
            v_var = P.op("vector", lambda e: e.tensor_scalar(
                st_tmp[:], st_tmp[:], 1e-5, None, ALU.add), veC)
            P.wait("scalar", veC, v_var)
            a_sq = P.op("scalar", lambda e: e.activation(st_tmp[:], st_tmp[:], AF.Sqrt), acC)
            P.wait("vector", acC, a_sq)
            v_r = P.op("vector", lambda e: e.reciprocal(st_r[:], st_tmp[:]), veC)
            P.op("gpsimd", lambda e: e.memset(st_mush[0:1, :], 0.0), poC)
            P.op("gpsimd", lambda e: e.memset(st_rsh[0:1, :], 1.0), poC)
            P.wait("gpsimd", veC, v_r)
            P.dma("gpsimd", st_mush[1:128, :], st_mu[0:127, :], ldG)
            P.dma("gpsimd", st_rsh[1:128, :], st_r[0:127, :], ldG)
            P.dma("gpsimd", st_mush[0:1, 1:16], st_mu[127:128, 0:15], ldG)
            P.dma("gpsimd", st_rsh[0:1, 1:16], st_r[127:128, 0:15], ldG)
            ld_sh = ldG.v
            st_done = (veC, v_r)

            tap("st_mu", st_mu[:], [st_done])
            tap("st_r", st_r[:], [st_done])

            _stage_gate(0.2)
            P.wait("tensor", veC, v_r)
            P.wait("tensor", ldS)
            m_t1 = P.op("tensor", lambda e: e.transpose(pb[0][0:16, 0:128], st_mu[:], c_eyef[:]), mmC)
            m_t2 = P.op("tensor", lambda e: e.transpose(pb[0][0:16, 128:256], st_r[:], c_eyef[:]), mmC)
            P.wait("vector", mmC, m_t2)
            v_rw = P.op("vector", lambda e: e.tensor_copy(rtmp[0:16, 0:128], pb[0][0:16, 0:128]), veC)
            v_rw = P.op("vector", lambda e: e.tensor_copy(rtmp[0:16, 128:256], pb[0][0:16, 128:256]), veC)
            P.wait("gpsimd", veC, v_rw)
            P.dma("gpsimd", rowscr[40:41, :].rearrange("o (m t) -> (o m) t", m=16), rtmp[0:16, 0:128], ldG)
            g_rw = P.dma("gpsimd", rowscr[41:42, :].rearrange("o (m t) -> (o m) t", m=16), rtmp[0:16, 128:256], ldG)
            P.wait("sync", ldG)
            s_bc1 = P.dma("sync", mrbc[:, 0:T], rowscr[40:41, :].squeeze(0).partition_broadcast(128), ldS)
            s_bc2 = P.dma("sync", mrbc[:, T:2 * T], rowscr[41:42, :].squeeze(0).partition_broadcast(128), ldS)
            P.wait("vector", ldS)
            for k in range(DC):
                P.op("vector", lambda e, k=k: e.tensor_tensor(
                    xtmp[:], xd[:, T * k:T * (k + 1)], mrbc[:, 0:T], ALU.subtract), veC)
                P.op("vector", lambda e, k=k: e.tensor_tensor(
                    xl1d[:, (T + 1) * k + 1:(T + 1) * (k + 1)], xtmp[:], mrbc[:, T:2 * T],
                    ALU.mult), veC)
            P.op("vector", lambda e: e.memset(
                xl1d[:, 0:DC * (T + 1)].rearrange("p (c t) -> p c t", c=DC)[:, :, 0:1], 0.0), veC)
            v_xl1 = veC.v
            tap("xl1d", xl1d[:, 0:T + 1], [(veC, v_xl1)])

            _stage_gate(0.3)
            P.wait("tensor", ldG)
            P.wait("tensor", veC, v_xl1)
            ba_ev = []
            for m in range(NT):
                if m >= 2:
                    P.wait("tensor", veC, ba_ev[m - 2])
                for k in range(DC):
                    P.op("tensor", lambda e, m=m, k=k: e.matmul(
                        pb[m % 2][0:128, 0:16],
                        xl1d[:, (T + 1) * k + 1 + 128 * m:(T + 1) * k + 1 + 128 * (m + 1)],
                        wba16[:, 16 * k:16 * (k + 1)],
                        start=(k == 0), stop=(k == DC - 1)),
                        mmC if k == DC - 1 else None)
                P.wait("vector", mmC, mmC.v)
                P.op("vector", lambda e, m=m: e.tensor_copy(
                    ba[:, 16 * m:16 * (m + 1)], pb[m % 2][0:128, 0:16]), veC)
                ba_ev.append(veC.v)
                if m == 0:
                    _stage_gate(0.32)
                if m == 1:
                    _stage_gate(0.33)
                if m == 2:
                    _stage_gate(0.34)
                if m == 3:
                    _stage_gate(0.36)
            v_ba = veC.v
            tap("ba", ba[:], [(veC, v_ba)])

            _stage_gate(0.4)
            bav = ba[:].rearrange("p (m k) -> p m k", k=16)
            braw = bav[:, :, 0:8]
            araw = bav[:, :, 8:16]
            d3 = lambda t: t.rearrange("p (m h) -> p m h", h=8)
            P.wait("scalar", veC, v_ba)
            a_beta = P.op("scalar", lambda e: e.activation(d3(sc_beta[:]), braw, AF.Sigmoid), acC)
            P.op("vector", lambda e: e.tensor_scalar(d3(sptmp[:]), braw, -1.0, None, ALU.mult), veC)
            v_ab = P.op("vector", lambda e: e.tensor_tensor(d3(sptmp[:]), braw, d3(sptmp[:]), ALU.min), veC)
            P.wait("scalar", veC, v_ab)
            a_e = P.op("scalar", lambda e: e.activation(sptmp[:], sptmp[:], AF.Exp), acC)
            a_l = P.op("scalar", lambda e: e.activation(sptmp[:], sptmp[:], AF.Ln, bias=1.0), acC)
            P.wait("vector", acC, a_l)
            P.op("vector", lambda e: e.tensor_scalar(d3(sc_lnb[:]), braw, 0.0, None, ALU.min), veC)
            v_lnb = P.op("vector", lambda e: e.tensor_tensor(
                sc_lnb[:], sc_lnb[:], sptmp[:], ALU.subtract), veC)
            P.wait("vector", acC, a_beta)
            dview = dtbbc[:].unsqueeze(1).broadcast_to([128, NT, NH])
            P.op("vector", lambda e: e.tensor_tensor(d3(sptmp2[:]), araw, dview, ALU.add), veC)
            P.op("vector", lambda e: e.tensor_scalar(sptmp[:], sptmp2[:], -1.0, None, ALU.mult), veC)
            v_z = P.op("vector", lambda e: e.tensor_tensor(sptmp[:], sptmp2[:], sptmp[:], ALU.min), veC)
            P.wait("scalar", veC, v_z)
            a_e2 = P.op("scalar", lambda e: e.activation(sptmp[:], sptmp[:], AF.Exp), acC)
            a_l2 = P.op("scalar", lambda e: e.activation(sptmp[:], sptmp[:], AF.Ln, bias=1.0), acC)
            P.wait("scalar", ldG)
            a_ax = P.op("scalar", lambda e: e.activation(alogbc[:], alogbc[:], AF.Exp), acC)
            P.wait("vector", acC, a_l2)
            P.op("vector", lambda e: e.tensor_scalar(sptmp2[:], sptmp2[:], 0.0, None, ALU.max), veC)
            P.op("vector", lambda e: e.tensor_tensor(sptmp2[:], sptmp2[:], sptmp[:], ALU.add), veC)
            P.wait("vector", acC)
            aview = alogbc[:].unsqueeze(1).broadcast_to([128, NT, NH])
            v_ld = P.op("vector", lambda e: e.scalar_tensor_tensor(
                d3(sc_ld16[:]), d3(sptmp2[:]), -1.0, aview, ALU.mult, ALU.mult), veC)
            tap("ld16", sc_ld16[:], [(veC, v_ld)])

            P.wait("tensor", veC, v_ld)
            P.wait("tensor", ldG)
            for m in range(NT):
                P.op("tensor", lambda e, m=m: e.matmul(
                    pb[2][0:128, 8 * m:8 * m + 8], c_cumtri[:],
                    sc_ld16[:, 8 * m:8 * (m + 1)], start=True, stop=True), None)
                P.op("tensor", lambda e, m=m: e.matmul(
                    pb[3][0:128, 8 * m:8 * m + 8], c_blksum[:],
                    sc_ld16[:, 8 * m:8 * (m + 1)], start=True, stop=True), None)
            m_cum = P.op("tensor", lambda e: e.nop(), mmC)
            P.wait("vector", mmC, m_cum)
            P.op("vector", lambda e: e.tensor_copy(sc_cum[:], pb[2][0:128, 0:128]), veC)
            v_cum = P.op("vector", lambda e: e.tensor_copy(sc_ccum[:], pb[3][0:128, 0:128]), veC)
            tap("cum", sc_cum[:], [(veC, v_cum)])

            am_ = [0, 0]
            tsm_ = [0, 0]
            for m in range(NT):
                s = m % 2
                buf = xt16 if s == 0 else xsh16
                cur = xcur16 if s == 0 else o16
                P.wait("gpsimd", veC, tsm_[s])
                lv = P.dma("gpsimd", buf[:], x_tok[128 * m:128 * (m + 1), :], ldG)
                P.wait("vector", ldG, lv)
                P.wait("vector", veC, v_cum)
                P.wait("vector", acC, am_[s])
                vv = P.op("vector", lambda e, m=m, buf=buf, cur=cur: e.tensor_scalar(
                    cur[:, 0:1024], buf[:], st_mu[:, m:m + 1], st_r[:, m:m + 1],
                    ALU.subtract, ALU.mult), veC)
                tsm_[s] = vv
                P.wait("scalar", veC, vv)
                for h in range(NH):
                    P.op("scalar", lambda e, m=m, h=h, cur=cur: e.activation(
                        sptmp2[:, 0:128], cur[:, 128 * h:128 * (h + 1)],
                        AF.Square, accum_out=sc_ss[:, 8 * m + h:8 * m + h + 1]), acC)
                am_[s] = acC.v
            a_ss = acC.v
            P.wait("vector", acC, a_ss)
            v_c = P.op("vector", lambda e: e.tensor_scalar(
                sc_lnn2[:], sc_ss[:], 1e-19, None, ALU.max), veC)
            P.wait("scalar", veC, v_c)
            a_ln = P.op("scalar", lambda e: e.activation(sc_lnn2[:], sc_lnn2[:], AF.Ln), acC)
            P.wait("gpsimd", acC, a_ln)
            P.op("gpsimd", lambda e: e.memset(sc_lnn2sh[0:1, 0:8], 0.0), poC)
            P.dma("gpsimd", sc_lnn2sh[1:128, :], sc_lnn2[0:127, :], ldG)
            g_sh = P.dma("gpsimd", sc_lnn2sh[0:1, 8:128], sc_lnn2[127:128, 0:120], ldG)

            P.wait("vector", ldG)
            P.wait("vector", veC, v_lnb)
            P.op("vector", lambda e: e.tensor_tensor(sc_p1[:], sc_cum[:], sc_lnb[:], ALU.add), veC)
            P.op("vector", lambda e: e.scalar_tensor_tensor(
                sc_p1[:], sc_lnn2sh[:], -0.5, sc_p1[:], ALU.mult, ALU.add), veC)
            P.op("vector", lambda e: e.scalar_tensor_tensor(
                sc_p2[:], sc_lnn2sh[:], 0.5, sc_cum[:], ALU.mult, ALU.add), veC)
            P.op("vector", lambda e: e.tensor_scalar(sc_p2[:], sc_p2[:], -1.0, None, ALU.mult), veC)
            P.op("vector", lambda e: e.scalar_tensor_tensor(
                sptmp[:], sc_lnn2[:], -0.5, sc_cum[:], ALU.mult, ALU.add), veC)
            P.op("vector", lambda e: e.tensor_tensor(sptmp2[:], sc_ccum[:], sc_cum[:], ALU.subtract), veC)
            P.op("vector", lambda e: e.scalar_tensor_tensor(
                sc_usc[:], sc_lnn2sh[:], -0.5, sptmp2[:], ALU.mult, ALU.add), veC)
            v_combo = veC.v
            P.wait("scalar", veC, v_combo)
            a_tg = P.op("scalar", lambda e: e.activation(sc_tg[:], sc_p1[:], AF.Exp, scale=1.0), acC)
            a_us = P.op("scalar", lambda e: e.activation(sc_usc[:], sc_usc[:], AF.Exp), acC)
            P.wait("vector", acC, a_tg)
            v_tg = P.op("vector", lambda e: e.tensor_scalar(sc_tg[:], sc_tg[:], -1.0, None, ALU.mult), veC)

            sckv = lambda k: sck[:].rearrange("p (m k h) -> p m k h", m=NT, k=6, h=8)[:, :, k:k + 1, :]
            scv = lambda t: t.rearrange("p (m h) -> p m h", m=NT).unsqueeze(2)
            P.op("vector", lambda e: e.tensor_copy(sckv(0), scv(sc_p1[:])), veC)
            P.op("vector", lambda e: e.tensor_copy(sckv(1), scv(sc_p2[:])), veC)
            P.op("vector", lambda e: e.tensor_copy(sckv(2), scv(sptmp[:])), veC)
            v_sck1 = veC.v
            P.wait("scalar", veC, v_sck1)
            a_rk4 = P.op("scalar", lambda e: e.activation(sckv(3), scv(sptmp[:]), AF.Exp), acC)
            a_ced = P.op("scalar", lambda e: e.activation(sckv(4), scv(sc_ccum[:]), AF.Exp), acC)
            P.wait("tensor", acC, a_ced)
            P.wait("tensor", veC, v_sck1)
            for m in range(NT):
                P.op("tensor", lambda e, m=m: e.transpose(
                    pb[2][0:40, 128 * (m % 4):128 * (m % 4) + 128],
                    sck[:, 48 * m:48 * m + 40], c_eyef[:]), mmC)
                if m % 4 == 3:
                    mv = mmC.v
                    P.wait("vector", mmC, mv)
                    for mm2 in range(m - 3, m + 1):
                        P.op("vector", lambda e, mm2=mm2: e.tensor_copy(
                            rows[0:40, 128 * mm2:128 * (mm2 + 1)],
                            pb[2][0:40, 128 * (mm2 % 4):128 * (mm2 % 4) + 128]), veC)
                    P.wait("tensor", veC, veC.v)
            v_rows = veC.v
            P.wait("gpsimd", veC, v_rows)
            g_rows = P.dma("gpsimd", rowscr[0:40, :], rows[0:40, :], ldG)
            P.dma("gpsimd", rows2.rearrange("m k (h c) -> m k h c", h=8),
                  rowscr[0:24, :].rearrange("(k h) (m c) -> m k h c", h=8, c=128), ldG)
            P.dma("gpsimd", rk42.rearrange("m (h c) -> m h c", h=8),
                  rowscr[24:32, :].rearrange("h (m c) -> m h c", c=128), ldG)
            for cc2 in range(2):
                g_rows2 = P.dma(
                    "gpsimd",
                    ced2.rearrange("m (h c) -> m h c", h=8)[:, :, cc2:cc2 + 1],
                    rowscr[32:40, 63 + 64 * cc2::128].rearrange("h m -> m h").unsqueeze(2),
                    ldG, allow_slow_non_contiguous=True)
            tap("rows", rows[:, 0:256], [(veC, v_rows)])


            _stage_gate(0.8)
            P.wait("tensor", ldS)
            P.op("vector", lambda e: e.memset(s16[:], 0.0), veC)
            P.wait("vector", acC)

            TP1 = T + 1
            hview = lambda t, w=128: t.rearrange("p (h t2) -> p h t2", t2=w)
            xlv = xl1d[:, 0:DC * TP1].rearrange("p (h t2) -> p h t2", t2=TP1)

            def scvw(t, c0, n, rows_=128, p0=0, w=128):
                return t[p0:p0 + rows_, c0:c0 + n].unsqueeze(2).broadcast_to([rows_, n, w])

            kview = lambda cst: cst[:].unsqueeze(1).broadcast_to([128, 4, 128])
            g4 = lambda t, base: t[:, base:base + 512].rearrange("p (a b) -> p a b", a=4)
            pbv = lambda i: pb[i][0:128, 0:512].rearrange("p (a b) -> p a b", a=4)
            exv = lambda oi, hg: expw[:, 1024 * oi + 512 * hg:1024 * oi + 512 * hg + 512].rearrange(
                "p (a b) -> p a b", a=4)

            x2_done = []
            fbm = {}
            abm = {}
            dblm = {}
            vgtm = {}
            veqm = {}
            ssttm = {}
            supdm = {}
            intram = {}
            oevm = {}
            vqum = {}
            rkdu = {}
            st8 = {}
            for mi in range(NT + 1):
                mp = mi
                msc = mi - 1
                if mp < NT:
                    co = 8 * mp
                    ml_p = mp % 2
                    sl = slice(128 * mp, 128 * mp + 128)
                    P.wait("sync", veC, abm.get(mp - 1, 0))
                    P.wait("sync", mmC, dblm.get(mp - 1, 0))
                    P.wait("sync", veC, vqum.get(mp - 2, 0))
                    P.wait("sync", veC, ssttm.get(mp - 2, 0))
                    P.wait("sync", ldG, g_rows2)
                    P.dma("sync", rk4bcx[ml_p][:],
                          rk42[mp:mp + 1, :].squeeze(0).partition_broadcast(128),
                          ldR, serial=False)
                    P.dma("sync", cedbcx[ml_p][:],
                          ced2[mp:mp + 1, :].squeeze(0).partition_broadcast(128),
                          ldR, serial=False)
                    P.dma("sync", fbc[:, 0:3072],
                          rows2[mp].rearrange("k f -> (k f)").partition_broadcast(128),
                          ldR, serial=False)
                    fbm[mp] = ldR.v

                    P.wait("tensor", veC)
                    for half in (0, 1):
                        for k in range(DC):
                            P.op("tensor", lambda e, mp=mp, k=k, half=half: e.matmul(
                                pb[half][:, 0:512],
                                xl1d[:, TP1 * k + 1 + 128 * mp:TP1 * k + 1 + 128 * (mp + 1)],
                                wwb[:, 1024 * k + 512 * half:1024 * k + 512 * half + 512],
                                start=(k == 0), stop=(k == DC - 1)),
                                mmC if k == DC - 1 else None)
                    P.wait("vector", mmC)
                    for half in (0, 1):
                        P.op("vector", lambda e, half=half: e.tensor_copy(
                            v16[:, 512 * half:512 * half + 512], pb[half][:, 0:512]), veC)

                    PKI = (1, 0, 1)
                    P.wait("vector", ldR, fbm[mp])
                    P.wait("scalar", mmC, dblm.get(mp - 1, 0))
                    a_exp = acC.v
                    ab3 = lambda t: t.rearrange("p (h t2) -> p h t2", t2=128)
                    for oi in range(3):
                        kc = c_kaddL if oi == 1 else c_kaddU
                        if oi > 0:
                            P.wait("vector", acC, a_exp)
                        P.op("vector", lambda e, oi=oi, mp=mp, ab3=ab3: e.tensor_tensor(
                            ab3(argbuf[:]),
                            ab3(fbc[:, 1024 * oi:1024 * oi + 1024]),
                            sck[:, 48 * mp + 8 * PKI[oi]:48 * mp + 8 * PKI[oi] + 8]
                            .unsqueeze(2).broadcast_to([128, 8, 128]), ALU.add), veC)
                        vv = P.op("vector", lambda e, kc=kc, ab3=ab3: e.tensor_tensor(
                            ab3(argbuf[:]), ab3(argbuf[:]),
                            kc[:].unsqueeze(1).broadcast_to([128, 8, 128]), ALU.add), veC)
                        P.wait("scalar", veC, vv)
                        a_exp = P.op("scalar", lambda e, oi=oi: e.activation(
                            expw[:, 1024 * oi:1024 * oi + 1024], argbuf[:], AF.Exp), acC)

                    P.wait("tensor", veC)
                    for hh in range(4):
                        for hg in (0, 1):
                            h = 4 * hg + hh
                            hb = TP1 * h
                            wk_sl = xl1d[:, hb + 128 * mp:hb + 128 * mp + 128]
                            rk_sl = xl1d[:, hb + 128 * mp + 1:hb + 128 * mp + 129]
                            bU = 2 - 2 * hg
                            P.op("tensor", lambda e, wk_sl=wk_sl, hh=hh, bU=bU: e.matmul(
                                pb[bU][0:128, 128 * hh:128 * hh + 128], wk_sl, wk_sl,
                                start=True, stop=True), mmC)
                            P.op("tensor", lambda e, wk_sl=wk_sl, rk_sl=rk_sl, hh=hh, bU=bU: e.matmul(
                                pb[bU + 1][0:128, 128 * hh:128 * hh + 128], wk_sl, rk_sl,
                                start=True, stop=True), mmC)
                    gm = mmC.v
                    P.wait("vector", mmC, gm)
                    P.wait("vector", acC, a_exp)
                    P.op("vector", lambda e: e.scalar_tensor_tensor(
                        g4(wkU, 0), pbv(2), -1.0, exv(0, 0), ALU.mult, ALU.mult), veC)
                    P.op("vector", lambda e: e.scalar_tensor_tensor(
                        g4(wkL, 0), pbv(2), -1.0, exv(1, 0), ALU.mult, ALU.mult), veC)
                    P.op("vector", lambda e: e.scalar_tensor_tensor(
                        g4(wkU2, 0), pbv(0), -1.0, exv(0, 1), ALU.mult, ALU.mult), veC)
                    P.op("vector", lambda e: e.scalar_tensor_tensor(
                        g4(wkL2, 0), pbv(0), -1.0, exv(1, 1), ALU.mult, ALU.mult), veC)
                    P.op("vector", lambda e: e.tensor_tensor(
                        g4(wkQ, 0), g4(wkU, 0), kview(c_eye16), ALU.add), veC)
                    P.op("vector", lambda e: e.tensor_tensor(
                        g4(wkQ2, 0), g4(wkU2, 0), kview(c_eye16), ALU.add), veC)
                    P.op("vector", lambda e, ml_p=ml_p: e.scalar_tensor_tensor(
                        g4(wkITx[ml_p], 0), pbv(3), 1.0, exv(2, 0), ALU.mult, ALU.mult), veC)
                    qs = [veC.v, veC.v]
                    P.op("vector", lambda e, ml_p=ml_p: e.scalar_tensor_tensor(
                        g4(wkITx[ml_p], 512), pbv(1), 1.0, exv(2, 1), ALU.mult, ALU.mult), veC)
                    qs[1] = veC.v
                    st8[mp] = {"qs": qs, "lvs": [0, 0], "uvs": [qs[1], qs[1]]}

                if msc >= 0:
                    cox = 8 * msc
                    ms2 = msc % 2
                    _scan_chunk = None

                def scan_chunk(msc, c, prev_sstt=0, prev_mgts=0):
                    cs = 64 * c
                    cox = 8 * msc
                    ms2 = msc % 2
                    if c == 0:
                        P.wait("vector", ldR, fbm[msc])
                        P.wait("vector", mmC, rkdu.get(msc - 1, 0))
                    else:
                        P.wait("vector", mmC, prev_mgts)
                    P.op("vector", lambda e, msc=msc, cs=cs, ms2=ms2: e.tensor_tensor(
                        rkds[:].rearrange("p (h t2) -> p h t2", t2=64),
                        xlv[:, :, 128 * msc + 1 + cs:128 * msc + 1 + cs + 64],
                        rk4bcx[ms2][:].rearrange("p (h t2) -> p h t2", t2=128)[:, :, cs:cs + 64],
                        ALU.mult), veC)
                    v_rk = veC.v
                    if c == 0:
                        P.wait("tensor", veC, ssttm.get(msc - 1, 0))
                        P.wait("tensor", veC, vgtm[msc])
                        P.wait("tensor", veC, oevm.get(msc - 1, 0))
                        P.wait("tensor", veC, vqum.get(msc - 1, 0))
                    else:
                        P.wait("tensor", veC, prev_sstt)
                    for h in range(NH):
                        hh = h % 4
                        P.op("tensor", lambda e, h=h, hh=hh, cs=cs: e.matmul(
                            pb[4 + h // 4][cs:cs + 64, 128 * hh:128 * hh + 128],
                            wkGT[:, 128 * h + cs:128 * h + cs + 64],
                            s16[:, 128 * h:128 * h + 128],
                            start=True, stop=True), mmC)
                    P.wait("tensor", veC, v_rk)
                    for h in range(NH):
                        hh = h % 4
                        P.op("tensor", lambda e, h=h, hh=hh, cs=cs: e.matmul(
                            pb[6 + h // 4][0:128, 128 * hh + cs:128 * hh + cs + 64],
                            s16[:, 128 * h:128 * h + 128],
                            rkds[:, 64 * h:64 * h + 64],
                            start=(cs == 0 and hh == 0), stop=False,
                            skip_group_check=True), mmC)
                    mgts = mmC.v
                    if c == 1:
                        rkdu[msc] = mgts
                    P.wait("vector", mmC, mgts)
                    for b in (0, 1):
                        P.op("vector", lambda e, b=b, cs=cs: e.tensor_tensor(
                            vq[cs:cs + 64, 512 * b:512 * b + 512],
                            eq16[cs:cs + 64, 512 * b:512 * b + 512],
                            pb[4 + b][cs:cs + 64, 0:512], ALU.add), veC)
                        P.op("vector", lambda e, b=b, cs=cs, cox=cox: e.tensor_tensor(
                            vqu[cs:cs + 64, 512 * b:512 * b + 512].rearrange(
                                "p (a b2) -> p a b2", a=4),
                            vq[cs:cs + 64, 512 * b:512 * b + 512].rearrange(
                                "p (a b2) -> p a b2", a=4),
                            scvw(sc_usc, cox + 4 * b, 4, rows_=64, p0=cs), ALU.mult), veC)
                    v_vq = veC.v
                    if c == 1:
                        vqum[msc] = v_vq
                    P.wait("tensor", veC, v_vq)
                    for h in range(NH):
                        hh = h % 4
                        P.op("tensor", lambda e, h=h, hh=hh, cs=cs, ms2=ms2: e.matmul(
                            pb[4 + h // 4][0:128, 128 * hh:128 * hh + 128],
                            xsh16[cs:cs + 64, 128 * h:128 * h + 128],
                            vqu[cs:cs + 64, 512 * (h // 4) + 128 * hh:512 * (h // 4) + 128 * hh + 128],
                            start=True, stop=True), mmC)
                    msup = mmC.v
                    if c == 1:
                        supdm[msc] = msup
                    P.wait("vector", mmC, msup)
                    for h in range(NH):
                        hh = h % 4
                        P.op("vector", lambda e, h=h, hh=hh, c=c, ms2=ms2: e.scalar_tensor_tensor(
                            s16[:, 128 * h:128 * h + 128], s16[:, 128 * h:128 * h + 128],
                            cedbcx[ms2][:, 2 * h + c:2 * h + c + 1],
                            pb[4 + h // 4][0:128, 128 * hh:128 * hh + 128],
                            ALU.mult, ALU.add), veC)
                    if c == 1:
                        ssttm[msc] = veC.v
                    return veC.v, mgts

                sc0 = {}
                if msc >= 0:
                    sc0[msc] = scan_chunk(msc, 0)

                if mp < NT:
                    st = st8[mp]
                    HGS = ((wkU, wkL, wkQ, 2, 3), (wkU2, wkL2, wkQ2, 0, 1))

                def dbl_levels(mp, l0, l1):
                    st = st8[mp]
                    qs, lvs, uvs = st["qs"], st["lvs"], st["uvs"]
                    HGS = ((wkU, wkL, wkQ, 2, 3), (wkU2, wkL2, wkQ2, 0, 1))
                    for lvl in range(l0, l1):
                        mul = [0, 0]
                        for hg in (0, 1):
                            U, L, Q, bU, bL = HGS[hg]
                            P.wait("tensor", veC, uvs[hg])
                            P.wait("tensor", veC, qs[hg])
                            P.wait("tensor", acC, lvs[hg])
                            if lvl < 5:
                                for hh in range(4):
                                    P.op("tensor", lambda e, hh=hh, U=U, L=L, bU=bU: e.matmul(
                                        pb[bU][0:128, 128 * hh:128 * hh + 128],
                                        L[:, 128 * hh:128 * hh + 128],
                                        U[:, 128 * hh:128 * hh + 128],
                                        start=True, stop=True), mmC)
                            for hh in range(4):
                                P.op("tensor", lambda e, hh=hh, U=U, L=L, bL=bL: e.matmul(
                                    pb[bL][0:128, 128 * hh:128 * hh + 128],
                                    U[:, 128 * hh:128 * hh + 128],
                                    L[:, 128 * hh:128 * hh + 128],
                                    start=True, stop=True), mmC)
                            mul[hg] = mmC.v
                        for hg in (0, 1):
                            U, L, Q, bU, bL = HGS[hg]
                            if lvl < 5:
                                P.wait("vector", mmC, mul[hg])
                                P.op("vector", lambda e, U=U, bU=bU: e.tensor_copy(
                                    g4(U, 0), pbv(bU)), veC)
                                uvs[hg] = veC.v
                            P.wait("scalar", mmC, mul[hg])
                            P.op("scalar", lambda e, L=L, bL=bL: e.activation(
                                g4(L, 0), pbv(bL), AF.Copy), acC)
                            lvs[hg] = acC.v
                        for hg in (0, 1):
                            U, L, Q, bU, bL = HGS[hg]
                            P.wait("tensor", acC, lvs[hg])
                            P.wait("tensor", veC, uvs[hg])
                            P.wait("tensor", veC, qs[hg])
                            for hh in range(4):
                                P.op("tensor", lambda e, hh=hh, L=L, Q=Q, bU=bU: e.matmul(
                                    pb[bU][0:128, 128 * hh:128 * hh + 128],
                                    L[:, 128 * hh:128 * hh + 128],
                                    Q[:, 128 * hh:128 * hh + 128],
                                    start=True, stop=True), mmC)
                            mq = mmC.v
                            P.wait("vector", mmC, mq)
                            P.op("vector", lambda e, Q=Q, bU=bU: e.scalar_tensor_tensor(
                                g4(Q, 0), pbv(bU), 1.0, g4(Q, 0), ALU.mult, ALU.add), veC)
                            qs[hg] = veC.v

                if mp < NT:
                    dbl_levels(mp, 1, 4)
                if msc >= 0:
                    scan_chunk(msc, 1, prev_sstt=sc0[msc][0], prev_mgts=sc0[msc][1])
                if mp < NT:
                    dbl_levels(mp, 4, 6)
                    st = st8.pop(mp)
                    qs = st["qs"]
                    co = 8 * mp
                    ml_p = mp % 2
                    P.wait("gpsimd", mmC, supdm.get(mp - 1, 0))
                    if mp == 0:
                        P.wait("vector", ldG)
                        P.op("vector", lambda e: e.memset(xshx16[0:1, :], 0.0), veC)
                        P.wait("gpsimd", veC)
                        ldX_t = P.dma("gpsimd", xshx16[1:128, :], x_tok[0:127, :], ldG)
                    else:
                        ldX_t = P.dma("gpsimd", xshx16[:],
                                      x_tok[128 * mp - 1:128 * mp + 127, :], ldG)
                    P.wait("vector", ldG, ldX_t)
                    P.wait("vector", mmC, supdm.get(mp - 1, 0))
                    P.op("vector", lambda e, mp=mp: e.tensor_scalar(
                        xsh16[:], xshx16[:], st_mush[:, mp:mp + 1], st_rsh[:, mp:mp + 1],
                        ALU.subtract, ALU.mult), veC)
                    v_xsh = veC.v
                    m_eg = 0
                    for hg in (0, 1):
                        hs = 4 * hg
                        wkQx = (wkQ, wkQ2)[hg]
                        bE = 2 - 2 * hg
                        if hg == 1:
                            P.wait("vector", mmC, m_eg)
                        P.op("vector", lambda e, hs=hs, co=co, wkQx=wkQx: e.tensor_tensor(
                            g4(wkAb, 0), g4(wkQx, 0), scvw(sc_beta, co + hs, 4), ALU.mult), veC)
                        P.op("vector", lambda e, hs=hs, co=co, wkQx=wkQx: e.tensor_tensor(
                            g4(wkAn, 0), g4(wkQx, 0), scvw(sc_tg, co + hs, 4), ALU.mult), veC)
                        vab = veC.v
                        P.wait("tensor", veC, max(vab, v_xsh))
                        for hh in range(4):
                            h = hs + hh
                            P.op("tensor", lambda e, h=h, hh=hh, bE=bE: e.matmul(
                                pb[bE][0:128, 128 * hh:128 * hh + 128],
                                wkAb[:, 128 * hh:128 * hh + 128],
                                v16[:, 128 * h:128 * h + 128],
                                start=True, stop=True), mmC)
                            P.op("tensor", lambda e, h=h, hh=hh, bE=bE: e.matmul(
                                pb[bE + 1][0:128, 128 * hh:128 * hh + 128],
                                xsh16[:, 128 * h:128 * h + 128],
                                wkAn[:, 128 * hh:128 * hh + 128],
                                start=True, stop=True), mmC)
                        m_eg = mmC.v
                        P.wait("vector", mmC, m_eg)
                        P.op("vector", lambda e, hg=hg, bE=bE: e.tensor_copy(
                            eq16[:, 512 * hg:512 * hg + 512].rearrange(
                                "p (a b) -> p a b", a=4), pbv(bE)), veC)
                        veqm[mp] = veC.v
                        P.op("vector", lambda e, hg=hg, bE=bE: e.tensor_copy(
                            wkGT[:, 512 * hg:512 * hg + 512].rearrange(
                                "p (a b) -> p a b", a=4), pbv(bE + 1)), veC)
                        vgtm[mp] = veC.v
                    abm[mp] = veC.v
                    dblm[mp] = mmC.v

                if msc >= 0:
                    cox = 8 * msc
                    ms2 = msc % 2
                    g = msc // 4
                    ml = msc % 4
                    P.wait("tensor", veC, vqum[msc])
                    for h in range(NH):
                        hh = h % 4
                        P.op("tensor", lambda e, h=h, hh=hh, ms2=ms2: e.matmul(
                            pb[6 + h // 4][0:128, 128 * hh:128 * hh + 128],
                            vq[:, 512 * (h // 4) + 128 * hh:512 * (h // 4) + 128 * hh + 128],
                            wkITx[ms2][:, 128 * h:128 * h + 128],
                            start=False, stop=True, skip_group_check=True), mmC)
                    intram[msc] = mmC.v
                    P.wait("vector", mmC, intram[msc])
                    o16v = o16[:].rearrange("p (h t2) -> p h t2", t2=512)
                    for b in (0, 1):
                        P.op("vector", lambda e, b=b, ml=ml: e.tensor_copy(
                            o16v[:, 4 * b:4 * b + 4, 128 * ml:128 * ml + 128],
                            pb[6 + b][0:128, 0:512].rearrange("p (a b2) -> p a b2", a=4)), veC)
                    oevm[msc] = veC.v
                    if ml == 3:
                        P.wait("tensor", veC)
                        for ec in range(DC):
                            pj = pb[ec % 2]
                            for k in range(DC):
                                P.op("tensor", lambda e, ec=ec, k=k, pj=pj: e.matmul(
                                    pj[:, 0:512],
                                    wrb[:, 1024 * k + 128 * ec:1024 * k + 128 * ec + 128],
                                    o16[:, 512 * k:512 * k + 512],
                                    start=(k == 0), stop=(k == DC - 1)),
                                    mmC if k == DC - 1 else None)
                            P.wait("vector", mmC)
                            P.op("vector", lambda e, ec=ec, g=g, pj=pj: e.tensor_tensor(
                                mstage[:, 0:512], pj[:, 0:512],
                                xd[:, 2048 * ec + 512 * g:2048 * ec + 512 * g + 512], ALU.add), veC)
                            P.op("vector", lambda e, ec=ec, g=g: e.tensor_tensor(
                                xd[:, 2048 * ec + 512 * g:2048 * ec + 512 * g + 512],
                                mstage[:, 0:512],
                                xl1d[:, TP1 * ec + 1 + 512 * g:TP1 * ec + 1 + 512 * g + 512],
                                ALU.add), veC)
                            if ec >= 1:
                                P.wait("tensor", veC)
                        x2_done.append(veC.v)
                        P.wait("gpsimd", veC, x2_done[-1])
                        for k in range(DC):
                            P.dma("gpsimd",
                                  sp_x2[128 * k:128 * k + 128, 512 * g:512 * g + 512],
                                  xd[:, 2048 * k + 512 * g:2048 * k + 512 * g + 512],
                                  spS, serial=False)

            tap("s16", s16[:], [(veC, veC.v)])
            tap("o16", o16[:], [(veC, veC.v)])
            tap("x2", xd[:, 0:T], [(veC, veC.v)])
            _stage_gate(1)

            _stage_gate(2)
            xl2q16 = ar.at_f16(16384, 16400)
            wgq16 = ar.at_f16(0, 24576)
            wuq16 = ar.at_f16(24584, 24576)
            deq_wld = ar.at_f32(12288, 512)
            deq_wld2 = ar.at_f32(12800, 512)
            deq_rint = ar.at_i32(13312, 512)
            sstgA = ar.at_f32(13824, 512, part=8)
            sstgB = ar.at_f32(14336, 512, part=8)
            ev16 = ar.at_f16(14848, 2048)
            sTs = ar.at_f32(12288, 3072)
            sTr = ar.at_f32(15360, 512)
            r2mu = ar.at_f32(UN0, 2048, part=1)
            r2sq = ar.at_f32(UN0 + 2048, 2048, part=1)
            r2tmp = ar.at_f32(UN0 + 4096, 2048, part=1)
            h16t = ar.at_f16(0, 2048)
            rgt = ar.at_i32(1024, 2048)
            tft = ar.at_f32(3072, 2048)
            ptt = ar.at_f32(5120, 2048)
            wdq16 = ar.at_f16(24584, 24576)
            of16 = ar.at_f16(0, 12288)
            qdt = ar.at_f32(6144, 6144)
            x2g = ar.at_f32(12288, 512)
            yt = ar.at_f32(12800, 512)

            P.wait("tensor", veC)
            P.wait("scalar", veC)
            for t in range(TG):
                for k in range(DC):
                    xsl = xd[:, 2048 * k + 512 * t:2048 * k + 512 * t + 512]
                    av = P.op("scalar", lambda e, xsl=xsl: e.activation(
                        mstage[:], xsl, AF.Square), acC)
                    P.op("tensor", lambda e, xsl=xsl, k=k: e.matmul(
                        pb[0][0:1, 0:512], c_onescol[:], xsl,
                        start=(k == 0), stop=(k == DC - 1)), mmC)
                    P.wait("tensor", acC, av)
                    mv = P.op("tensor", lambda e, k=k: e.matmul(
                        pb[1][0:1, 0:512], c_onescol[:], mstage[:],
                        start=(k == 0), stop=(k == DC - 1)), mmC)
                    P.wait("scalar", mmC, mv)
                P.wait("vector", mmC)
                P.op("vector", lambda e, t=t: e.tensor_copy(
                    r2mu[:, 512 * t:512 * t + 512], pb[0][0:1, 0:512]), veC)
                P.op("vector", lambda e, t=t: e.tensor_copy(
                    r2sq[:, 512 * t:512 * t + 512], pb[1][0:1, 0:512]), veC)
                P.wait("tensor", veC)
            P.op("vector", lambda e: e.tensor_scalar(
                r2mu[:], r2mu[:], 1.0 / D, None, ALU.mult), veC)
            P.op("vector", lambda e: e.tensor_tensor(r2tmp[:], r2mu[:], r2mu[:], ALU.mult), veC)
            P.op("vector", lambda e: e.scalar_tensor_tensor(
                r2sq[:], r2sq[:], 1.0 / D, r2tmp[:], ALU.mult, ALU.subtract), veC)
            vv = P.op("vector", lambda e: e.tensor_scalar(
                r2sq[:], r2sq[:], 1e-5, None, ALU.add), veC)
            P.wait("scalar", veC, vv)
            av = P.op("scalar", lambda e: e.activation(r2sq[:], r2sq[:], AF.Sqrt), acC)
            P.wait("vector", acC, av)
            P.op("vector", lambda e: e.reciprocal(r2sq[:], r2sq[:]), veC)
            P.wait("gpsimd", veC)
            P.dma("gpsimd", rowscr[42:43, :], r2mu[:], ldG)
            g_r2 = P.dma("gpsimd", rowscr[43:44, :], r2sq[:], ldG)
            P.wait("sync", ldG)
            P.wait("sync", veC)
            s_b1 = P.dma("sync", mrbc[:, 0:T], rowscr[42:43, :].squeeze(0).partition_broadcast(128), ldS)
            s_b2 = P.dma("sync", mrbc[:, T:2 * T], rowscr[43:44, :].squeeze(0).partition_broadcast(128), ldS)

            sp_x2_done = spS.v
            P.wait("vector", ldS)
            P.op("vector", lambda e: e.memset(mxc[:], 0.0), veC)
            P.op("vector", lambda e: e.memset(ccs[:], 0.0), veC)
            for k in range(DC):
                P.op("vector", lambda e, k=k: e.tensor_tensor(
                    xtmp[:], xd[:, 2048 * k:2048 * (k + 1)], mrbc[:, 0:T], ALU.subtract), veC)
                P.op("vector", lambda e: e.tensor_tensor(
                    xtmp[:], xtmp[:], mrbc[:, T:2 * T], ALU.mult), veC)
                P.op("vector", lambda e: e.tensor_reduce(
                    mxc[:, 1:2], xtmp[:], AX.X, ALU.max, apply_absolute_value=True), veC)
                P.op("vector", lambda e: e.tensor_tensor(
                    mxc[:, 0:1], mxc[:, 0:1], mxc[:, 1:2], ALU.max), veC)

            def cross_max(src_col, ar_i, slot=0):
                P.wait("tensor", veC)
                P.wait("tensor", acC)
                tv = P.op("tensor", lambda e, src_col=src_col: e.transpose(
                    pb[0][0:1, 0:128], src_col, c_eyef[:]), mmC)
                P.wait("vector", mmC, tv)
                P.op("vector", lambda e, slot=slot: e.tensor_reduce(
                    ccs[0:1, slot:slot + 1], pb[0][0:1, 0:128], AX.X, ALU.max), veC)

            def run_ar(ar_i, nslots=16):
                P.wait("gpsimd", veC)
                P.dma("gpsimd", cc_in[ar_i], ccs[:], ldG)
                P.wait("gpsimd", ldG)
                P.op("gpsimd", lambda e, ar_i=ar_i: e.collective_compute(
                    "AllReduce", ALU.max, replica_groups=[list(range(NCORES))],
                    ins=[cc_in[ar_i].opt()], outs=[cc_out[ar_i].opt()]), poC)
                P.wait("gpsimd", poC)
                P.dma("gpsimd", arbc[:], cc_out[ar_i].squeeze(0).partition_broadcast(128), ldG)
                return ldG.v

            cross_max(mxc[:, 0:1], 0, 0)
            arv = run_ar(0)
            P.wait("vector", ldG, arv)
            P.op("vector", lambda e: e.tensor_scalar(
                qsc[:, 0:1], arbc[:, 0:1], 1.0 / QMAX, 1e-12, ALU.mult, ALU.max), veC)
            P.op("vector", lambda e: e.reciprocal(qsc[:, 1:2], qsc[:, 0:1]), veC)

            for k in range(DC):
                P.op("vector", lambda e, k=k: e.tensor_tensor(
                    xtmp[:], xd[:, 2048 * k:2048 * (k + 1)], mrbc[:, 0:T], ALU.subtract), veC)
                P.op("vector", lambda e: e.tensor_tensor(
                    xtmp[:], xtmp[:], mrbc[:, T:2 * T], ALU.mult), veC)
                P.op("vector", lambda e: e.tensor_scalar(
                    xtmp[:].bitcast(I32), xtmp[:], qsc[:, 1:2], None, ALU.mult), veC)
                P.op("vector", lambda e, k=k: e.tensor_scalar(
                    xl2q16[:, 2048 * k:2048 * (k + 1)], xtmp[:].bitcast(I32),
                    qsc[:, 0:1], None, ALU.mult), veC)
            v_xl2q = veC.v
            tap("xl2q", xl2q16[:, 0:T], [(veC, v_xl2q)])

            swld = ar.at_f32(7680, 3072)
            sTr = ar.at_f32(0, 3072)
            sTr2 = ar.at_f32(10752, 1024)

            def emit_scales(nat, sTd, sTrd, ntile, ncols, sTrt=None):
                nblk = ncols // 32
                sTrt = sTr if sTrt is None else sTrt
                nslot = 2 if 2 * ncols <= 3072 else 1
                for i in range(4):
                    P.wait("vector", spl[i], spl_v[i])
                P.wait("sync", veC)
                P.wait("sync", spS)
                P.wait("sync", mmC)
                sw_u = [0] * nslot
                for it in range(ntile):
                    s = it % nslot
                    P.wait("sync", veC, sw_u[s])
                    lv = P.dma("sync", swld[:, ncols * s:ncols * s + ncols],
                               nat[128 * it:128 * it + 128, :], swl[s], serial=True)
                    P.wait("vector", swl[s], lv)
                    P.op("vector", lambda e, s=s, ncols=ncols, nblk=nblk: e.tensor_reduce(
                        mstage[:, 0:nblk],
                        swld[:, ncols * s:ncols * s + ncols].rearrange("p (b w) -> p b w", w=32),
                        AX.X, ALU.max, apply_absolute_value=True), veC)
                    sw_u[s] = veC.v
                    P.op("vector", lambda e, nblk=nblk: e.tensor_scalar(
                        mstage[:, 0:nblk], mstage[:, 0:nblk], 1.0 / WQ, 1e-12,
                        ALU.mult, ALU.max), veC)
                    P.wait("tensor", veC)
                    tv = P.op("tensor", lambda e, nblk=nblk: e.transpose(
                        pb[0][0:nblk, 0:128], mstage[:, 0:nblk], c_eyef[:]), mmC)
                    P.wait("vector", mmC, tv)
                    P.op("vector", lambda e, it=it, nblk=nblk: e.tensor_copy(
                        sTs[0:nblk, 128 * it:128 * it + 128], pb[0][0:nblk, 0:128]), veC)
                P.wait("tensor", veC)
                P.op("vector", lambda e, ntile=ntile, nblk=nblk, sTrt=sTrt: e.reciprocal(
                    sTrt[0:nblk, 0:ntile * 128], sTs[0:nblk, 0:ntile * 128]), veC)
                P.wait("gpsimd", veC)
                P.dma("gpsimd", sTd, sTs[0:nblk, 0:ntile * 128], ldG)
                v = P.dma("gpsimd", sTrd, sTrt[0:nblk, 0:ntile * 128], ldG)
                return v

            def emit_deq(wT, sTd, sTrd, dst16, nk, kcols, blk_per_128):
                P.wait("tensor", veC)
                P.wait("tensor", acC)
                P.wait("sync", ldG)
                wld_u = [0, 0]
                sMM_u = [0, 0]
                bank_u = [0, 0]
                i = 0
                for k in range(nk):
                    for pc in range(kcols // 512):
                        c0 = 512 * pc
                        s = i % 2
                        wld = deq_wld if s == 0 else deq_wld2
                        sst = sstgA if s == 0 else sstgB
                        bR = pb[2 + 2 * s]
                        bS = pb[3 + 2 * s]
                        P.wait("sync", veC, wld_u[s])
                        lw = P.dma("sync", wld[:, 0:512],
                                   wT[128 * k:128 * k + 128, c0:c0 + 512],
                                   wls[s], serial=True)
                        P.wait("sync", mmC, sMM_u[s])
                        row = blk_per_128 * k
                        P.dma("sync", sst[0:4, 0:512], sTrd[row:row + 4, c0:c0 + 512],
                              sls[s], serial=True)
                        ls = P.dma("sync", sst[4:8, 0:512], sTd[row:row + 4, c0:c0 + 512],
                                   sls[s], serial=False)
                        P.wait("tensor", sls[s], ls)
                        P.wait("tensor", veC, bank_u[s])
                        P.op("tensor", lambda e, bR=bR, sst=sst: e.matmul(
                            bR[:, 0:512], c_ind8[0:8, 0:128], sst[0:8, 0:512],
                            start=True, stop=True), mmC)
                        mmv = P.op("tensor", lambda e, bS=bS, sst=sst: e.matmul(
                            bS[:, 0:512], c_ind8[0:8, 128:256], sst[0:8, 0:512],
                            start=True, stop=True), mmC)
                        sMM_u[s] = mmv
                        P.wait("vector", wls[s], lw)
                        P.wait("vector", mmC, mmv)
                        P.op("vector", lambda e, wld=wld, bR=bR: e.tensor_tensor(
                            deq_rint[:, 0:512], wld[:, 0:512], bR[:, 0:512], ALU.mult), veC)
                        P.op("vector", lambda e, k=k, c0=c0, dst16=dst16, kcols=kcols, bS=bS: e.tensor_tensor(
                            dst16[:, kcols * k + c0:kcols * k + c0 + 512],
                            deq_rint[:, 0:512], bS[:, 0:512], ALU.mult), veC)
                        wld_u[s] = veC.v
                        bank_u[s] = veC.v
                        i += 1
                return veC.v

            spl_v = [0, 0, 0, 0]

            def emit_big_mms(w16, src16, spill, mx_slot, kcols):
                P.op("vector", lambda e, mx_slot=mx_slot: e.memset(mxc[:, mx_slot:mx_slot + 1], 0.0), veC)
                P.wait("tensor", veC)
                P.wait("tensor", acC)
                ev_v = {}
                ev_a = {}
                for t in range(TG):
                    for it in range(IT):
                        key = t * IT + it
                        pj = pb[key % 4]
                        sl = key % 4
                        if key >= 4:
                            P.wait("tensor", veC, ev_v[key - 4])
                            P.wait("tensor", acC, ev_a[key - 4])
                        for k in range(DC):
                            P.op("tensor", lambda e, pj=pj, k=k, it=it, t=t, w16=w16, kcols=kcols: e.matmul(
                                pj[:, 0:512],
                                w16[:, kcols * k + 128 * it:kcols * k + 128 * it + 128],
                                xl2q16[:, 2048 * k + 512 * t:2048 * k + 512 * t + 512],
                                start=(k == 0), stop=(k == DC - 1)),
                                mmC if k == DC - 1 else None)
                        mmv = mmC.v
                        P.wait("scalar", mmC, mmv)
                        P.wait("scalar", spl[sl], spl_v[sl])
                        P.op("scalar", lambda e, pj=pj, sl=sl: e.activation(
                            ev16[:, 512 * sl:512 * sl + 512], pj[:, 0:512], AF.Copy), acC)
                        ev_a[key] = acC.v
                        P.wait("vector", acC, ev_a[key])
                        P.op("vector", lambda e, pj=pj: e.tensor_reduce(
                            mxc[:, 6:7], pj[:, 0:512], AX.X, ALU.max,
                            apply_absolute_value=True), veC)
                        P.op("vector", lambda e, mx_slot=mx_slot: e.tensor_tensor(
                            mxc[:, mx_slot:mx_slot + 1], mxc[:, mx_slot:mx_slot + 1],
                            mxc[:, 6:7], ALU.max), veC)
                        ev_v[key] = veC.v
                        P.wait("sync", acC, ev_a[key])
                        spl_v[sl] = P.dma(
                            "sync", spill[128 * it:128 * it + 128, 512 * t:512 * t + 512],
                            ev16[:, 512 * sl:512 * sl + 512], spl[sl], serial=False)

            _stage_gate(3)
            P.wait("gpsimd", veC, v_xl2q)
            P.wait("gpsimd", spS, sp_x2_done)
            sv = emit_scales(wg_nat, sT_g, sTr_g, 24, D)
            P.wait("gpsimd", ldG, sv)
            emit_deq(wgT, sT_g, sTr_g, wgq16, DC, FI, 4)
            P.wait("tensor", veC)
            emit_big_mms(wgq16, xl2q16, sp_gate, 2, FI)
            cross_max(mxc[:, 2:3], 1, 0)
            arv = run_ar(1)

            _stage_gate(5)
            sv = emit_scales(wu_nat, sT_u, sTr_u, 24, D)
            P.wait("gpsimd", ldG, sv)
            emit_deq(wuT, sT_u, sTr_u, wuq16, DC, FI, 4)
            P.wait("vector", ldG, arv)
            P.op("vector", lambda e: e.tensor_scalar(
                qsc[:, 2:3], arbc[:, 0:1], 1.0 / QMAX, 1e-12, ALU.mult, ALU.max), veC)
            P.op("vector", lambda e: e.reciprocal(qsc[:, 3:4], qsc[:, 2:3]), veC)
            P.wait("tensor", veC)
            emit_big_mms(wuq16, xl2q16, sp_up, 4, FI)

            _stage_gate(5.5)
            sv = emit_scales(wd_nat, sT_d, sTr_d, DC, FI, sTrt=sTr2)
            P.wait("gpsimd", ldG, sv)
            emit_deq(wdT, sT_d, sTr_d, wdq16, IT, D, 4)

            _stage_gate(6)
            pttB = [ptt, ar.at_f32(16384, 2048)]
            rgtB = [rgt, ar.at_i32(18432, 2048)]
            tftB = [tft, ar.at_f32(20480, 2048)]
            h16B = [h16t, ar.at_f16(22528, 2048)]
            hRm = [0, 0]
            rgU = [0, 0]
            tfW = [0, 0]
            spPv = [0, 0]
            P.op("vector", lambda e: e.memset(mxc[:, 5:6], 0.0), veC)
            P.wait("scalar", veC)
            for i in range(4):
                P.wait("gpsimd", spl[i], spl_v[i])
                P.wait("sync", spl[i], spl_v[i])
            for it in range(IT):
                s = it % 2
                P.wait("sync", acC, hRm[s])
                lv = P.dma("sync", h16B[s][:, 0:T], sp_gate[128 * it:128 * it + 128, :],
                           wls[s], serial=True)
                P.wait("scalar", wls[s], lv)
                P.wait("scalar", veC, rgU[s])
                rv = P.op("scalar", lambda e, s=s: e.activation(
                    rgtB[s][:, 0:T], h16B[s][:, 0:T], AF.Copy, scale=qsc[:, 3:4]), acC)
                hRm[s] = rv
                P.wait("scalar", veC, tfW[s])
                av = P.op("scalar", lambda e, s=s: e.activation(
                    tftB[s][:, 0:T], rgtB[s][:, 0:T], AF.Sigmoid, scale=qsc[:, 2:3]), acC)
                P.wait("vector", acC, av)
                P.op("vector", lambda e, s=s: e.tensor_scalar(
                    tftB[s][:, 0:T].bitcast(I32), tftB[s][:, 0:T], 65536.0, 65535.0,
                    ALU.mult, ALU.min), veC)
                P.wait("vector", spP[s], spPv[s])
                P.op("vector", lambda e, s=s: e.tensor_tensor(
                    pttB[s][:, 0:T], rgtB[s][:, 0:T], tftB[s][:, 0:T].bitcast(I32),
                    ALU.mult), veC)
                rgU[s] = veC.v
                tfW[s] = veC.v
                P.op("vector", lambda e, s=s: e.tensor_reduce(
                    mxc[:, 6:7], pttB[s][:, 0:T], AX.X, ALU.max,
                    apply_absolute_value=True), veC)
                P.op("vector", lambda e: e.tensor_tensor(
                    mxc[:, 5:6], mxc[:, 5:6], mxc[:, 6:7], ALU.max), veC)
                P.wait("gpsimd", veC)
                spPv[s] = P.dma("gpsimd", sp_P[128 * it:128 * it + 128, :],
                                pttB[s][:, 0:T], spP[s], serial=False)
            cross_max(mxc[:, 4:5], 2, 0)
            cross_max(mxc[:, 5:6], 2, 1)
            arv = run_ar(2)
            P.wait("vector", ldG, arv)
            P.op("vector", lambda e: e.tensor_scalar(
                qsc[:, 4:5], arbc[:, 0:1], 1.0 / QMAX, 1e-12, ALU.mult, ALU.max), veC)
            P.op("vector", lambda e: e.reciprocal(qsc[:, 5:6], qsc[:, 4:5]), veC)
            P.op("vector", lambda e: e.tensor_scalar(
                mxc[:, 7:8], arbc[:, 1:2], qsc[:, 2:3], 1.0 / 65536.0, ALU.mult, ALU.mult), veC)
            P.op("vector", lambda e: e.tensor_scalar(
                qsc[:, 6:7], mxc[:, 7:8], 1.0 / QMAX, 1e-12, ALU.mult, ALU.max), veC)
            P.op("vector", lambda e: e.reciprocal(qsc[:, 7:8], qsc[:, 6:7]), veC)
            P.op("vector", lambda e: e.tensor_scalar(
                qsc[:, 7:8], qsc[:, 7:8], qsc[:, 2:3], 1.0 / 65536.0, ALU.mult, ALU.mult), veC)

            _stage_gate(7)
            P.op("vector", lambda e: e.memset(mxc[:, 5:6], 0.0), veC)
            P.wait("scalar", veC)
            for i in range(4):
                P.wait("gpsimd", spl[i], spl_v[i])
                P.wait("sync", spl[i], spl_v[i])
            for it in range(IT):
                s = it % 2
                P.wait("sync", acC, hRm[s])
                lv = P.dma("sync", h16B[s][:, 0:T], sp_up[128 * it:128 * it + 128, :],
                           wls[s], serial=True)
                P.wait("gpsimd", veC, tfW[s])
                lv2 = P.dma("gpsimd", tftB[s][:, 0:T], sp_P[128 * it:128 * it + 128, :],
                            ltf[s], serial=True)
                P.wait("scalar", wls[s], lv)
                P.wait("scalar", veC, rgU[s])
                rv = P.op("scalar", lambda e, s=s: e.activation(
                    rgtB[s][:, 0:T], h16B[s][:, 0:T], AF.Copy, scale=qsc[:, 5:6]), acC)
                hRm[s] = rv
                P.wait("scalar", ltf[s], lv2)
                av = P.op("scalar", lambda e, s=s: e.activation(
                    tftB[s][:, 0:T].bitcast(I32), tftB[s][:, 0:T], AF.Copy,
                    scale=qsc[:, 7:8]), acC)
                P.wait("vector", acC, av)
                P.wait("vector", spP[s], spPv[s])
                P.op("vector", lambda e, s=s: e.tensor_tensor(
                    pttB[s][:, 0:T], tftB[s][:, 0:T].bitcast(I32), rgtB[s][:, 0:T],
                    ALU.mult), veC)
                rgU[s] = veC.v
                tfW[s] = veC.v
                P.op("vector", lambda e, s=s: e.tensor_reduce(
                    mxc[:, 6:7], pttB[s][:, 0:T], AX.X, ALU.max,
                    apply_absolute_value=True), veC)
                P.op("vector", lambda e: e.tensor_tensor(
                    mxc[:, 5:6], mxc[:, 5:6], mxc[:, 6:7], ALU.max), veC)
                P.wait("gpsimd", veC)
                spPv[s] = P.dma("gpsimd", sp_Q[128 * it:128 * it + 128, :],
                                pttB[s][:, 0:T], spP[s], serial=False)
            cross_max(mxc[:, 5:6], 3, 0)
            arv = run_ar(3)
            P.wait("vector", ldG, arv)
            P.op("vector", lambda e: e.tensor_scalar(
                mxc[:, 7:8], arbc[:, 0:1], qsc[:, 6:7], None, ALU.mult), veC)
            P.op("vector", lambda e: e.tensor_tensor(
                mxc[:, 7:8], mxc[:, 7:8], qsc[:, 4:5], ALU.mult), veC)
            P.op("vector", lambda e: e.tensor_scalar(
                qsc[:, 8:9], mxc[:, 7:8], 1.0 / QMAX, 1e-12, ALU.mult, ALU.max), veC)
            P.op("vector", lambda e: e.reciprocal(qsc[:, 9:10], qsc[:, 8:9]), veC)
            P.op("vector", lambda e: e.tensor_scalar(
                qsc[:, 9:10], qsc[:, 9:10], qsc[:, 6:7], None, ALU.mult), veC)
            P.op("vector", lambda e: e.tensor_tensor(
                qsc[:, 9:10], qsc[:, 9:10], qsc[:, 4:5], ALU.mult), veC)

            _stage_gate(8)
            of16B = ar.at_f16(16384, 12288)
            x2gB = ar.at_f32(13312, 512)
            ytB = ar.at_f32(13824, 512)
            qld_v = [0, 0]
            x2u_v = [0, 0]
            sty_v = [0, 0]
            dn_v = {}
            dn_a = {}

            qdt_u = [0, 0]

            def build_of(t, dst):
                def one(it):
                    p = it % 2
                    P.wait("sync", veC, qdt_u[p])
                    lv = P.dma("sync", qdt[:, 1024 * p:1024 * p + 512],
                               sp_Q[128 * it:128 * it + 128, 512 * t:512 * t + 512],
                               qls[p], serial=True)
                    P.wait("vector", qls[p], lv)
                    P.op("vector", lambda e, p=p: e.tensor_scalar(
                        qdt[:, 1024 * p + 512:1024 * p + 1024].bitcast(I32),
                        qdt[:, 1024 * p:1024 * p + 512], qsc[:, 9:10],
                        None, ALU.mult), veC)
                    P.op("vector", lambda e, it=it, p=p, dst=dst: e.tensor_scalar(
                        dst[:, 512 * it:512 * it + 512],
                        qdt[:, 1024 * p + 512:1024 * p + 1024].bitcast(I32),
                        qsc[:, 8:9], None, ALU.mult), veC)
                    qdt_u[p] = veC.v
                return one

            buf = [of16, of16B]
            P.wait("sync", spS)
            P.wait("sync", spP[0], spPv[0])
            P.wait("sync", spP[1], spPv[1])
            P.wait("sync", ldG)
            P.wait("sync", veC)
            x2u_v[0] = veC.v
            x2u_v[1] = veC.v
            b0 = build_of(0, buf[0])
            for it in range(IT):
                b0(it)
            for t in range(TG):
                cur = buf[t % 2]
                bnext = build_of(t + 1, buf[(t + 1) % 2]) if t + 1 < TG else None
                P.wait("tensor", veC)
                P.wait("tensor", acC)
                for hc in range(DC):
                    key = t * DC + hc
                    pj = pb[key % 4]
                    s = key % 2
                    xg = x2g if s == 0 else x2gB
                    yy = yt if s == 0 else ytB
                    P.wait("sync", veC, x2u_v[s])
                    lx = P.dma("sync", xg[:, 0:512],
                               sp_x2[128 * hc:128 * hc + 128, 512 * t:512 * t + 512],
                               lxs[s], serial=True)
                    if key >= 4:
                        P.wait("tensor", veC, dn_v[key - 4])
                    for it in range(IT):
                        P.op("tensor", lambda e, pj=pj, hc=hc, it=it, cur=cur: e.matmul(
                            pj[:, 0:512],
                            wdq16[:, D * it + 128 * hc:D * it + 128 * hc + 128],
                            cur[:, 512 * it:512 * it + 512],
                            start=(it == 0), stop=(it == IT - 1)),
                            mmC if it == IT - 1 else None)
                    mmv = mmC.v
                    P.wait("vector", lxs[s], lx)
                    P.wait("vector", mmC, mmv)
                    P.wait("vector", sty[s], sty_v[s])
                    P.op("vector", lambda e, pj=pj, yy=yy, xg=xg: e.tensor_tensor(
                        yy[:, 0:512], pj[:, 0:512], xg[:, 0:512], ALU.add), veC)
                    dn_v[key] = veC.v
                    x2u_v[s] = veC.v
                    P.wait("sync", veC, dn_v[key])
                    sty_v[s] = P.dma(
                        "sync", y[128 * hc:128 * hc + 128, 512 * t:512 * t + 512],
                        yy[:, 0:512], sty[s], serial=False)
                    if bnext is not None:
                        for j in range(3):
                            bnext(3 * hc + j)


        except _StageDone:
            pass
        _final_emit()

        block = ctx.enter_context(nc.Block())

        def mk(eng_name):
            def body(eng):
                for fn in P.q[eng_name]:
                    fn(eng)
            return body
        block.sync(mk("sync"))
        block.scalar(mk("scalar"))
        block.vector(mk("vector"))
        block.tensor(mk("tensor"))
        block.gpsimd(mk("gpsimd"))

    return nc, taps


def prepare_inputs(inputs):
    x = np.asarray(inputs["L0_attn_out"], dtype=np.float32)
    ww = np.asarray(inputs["L0_attn_W_write"], dtype=np.float32)
    wr = np.asarray(inputs["L0_attn_W_read"], dtype=np.float32)
    wa = np.asarray(inputs["L0_attn_W_alpha"], dtype=np.float32)
    wb = np.asarray(inputs["L0_attn_W_beta"], dtype=np.float32)
    alog = np.asarray(inputs["L0_attn_A_log"], dtype=np.float32).reshape(1, NH)
    dtb = np.asarray(inputs["L0_attn_dt_bias"], dtype=np.float32).reshape(1, NH)
    wg = np.asarray(inputs["L0_ffn_w_gate"], dtype=np.float32)
    wu = np.asarray(inputs["L0_ffn_w_up"], dtype=np.float32)
    wd = np.asarray(inputs["L0_ffn_w_down"], dtype=np.float32)

    shared = {
        "wwT": np.ascontiguousarray(ww.T),
        "wrT": np.ascontiguousarray(wr.T),
        "wbaT": np.ascontiguousarray(np.concatenate([wb.T, wa.T], axis=1)),
        "alog": np.exp(alog) if False else alog,
        "dtb": dtb,
        "wuT": np.ascontiguousarray(wu.T),
        "wgT": np.ascontiguousarray(wg.T),
        "wdT": np.ascontiguousarray(wd.T),
        "wu_nat": wu, "wg_nat": wg, "wd_nat": wd,
        "eyef32": np.eye(128, dtype=np.float32),
        "eye16": np.eye(128, dtype=np.float16),
        "ones1": np.ones((1, 128), dtype=np.float32),
    }
    j = np.arange(128)[:, None]
    i = np.arange(128)[None, :]
    same = (j // 64) == (i // 64)
    shared["cumtri16"] = (same & (j <= i)).astype(np.float16)
    shared["blksum16"] = same.astype(np.float16)
    keepU = (same & (j < i))
    keepL = (same & (j > i))
    shared["kaddU16"] = np.where(keepU, 0.0, -60000.0).astype(np.float16)
    shared["kaddL16"] = np.where(keepL, 0.0, -60000.0).astype(np.float16)
    shared["onescol"] = np.ones((128, 1), dtype=np.float32)
    shared["ones1k"] = np.ones((1, 1024), dtype=np.float32)
    r8 = np.arange(8)[:, None]
    p8 = np.arange(128)[None, :]
    i8 = np.zeros((8, 256), dtype=np.float32)
    i8[:, 0:128] = (r8 == p8 // 32) & (r8 < 4)
    i8[:, 128:256] = (r8 - 4) == (p8 // 32)
    shared["ind8"] = i8

    in_maps = []
    for b in range(NCORES):
        m = dict(shared)
        m["x_tok"] = np.ascontiguousarray(x[b])
        m["x_dmaj"] = np.ascontiguousarray(x[b].T)
        in_maps.append(m)
    return in_maps


def kernel(**inputs):
    in_maps = prepare_inputs(inputs)
    nc, _ = build()
    res = run_bass_kernel_spmd(nc, in_maps, core_ids=list(range(NCORES)))
    out = np.stack([np.asarray(r["y"]).T for r in res.results], axis=0)
    return np.ascontiguousarray(out, dtype=np.float32)


if __name__ == "__main__":
    pass
